# version 16
# baseline (speedup 1.0000x reference)
import numpy as np
import concourse.bass as bass
import concourse.mybir as mybir
from concourse.bass_utils import run_bass_kernel_spmd

F32 = mybir.dt.float32
F16 = mybir.dt.float16
AF = mybir.ActivationFunctionType
ALU = mybir.AluOpType
AX = mybir.AxisListType
GELU = AF.Gelu_apprx_tanh

D, DC = 1024, 8
FF, FFC = 4096, 32
NL = 4
NE, EHC = 16, 2
SHC = 16
NG, EPG = 4, 4
EPS_IN, EPS_OUT = 1e-6, 1e-5
T = 512
TSUB = T // 128
WSLOT = 1024
NSLOT = 8

ENGINES = ("sync", "gpsimd", "tensor", "vector", "scalar")

_CP = [("bq", 32), ("bk", 32), ("bv", 32), ("bo", 32), ("l1s", 32), ("l1b", 32),
       ("l2s", 32), ("l2b", 32), ("b2", 32), ("b1", 128), ("cbb", 16),
       ("lt4", 16), ("lt16", 256), ("ident", 128), ("onec", 1)]
CPOFF = {}
_o = 0
for _n, _w in _CP:
    CPOFF[_n] = _o
    _o += _w
CPW = _o


class Bld:

    def __init__(self, nc):
        self.nc = nc
        self.q = {e: [] for e in ENGINES}
        self.sem = {}
        self.semv = {}
        for s in ("cc", "pe", "dv", "ac", "gq"):
            self.getsem(s)
        self.wm = {e: {} for e in ENGINES}
        self.dma_sems = {e: set() for e in ENGINES}
        self.tw = {}
        self.tr = {}
        self.phase = "init"
        self.oplog = []

    def getsem(self, s):
        if s not in self.sem:
            self.sem[s] = self.nc.alloc_semaphore(name="sem_" + s)
            self.semv[s] = 0
        return self.sem[s]

    def _deps(self, regions, writes):
        deps = []
        for (nm, lo, hi) in regions:
            for rec in self.tw.get(nm, ()):
                if rec[0] < hi and lo < rec[1]:
                    deps.append(rec)
            if writes:
                for rec in self.tr.get(nm, ()):
                    if rec[0] < hi and lo < rec[1]:
                        deps.append(rec)
        return deps

    def _record(self, table, regions, eng, sem, val, reads=False):
        for (nm, lo, hi) in regions:
            lst = table.setdefault(nm, [])
            if reads:
                lst[:] = [r for r in lst
                          if not (lo <= r[0] and r[1] <= hi and r[3] == sem)]
            else:
                lst[:] = [r for r in lst if not (lo <= r[0] and r[1] <= hi)]
            lst.append([lo, hi, eng, sem, val])

    def op(self, eng, fn, reads=(), writes=(), inc=None, amt=1, recsem=None,
           skipval=None):
        deps = self._deps(reads, False) + self._deps(writes, True)
        want = {}
        for (_, _, e2, s2, v2) in deps:
            assert s2 is not None, f"untracked producer consumed ({eng})"
            if skipval is not None and s2 == "pe" and v2 == skipval:
                continue
            if v2 > want.get(s2, -1):
                want[s2] = v2
        wl = []
        for s2, v2 in want.items():
            if self.wm[eng].get(s2, 0) >= v2:
                continue
            self.wm[eng][s2] = v2
            wl.append((s2, v2))
        if inc is not None:
            self.getsem(inc)
            self.semv[inc] += amt
            rs, rv = (inc, self.semv[inc]) if recsem is None else recsem
        else:
            assert recsem is not None, "op without inc needs recsem"
            rs, rv = recsem
        self._record(self.tr, reads, eng, rs, rv, reads=True)
        self._record(self.tw, writes, eng, rs, rv)
        for (nm, lo, hi) in writes:
            lst = self.tr.get(nm)
            if lst:
                lst[:] = [r for r in lst if not (lo <= r[0] and r[1] <= hi)]
        self.oplog.append((self.phase, eng, tuple(s for s, _ in wl)))
        self.q[eng].append((wl, fn, self.sem[inc] if inc else None, amt))

    def wait(self, eng, sem, val):
        if self.wm[eng].get(sem, 0) >= val:
            return
        self.wm[eng][sem] = val
        self.q[eng].append(([(sem, val)], None, None, 0))

    def realize(self):
        nc = self.nc
        with nc.Block() as block:
            for ename in ENGINES:
                ops = self.q[ename]

                def body(q, ops=ops):
                    for wl, fn, isem, amt in ops:
                        for (s, v) in wl:
                            q.wait_ge(self.sem[s], v)
                        if fn is not None:
                            inst = fn(q)
                            if isem is not None:
                                inst.then_inc(isem, amt)

                getattr(block, ename)(body)


def apx(t, off, dims):
    f = t.shape[1]
    return bass.AP(t, off, [[f, t.shape[0]]] + [list(d) for d in dims])


def build(n_cores, SC, nl=NL, do_outer=True, do_moe=True):
    NT = SC // T
    nc = bass.Bass("TRN2", target_bir_lowering=False)

    xin = nc.dram_tensor("xin", [128, DC * SC], F32, kind="ExternalInput")
    wq = nc.dram_tensor("wq", [NL * DC, 128, D], F32, kind="ExternalInput")
    wk = nc.dram_tensor("wk", [NL * DC, 128, D], F32, kind="ExternalInput")
    wv = nc.dram_tensor("wv", [NL * DC, 128, D], F32, kind="ExternalInput")
    wo = nc.dram_tensor("wo", [NL * DC, 128, D], F32, kind="ExternalInput")
    w1 = nc.dram_tensor("w1", [NL * FFC, 128, D], F32, kind="ExternalInput")
    w2 = nc.dram_tensor("w2", [NL * DC, 128, FF], F32, kind="ExternalInput")
    gw = nc.dram_tensor("gw", [NE * EHC, 128, D], F32, kind="ExternalInput")
    uw = nc.dram_tensor("uw", [NE * EHC, 128, D], F32, kind="ExternalInput")
    dw = nc.dram_tensor("dw", [DC, 128, NE * 256], F32, kind="ExternalInput")
    sgw = nc.dram_tensor("sgw", [SHC, 128, D], F32, kind="ExternalInput")
    suw = nc.dram_tensor("suw", [SHC, 128, D], F32, kind="ExternalInput")
    sdw = nc.dram_tensor("sdw", [DC, 128, 2048], F32, kind="ExternalInput")
    rwt = nc.dram_tensor("rwt", [128, DC * NE], F32, kind="ExternalInput")
    cpk = nc.dram_tensor("cpk", [128, CPW], F32, kind="ExternalInput")
    onr = nc.dram_tensor("onr", [128, 128], F32, kind="ExternalInput")
    yout = nc.dram_tensor("yout", [128, DC * SC], F32, kind="ExternalOutput")
    cci = nc.dram_tensor("cci", [2 * D], F32)
    cco = nc.dram_tensor("cco", [2 * D], F32)
    wqp = nc.dram_tensor("wqp", [NL * DC, 128, D], F32)
    wkp = nc.dram_tensor("wkp", [NL * DC, 128, D], F32)
    wvp = nc.dram_tensor("wvp", [NL * DC, 128, D], F32)
    wop = nc.dram_tensor("wop", [NL * DC, 128, D], F32)
    w1p = nc.dram_tensor("w1p", [NL * FFC, 128, D], F32)
    w2p = nc.dram_tensor("w2p", [NL * DC, 128, FF], F32)
    gwp = nc.dram_tensor("gwp", [NE * EHC, 128, D], F32)
    uwp = nc.dram_tensor("uwp", [NE * EHC, 128, D], F32)
    sgp = nc.dram_tensor("sgp", [SHC, 128, D], F32)
    sup = nc.dram_tensor("sup", [SHC, 128, D], F32)
    dwp = nc.dram_tensor("dwp", [DC, 128, NE * 256], F32)
    sdp = nc.dram_tensor("sdp", [DC, 128, 2048], F32)

    x_sb = nc.alloc_sbuf_tensor("x_sb", [128, DC * SC], F32)
    pool = nc.alloc_sbuf_tensor("pool", [128, 8192 + FFC * T], F32)
    wbuf = nc.alloc_sbuf_tensor("wbuf", [128, NSLOT * WSLOT], F32)
    cp = nc.alloc_sbuf_tensor("cp", [128, CPW], F32)
    onesr = nc.alloc_sbuf_tensor("onesr", [128, 128], F32)
    s_rwt = nc.alloc_sbuf_tensor("s_rwt", [128, DC * NE], F32)
    scr = [nc.alloc_sbuf_tensor(f"scr{i}", [128, T], F32) for i in range(3)]
    rows_t = nc.alloc_sbuf_tensor("rows_t", [128, T], F32)
    crow = [(rows_t[32:33, :], ("rows_t", 0, T)),
            (rows_t[64:65, :], ("rows_t", 0, T))]
    kvp4 = nc.alloc_sbuf_tensor("kvp4", [128, DC * NT], F32)
    ksp4 = nc.alloc_sbuf_tensor("ksp4", [128, DC * NT], F32)
    kvr = nc.alloc_sbuf_tensor("kvr", [128, DC], F32)
    ksr = nc.alloc_sbuf_tensor("ksr", [128, DC], F32)
    kvg = nc.alloc_sbuf_tensor("kvg", [128, DC], F32)
    ksg = nc.alloc_sbuf_tensor("ksg", [128, DC], F32)
    r_sc = nc.alloc_sbuf_tensor("r_sc", [128, NE * TSUB], F32)
    r_sb = nc.alloc_sbuf_tensor("r_sb", [128, NE * TSUB], F32)
    r_pr = nc.alloc_sbuf_tensor("r_pr", [128, 24], F32)
    r_gs = nc.alloc_sbuf_tensor("r_gs", [128, NG], F32)
    r_ta = nc.alloc_sbuf_tensor("r_ta", [128, NG], F32)
    r_tb = nc.alloc_sbuf_tensor("r_tb", [128, NG], F32)
    r_rk = nc.alloc_sbuf_tensor("r_rk", [128, NE], F32)
    r_pn = nc.alloc_sbuf_tensor("r_pn", [128, NG], F32)
    r_mb = nc.alloc_sbuf_tensor("r_mb", [128, NE], F32)
    r_kp = nc.alloc_sbuf_tensor("r_kp", [128, NE], F32)
    r_dn = nc.alloc_sbuf_tensor("r_dn", [128, 1], F32)
    s_cmb = nc.alloc_sbuf_tensor("s_cmb", [128, NE * TSUB], F32)
    s_cem = nc.alloc_sbuf_tensor("s_cem", [16, T], F32)

    pA = nc.alloc_psum_tensor("pA", [128, T], F32)
    pB = nc.alloc_psum_tensor("pB", [128, T], F32)
    pD = nc.alloc_psum_tensor("pD", [128, T], F32)
    pC = nc.alloc_psum_tensor("pC", [128, T], F32)
    pS1 = nc.alloc_psum_tensor("pS1", [1, T], F32)
    pS2 = nc.alloc_psum_tensor("pS2", [1, T], F32)
    pM = nc.alloc_psum_tensor("pM", [128, T], F32)
    pR = nc.alloc_psum_tensor("pR", [128, T], F32)

    b = Bld(nc)
    BIG = 1 << 30

    def R(t, lo=0, hi=BIG):
        return (t.name, lo, hi)

    def PS(t):
        return (t.name, 0, BIG)

    def dma(qeng, out_ap, in_ap, reads, writes, sem, recfinal=None):
        b.getsem(sem)
        b.dma_sems[qeng].add(sem)
        rs = (sem, recfinal) if recfinal is not None else None
        b.op(qeng, lambda e, o=out_ap, i=in_ap: e.dma_start(out=o, in_=i),
             reads=reads, writes=writes, inc=sem, amt=16, recsem=rs)

    def vop(fn, reads, writes):
        b.op("vector", fn, reads, writes, inc="dv")

    def aop(fn, reads, writes):
        b.op("scalar", fn, reads, writes, inc="ac")

    def gop(fn, reads, writes):
        b.op("gpsimd", fn, reads, writes, inc="gq")

    def pop_(par, fn, reads, writes):
        if par % 2 == 0:
            vop(fn, reads, writes)
        else:
            gop(fn, reads, writes)

    rot = [0]
    rot_list = [[None]]

    def bank():
        lst = rot_list[0]
        rot[0] = (rot[0] + 1) % len(lst)
        return lst[rot[0]]

    def set_rot(lst):
        rot_list[0] = lst
        rot[0] = 0

    def mmg(pt, pairs, ap_out=None):
        gv = b.semv["pe"] + 1
        n = len(pairs)
        out = ap_out if ap_out is not None else pt[:, :]
        for i, (la, lrg, ra, rrg) in enumerate(pairs):
            last = i == n - 1
            b.op("tensor",
                 lambda e, la=la, ra=ra, s=(i == 0), z=last, o=out:
                 e.matmul(o, la, ra, start=s, stop=z),
                 reads=[lrg, rrg], writes=[PS(pt)],
                 inc=("pe" if last else None), amt=1, recsem=("pe", gv),
                 skipval=gv)
        assert b.semv["pe"] == gv

    def mmg3(pt, items, ap_out=None):
        gv = b.semv["pe"] + 1
        out = ap_out if ap_out is not None else pt[:, :]
        n3 = len(items) * 3
        i = 0
        for (wh, wl, wrg, xh, xl, xrg) in items:
            for (la, ra) in ((wh, xh), (wl, xh), (wh, xl)):
                last = i == n3 - 1
                b.op("tensor",
                     lambda e, la=la, ra=ra, s=(i == 0), z=last, o=out:
                     e.matmul(o, la, ra, start=s, stop=z),
                     reads=[wrg, xrg], writes=[PS(pt)],
                     inc=("pe" if last else None), amt=1, recsem=("pe", gv),
                     skipval=gv)
                i += 1
        assert b.semv["pe"] == gv

    ring = [0]

    def wslot(units):
        if ring[0] + units > NSLOT:
            ring[0] = 0
        off = ring[0] * WSLOT
        ring[0] += units
        return off

    split_done = set()
    presplit_jobs = []

    def wpair(dram_t, blk, cols, colslice=None, cache=None, stage_q="sync"):
        u = cols // WSLOT
        if cache is not None:
            pdram, fam, idx = cache[0], cache[1], cache[2]
            first = (fam, idx) not in split_done
            if first:
                split_done.add((fam, idx))
        else:
            first = True
        if not first:
            po = wslot(u)
            prg = R(wbuf, po, po + cols)
            src = pdram[blk, :, :] if colslice is None else \
                pdram[blk, :, colslice[0]:colslice[1]]
            dma("sync", wbuf[:, po:po + cols], src,
                reads=[(fam, idx, idx + 1)], writes=[prg],
                sem=f"ws{po // WSLOT}")
        else:
            so = wslot(2 * u)
            po = so + cols
            srg = R(wbuf, so, so + cols)
            src = dram_t[blk, :, :] if colslice is None else \
                dram_t[blk, :, colslice[0]:colslice[1]]
            dma(stage_q, wbuf[:, so:so + cols], src, reads=[], writes=[srg],
                sem=f"w{'s' if stage_q == 'sync' else 'p'}{so // WSLOT}")
            prg = R(wbuf, po, po + cols)
            half = cols // 2
            hi_full = wbuf[:, po:po + half].bitcast(F16)
            lo_full = wbuf[:, po + half:po + cols].bitcast(F16)
            gop(lambda e, o=hi_full, i=wbuf[:, so:so + cols]: e.tensor_copy(o, i),
                reads=[srg], writes=[prg])
            vop(lambda e, o=lo_full, i=wbuf[:, so:so + cols], h=hi_full:
                e.tensor_tensor(o, i, h, ALU.subtract),
                reads=[srg, prg], writes=[prg])
            if cache is not None:
                dst = pdram[blk, :, :] if colslice is None else \
                    pdram[blk, :, colslice[0]:colslice[1]]
                dma("gpsimd", dst, wbuf[:, po:po + cols],
                    reads=[prg], writes=[(fam, idx, idx + 1)],
                    sem=f"wb{po // WSLOT}")
        half = cols // 2

        def chunk(k, po=po, half=half):
            whk = wbuf[:, po + k * 64:po + (k + 1) * 64].bitcast(F16)
            wlk = wbuf[:, po + half + k * 64:po + half + (k + 1) * 64].bitcast(F16)
            return whk, wlk

        return chunk, prg

    def presplit(k=1):
        while k > 0 and presplit_jobs:
            dram_t, blk, cols, colslice, cache = presplit_jobs.pop(0)
            fam, idx = cache[1], cache[2]
            if (fam, idx) in split_done:
                continue
            wpair(dram_t, blk, cols, colslice=colslice, cache=cache)
            k -= 1

    def queue_layer_presplit(L):
        for n in range(DC):
            presplit_jobs.append((wk, L * DC + n, D, None, (wkp, "k", L * DC + n)))
            presplit_jobs.append((wv, L * DC + n, D, None, (wvp, "v", L * DC + n)))
        for n in range(DC):
            presplit_jobs.append((wq, L * DC + n, D, None, (wqp, "q", L * DC + n)))
        for n in range(DC):
            presplit_jobs.append((wo, L * DC + n, D, None, (wop, "o", L * DC + n)))
        for ff in range(FFC):
            presplit_jobs.append((w1, L * FFC + ff, D, None, (w1p, "1", L * FFC + ff)))
        for n in range(DC):
            presplit_jobs.append((w2, L * DC + n, FF, None,
                                  (w2p, "2", L * DC + n)))

    def queue_moe_presplit():
        for j in range(SHC):
            presplit_jobs.append((sgw, j, D, None, (sgp, "sg", j)))
        for j in range(SHC):
            presplit_jobs.append((suw, j, D, None, (sup, "su", j)))
        for e_ in range(NE):
            for nb in range(EHC):
                presplit_jobs.append((gw, e_ * EHC + nb, D, None,
                                      (gwp, "g", e_ * EHC + nb)))
                presplit_jobs.append((uw, e_ * EHC + nb, D, None,
                                      (uwp, "u", e_ * EHC + nb)))
        for n in range(DC):
            presplit_jobs.append((sdw, n, 2048, None, (sdp, "sd", n)))
            for wv_ in range(2):
                presplit_jobs.append((dw, n, 2048, (wv_ * 2048, (wv_ + 1) * 2048),
                                      (dwp, "d", n * 2 + wv_)))

    def cpcol(name, i):
        o = CPOFF[name] + i
        return cp[:, o:o + 1]

    CPRG = R(cp)
    x_rg = lambda n, t0, t1: R(x_sb, n * SC + t0, n * SC + t1)
    x_ap = lambda n, t0, t1: x_sb[:, n * SC + t0:n * SC + t1]

    def PL(off, cols=T):
        return pool[:, off:off + cols], R(pool, off, off + cols)

    def PLH(off, cols):
        return pool[:, off:off + cols].bitcast(F16), R(pool, off, off + cols)

    def split_to_pool(src_ap, src_rg, off, caster="scalar", subeng=0):
        hi_ap, hi_rg = PLH(off, T // 2)
        lo_ap, lo_rg = PLH(off + T // 2, T // 2)
        if caster == "gpsimd":
            gop(lambda e, o=hi_ap, i=src_ap: e.tensor_copy(o, i),
                reads=[src_rg], writes=[hi_rg])
        else:
            aop(lambda e, o=hi_ap, i=src_ap: e.activation(o, i, AF.Copy),
                reads=[src_rg], writes=[hi_rg])
        pop_(subeng, lambda e, o=lo_ap, i=src_ap, h=hi_ap:
             e.tensor_tensor(o, i, h, ALU.subtract),
             [src_rg, hi_rg], [lo_rg])

    def xsplit(base, t0):
        res = []
        for k in range(DC):
            s_ap, s_rg = x_ap(k, t0, t0 + T), x_rg(k, t0, t0 + T)
            off = base + k * T
            split_to_pool(s_ap, s_rg, off)
            hi_ap = pool[:, off:off + T // 2].bitcast(F16)
            lo_ap = pool[:, off + T // 2:off + T].bitcast(F16)
            res.append((hi_ap, lo_ap, R(pool, off, off + T)))
        return res

    set_rot([pA, pB, pD, pC])

    dma("sync", cp[:, :], cpk[:, :], reads=[], writes=[CPRG],
        sem="scst", recfinal=48)
    dma("sync", onesr[:, :], onr[:, :], reads=[], writes=[R(onesr)],
        sem="scst", recfinal=48)
    dma("sync", s_rwt[:, :], rwt[:, :], reads=[], writes=[R(s_rwt)],
        sem="scst", recfinal=48)
    for n in range(DC):
        dma("sync", x_ap(n, 0, SC), xin[:, n * SC:(n + 1) * SC],
            reads=[], writes=[x_rg(n, 0, SC)], sem="sx", recfinal=16 * DC)

    ONEC = (cp[:, CPOFF["onec"]:CPOFF["onec"] + 1], CPRG)
    ONESR = (onesr[0:1, :], R(onesr))
    ONESR32 = (onesr[32:33, :], R(onesr))
    ONESR64 = (onesr[64:65, :], R(onesr))
    IDENT = (cp[:, CPOFF["ident"]:CPOFF["ident"] + 128], CPRG)

    def eluplus1_head(psum_t, bcol, sA, sB):
        vop(lambda e, sA=sA: e.tensor_scalar(sA[0], psum_t[:, :], bcol, 0.0,
                                             ALU.add, ALU.min),
            reads=[PS(psum_t), CPRG], writes=[sA[1]])
        vop(lambda e, sB=sB: e.tensor_scalar(sB[0], psum_t[:, :], bcol, 0.0,
                                             ALU.add, ALU.max),
            reads=[PS(psum_t), CPRG], writes=[sB[1]])
        aop(lambda e, sA=sA: e.activation(sA[0], sA[0], AF.Exp),
            reads=[sA[1]], writes=[sA[1]])

    def eluplus1_tail(out_ap, out_rg, sA, sB, acc_ap=None, acc_rg=None):
        if acc_ap is not None:
            vop(lambda e, sA=sA, sB=sB: e.scalar_tensor_tensor(
                    out_ap, sA[0], 1.0, sB[0], ALU.mult, ALU.add,
                    accum_out=acc_ap),
                reads=[sA[1], sB[1]], writes=[out_rg, acc_rg])
        else:
            vop(lambda e, sA=sA, sB=sB: e.scalar_tensor_tensor(
                    out_ap, sA[0], 1.0, sB[0], ALU.mult, ALU.add),
                reads=[sA[1], sB[1]], writes=[out_rg])

    def layernorm(y, eps, out, scale_name=None, bias_name=None, Lix=0,
                  store_after=None, sbase=4096):
        sD_ap, sD_rg = PL(sbase)
        sE_ap, sE_rg = PL(sbase + T)
        sC_ap, sC_rg = PL(sbase + 2 * T)
        vop(lambda e: e.tensor_add(sD_ap, y[0][0], y[1][0]),
            reads=[y[0][1], y[1][1]], writes=[sD_rg])
        for n in range(2, DC):
            vop(lambda e, n=n: e.tensor_add(sD_ap, sD_ap, y[n][0]),
                reads=[sD_rg, y[n][1]], writes=[sD_rg])
        vop(lambda e: e.tensor_mul(sE_ap, y[0][0], y[0][0]),
            reads=[y[0][1]], writes=[sE_rg])
        for n in range(1, DC):
            vop(lambda e, n=n: e.tensor_mul(sC_ap, y[n][0], y[n][0]),
                reads=[y[n][1]], writes=[sC_rg])
            vop(lambda e: e.tensor_add(sE_ap, sE_ap, sC_ap),
                reads=[sE_rg, sC_rg], writes=[sE_rg])
        mmg(pS1, [(ONEC[0], ONEC[1], sD_ap, sD_rg)])
        mmg(pS2, [(ONEC[0], ONEC[1], sE_ap, sE_rg)])
        rm_ap = pool[0:1, sbase + 3 * T:sbase + 4 * T]
        rm_rg = R(pool, sbase + 3 * T, sbase + 4 * T)
        rv_ap = pool[0:1, sbase + 4 * T:sbase + 5 * T]
        rv_rg = R(pool, sbase + 4 * T, sbase + 5 * T)
        vop(lambda e: e.tensor_scalar_mul(rm_ap, pS1[:, :], 1.0 / D),
            reads=[PS(pS1)], writes=[rm_rg])
        vop(lambda e: e.tensor_scalar_mul(rv_ap, pS2[:, :], 1.0 / D),
            reads=[PS(pS2)], writes=[rv_rg])
        mmg(pM, [(ONESR[0], ONESR[1], rm_ap, rm_rg)])
        vop(lambda e: e.tensor_mul(rm_ap, rm_ap, rm_ap),
            reads=[rm_rg], writes=[rm_rg])
        vop(lambda e: e.tensor_sub(rv_ap, rv_ap, rm_ap),
            reads=[rv_rg, rm_rg], writes=[rv_rg])
        vop(lambda e: e.tensor_scalar_add(rv_ap, rv_ap, float(eps)),
            reads=[rv_rg], writes=[rv_rg])
        aop(lambda e: e.activation(rv_ap, rv_ap, AF.Ln),
            reads=[rv_rg], writes=[rv_rg])
        aop(lambda e: e.activation(rv_ap, rv_ap, AF.Exp, scale=-0.5),
            reads=[rv_rg], writes=[rv_rg])
        mmg(pR, [(ONESR[0], ONESR[1], rv_ap, rv_rg)])
        for n in range(DC):
            vop(lambda e, n=n: e.tensor_sub(y[n][0], y[n][0], pM[:, :]),
                reads=[y[n][1], PS(pM)], writes=[y[n][1]])
            if scale_name is None:
                pop_(n, lambda e, n=n: e.tensor_mul(out[n][0], y[n][0], pR[:, :]),
                     [y[n][1], PS(pR)], [out[n][1]]) if False else \
                    vop(lambda e, n=n: e.tensor_mul(out[n][0], y[n][0], pR[:, :]),
                        reads=[y[n][1], PS(pR)], writes=[out[n][1]])
                if store_after is not None:
                    store_after(n)
            else:
                vop(lambda e, n=n: e.tensor_mul(y[n][0], y[n][0], pR[:, :]),
                    reads=[y[n][1], PS(pR)], writes=[y[n][1]])
                sc_ = cpcol(scale_name, Lix * DC + n)
                bc_ = cpcol(bias_name, Lix * DC + n)
                vop(lambda e, n=n, sc_=sc_, bc_=bc_: e.tensor_scalar(
                        out[n][0], y[n][0], sc_, bc_, ALU.mult, ALU.add),
                    reads=[y[n][1], CPRG], writes=[out[n][1]])

    for L in range(nl):
        if L + 1 < nl:
            queue_layer_presplit(L + 1)
        elif do_moe:
            queue_moe_presplit()
        for t in range(NT):
            t0 = t * T
            b.phase = "passA"
            xp = xsplit(4096, t0)
            pend_a = [None]
            for n in range(DC):
                kch, krg = wpair(wk, L * DC + n, D,
                                 cache=(wkp, "k", L * DC + n))
                vch, vrg2 = wpair(wv, L * DC + n, D,
                                  cache=(wvp, "v", L * DC + n))
                pk = bank()
                mmg3(pk, [(kch(k)[0], kch(k)[1], krg,
                           xp[k][0], xp[k][1], xp[k][2]) for k in range(DC)])
                pv = bank()
                mmg3(pv, [(vch(k)[0], vch(k)[1], vrg2,
                           xp[k][0], xp[k][1], xp[k][2]) for k in range(DC)])
                col = n * NT + t
                rr = (n % 2) * 4
                sk_ap, sk_rg = PL((rr + 2) * T)
                sv_ap, sv_rg = PL((rr + 3) * T)
                sA, sB = PL(rr * T), PL((rr + 1) * T)
                eluplus1_head(pk, cpcol("bk", L * DC + n), sA, sB)
                bvc = cpcol("bv", L * DC + n)
                vop(lambda e, pv=pv, o=sv_ap, bvc=bvc: e.tensor_scalar_add(
                        o, pv[:, :], bvc),
                    reads=[PS(pv), CPRG], writes=[sv_rg])
                if pend_a[0] is not None:
                    pend_a[0]()
                smul = (scr[2][:, :], R(scr[2])) if n % 2 == 0 else \
                    (rows_t[:, :], ("rows_t", 0, T))

                def tail_a(col=col, sA=sA, sB=sB, sk_ap=sk_ap, sk_rg=sk_rg,
                           sv_ap=sv_ap, sv_rg=sv_rg, smul=smul):
                    eluplus1_tail(sk_ap, sk_rg, sA, sB,
                                  acc_ap=ksp4[:, col:col + 1],
                                  acc_rg=R(ksp4, col, col + 1))
                    vop(lambda e: e.tensor_mul(smul[0], sk_ap, sv_ap),
                        reads=[sk_rg, sv_rg], writes=[smul[1]])
                    vop(lambda e: e.reduce_sum(
                            kvp4[:, col:col + 1], smul[0], axis=AX.X),
                        reads=[smul[1]], writes=[R(kvp4, col, col + 1)])
                pend_a[0] = tail_a
            pend_a[0]()
            pend_a[0] = None
        for (p4, red) in ((kvp4, kvr), (ksp4, ksr)):
            if NT == 1:
                vop(lambda e, p4=p4, red=red: e.tensor_copy(red[:, :], p4[:, :]),
                    reads=[R(p4)], writes=[R(red)])
            else:
                v0 = apx(p4, 0, [(NT, DC)])
                v1 = apx(p4, 1, [(NT, DC)])
                vop(lambda e, p4=p4, red=red, v0=v0, v1=v1: e.tensor_tensor(
                        red[:, :], v0, v1, ALU.add),
                    reads=[R(p4)], writes=[R(red)])
                for t in range(2, NT):
                    vt = apx(p4, t, [(NT, DC)])
                    vop(lambda e, red=red, vt=vt, p4=p4: e.tensor_tensor(
                            red[:, :], red[:, :], vt, ALU.add),
                        reads=[R(red), R(p4)], writes=[R(red)])
        gfin = 16 * 2 * (L + 1)
        dma("gpsimd", bass.AP(cci, 0, [[DC, 128], [1, DC]]), kvr[:, :],
            reads=[R(kvr)], writes=[R(cci, 0, D)], sem="gpi", recfinal=gfin)
        dma("gpsimd", bass.AP(cci, D, [[DC, 128], [1, DC]]), ksr[:, :],
            reads=[R(ksr)], writes=[R(cci, D, 2 * D)], sem="gpi", recfinal=gfin)
        b.op("gpsimd",
             lambda e: e.collective_compute(
                 "AllReduce", ALU.add, replica_groups=[list(range(n_cores))],
                 ins=[cci.ap().opt()], outs=[cco.ap().opt()]),
             reads=[R(cci, 0, 2 * D)], writes=[R(cco, 0, 2 * D)], inc="cc")
        dma("gpsimd", kvg[:, :], bass.AP(cco, 0, [[DC, 128], [1, DC]]),
            reads=[R(cco, 0, D)], writes=[R(kvg)], sem="gpo", recfinal=gfin)
        dma("gpsimd", ksg[:, :], bass.AP(cco, D, [[DC, 128], [1, DC]]),
            reads=[R(cco, D, 2 * D)], writes=[R(ksg)], sem="gpo", recfinal=gfin)

        for t in range(NT):
            t0 = t * T
            b.phase = "q"
            xp = xsplit(4096, t0)
            pend_q = [None]
            for n in range(DC):
                qch, qrg = wpair(wq, L * DC + n, D,
                                 cache=(wqp, "q", L * DC + n))
                pq = bank()
                mmg3(pq, [(qch(k)[0], qch(k)[1], qrg,
                           xp[k][0], xp[k][1], xp[k][2]) for k in range(DC)])
                q_ap, q_rg = PL(n * T)
                rr = n % 4
                sA, sB = PL(8192 + (2 * rr) * T), PL(8192 + (2 * rr + 1) * T)
                eluplus1_head(pq, cpcol("bq", L * DC + n), sA, sB)
                if pend_q[0] is not None:
                    pend_q[0]()
                pend_q[0] = (lambda q_ap=q_ap, q_rg=q_rg, sA=sA, sB=sB:
                             eluplus1_tail(q_ap, q_rg, sA, sB))
                presplit(1)
            pend_q[0]()
            b.phase = "V"
            vhl = []
            for n in range(DC):
                q_ap, q_rg = PL(n * T)
                sden = scr[0] if n % 2 == 0 else scr[2]
                vop(lambda e, q=q_ap, n=n, sd=sden: e.tensor_scalar(
                        sd[:, :], q, ksg[:, n:n + 1], EPS_IN,
                        ALU.mult, ALU.add),
                    reads=[q_rg, R(ksg)], writes=[R(sden)])
                vop(lambda e, sd=sden: e.reciprocal(sd[:, :], sd[:, :]),
                    reads=[R(sden)], writes=[R(sden)])
                sv2 = (scr[1][:, :], R(scr[1])) if n % 2 == 0 else \
                    (rows_t[:, :], ("rows_t", 0, T))
                vop(lambda e, q=q_ap, n=n, sd=sden, sv2=sv2: e.scalar_tensor_tensor(
                        sv2[0], q, kvg[:, n:n + 1], sd[:, :],
                        ALU.mult, ALU.mult),
                    reads=[q_rg, R(kvg), R(sden)], writes=[sv2[1]])
                split_to_pool(sv2[0], sv2[1], n * T)
                vh = pool[:, n * T:n * T + T // 2].bitcast(F16)
                vl = pool[:, n * T + T // 2:(n + 1) * T].bitcast(F16)
                vhl.append((vh, vl, R(pool, n * T, (n + 1) * T)))
            b.phase = "o"
            ych = []
            for n in range(DC):
                och, org = wpair(wo, L * DC + n, D,
                                 cache=(wop, "o", L * DC + n))
                po = bank()
                mmg3(po, [(och(k)[0], och(k)[1], org,
                           vhl[k][0], vhl[k][1], vhl[k][2])
                          for k in range(DC)])
                a_ap, a_rg = PL(4096 + n * T)
                boc = cpcol("bo", L * DC + n)
                aop(lambda e, po=po, o=a_ap, boc=boc: e.activation(
                        o, po[:, :], GELU, bias=boc),
                    reads=[PS(po), CPRG], writes=[a_rg])
                aop(lambda e, o=a_ap: e.activation(o, o, GELU),
                    reads=[a_rg], writes=[a_rg])
                ych.append((a_ap, a_rg))
                presplit(1)
            for n in range(DC):
                ya, yr, xa = ych[n][0], ych[n][1], x_ap(n, t0, t0 + T)
                vop(lambda e, ya=ya, xa=xa: e.tensor_add(ya, ya, xa),
                    reads=[yr, x_rg(n, t0, t0 + T)], writes=[yr])
            xout = [(x_ap(n, t0, t0 + T), x_rg(n, t0, t0 + T)) for n in range(DC)]
            layernorm(ych, EPS_IN, xout, "l1s", "l1b", L, sbase=0)
            b.phase = "W1"
            xp1 = xsplit(4096, t0)
            hph = []
            for ff in range(FFC):
                fch, frg = wpair(w1, L * FFC + ff, D,
                                 cache=(w1p, "1", L * FFC + ff))
                pf = bank()
                mmg3(pf, [(fch(k)[0], fch(k)[1], frg,
                           xp1[k][0], xp1[k][1], xp1[k][2]) for k in range(DC)])
                b1c = cpcol("b1", L * 32 + ff)
                sg_ = scr[ff % 3]
                aop(lambda e, pf=pf, b1c=b1c, sg_=sg_: e.activation(
                        sg_[:, :], pf[:, :], GELU, bias=b1c),
                    reads=[PS(pf), CPRG], writes=[R(sg_)])
                hoff = 8192 + ff * T
                split_to_pool(sg_[:, :], R(sg_), hoff)
                presplit(1)
                hh = pool[:, hoff:hoff + T // 2].bitcast(F16)
                hl = pool[:, hoff + T // 2:hoff + T].bitcast(F16)
                hph.append((hh, hl, R(pool, hoff, hoff + T)))
            b.phase = "W2"
            ych2 = []
            for n in range(DC):
                qch, qrg = wpair(w2, L * DC + n, FF,
                                 cache=(w2p, "2", L * DC + n))
                items = [(qch(k)[0], qch(k)[1], qrg,
                          hph[k][0], hph[k][1], hph[k][2]) for k in range(FFC)]
                pf2 = bank()
                mmg3(pf2, items)
                y2_ap, y2_rg = PL(n * T)
                b2c = cpcol("b2", L * DC + n)
                sg_ = scr[n % 3]
                aop(lambda e, pf2=pf2, b2c=b2c, sg_=sg_: e.activation(
                        sg_[:, :], pf2[:, :], GELU, bias=b2c),
                    reads=[PS(pf2), CPRG], writes=[R(sg_)])
                xa2 = x_ap(n, t0, t0 + T)
                vop(lambda e, o=y2_ap, xa2=xa2, sg_=sg_: e.tensor_add(
                        o, sg_[:, :], xa2),
                    reads=[R(sg_), x_rg(n, t0, t0 + T)], writes=[y2_rg])
                ych2.append((y2_ap, y2_rg))
                presplit(1)
            layernorm(ych2, EPS_IN, xout, "l2s", "l2b", L, sbase=4096)

    for t in range(NT if do_outer else 0):
        t0 = t * T
        ych = []
        for n in range(DC):
            xo_ap, xo_rg = PL(n * T)
            dma("sync", xo_ap, xin[:, n * SC + t0:n * SC + t0 + T],
                reads=[], writes=[xo_rg], sem="so",
                recfinal=16 * DC * (t + 1))
            xa3 = x_ap(n, t0, t0 + T)
            vop(lambda e, o=xo_ap, xa3=xa3: e.tensor_add(o, o, xa3),
                reads=[xo_rg, x_rg(n, t0, t0 + T)], writes=[xo_rg])
            ych.append((xo_ap, xo_rg))
        xout = [(x_ap(n, t0, t0 + T), x_rg(n, t0, t0 + T)) for n in range(DC)]
        layernorm(ych, EPS_OUT, xout, sbase=4096)

    LT16 = CPOFF["lt16"]
    LT4 = CPOFF["lt4"]
    XMH = 8192
    HWB = 12288
    FAB = 20480
    if not do_moe:
        for t in range(NT):
            t0 = t * T
            for n in range(DC):
                dma("scalar", yout[:, n * SC + t0:n * SC + t0 + T],
                    x_ap(n, t0, t0 + T), reads=[x_rg(n, t0, t0 + T)],
                    writes=[R(yout, n * SC + t0, n * SC + t0 + T)],
                    sem=f"st{n % 2}")
    for t in range(NT if do_moe else 0):
        t0 = t * T
        b.phase = "router"
        for c in range(TSUB):
            pr = bank()
            mmg(pr, [(x_ap(k, t0 + c * 128, t0 + (c + 1) * 128),
                      x_rg(k, t0 + c * 128, t0 + (c + 1) * 128),
                      s_rwt[:, k * NE:(k + 1) * NE], R(s_rwt))
                     for k in range(DC)],
                ap_out=pr[:, 0:NE])
            aop(lambda e, pr=pr, c=c: e.activation(
                    r_sc[:, c * NE:(c + 1) * NE], pr[:, 0:NE], AF.Sigmoid),
                reads=[PS(pr)], writes=[R(r_sc, c * NE, (c + 1) * NE)])
        for c in range(TSUB):
            vop(lambda e, c=c: e.tensor_tensor(
                    r_sb[:, c * NE:(c + 1) * NE], r_sc[:, c * NE:(c + 1) * NE],
                    cp[:, CPOFF["cbb"]:CPOFF["cbb"] + NE], ALU.add),
                reads=[R(r_sc, c * NE, (c + 1) * NE), CPRG],
                writes=[R(r_sb, c * NE, (c + 1) * NE)])
        RT = (scr[0][:, 0:256], R(scr[0]))
        RS = (scr[1][:, 0:256], R(scr[1]))
        C1 = (scr[2][:, 0:256], R(scr[2]))
        C2 = (rows_t[:, 0:256], ("rows_t", 0, T))
        for c in range(TSUB):
            o16 = c * NE
            sbrg = R(r_sb, o16, o16 + NE)
            pairs6 = [(0, 1), (0, 2), (0, 3), (1, 2), (1, 3), (2, 3)]
            for j, (i1, i2) in enumerate(pairs6):
                vop(lambda e, j=j, i1=i1, i2=i2, o16=o16: e.tensor_tensor(
                        r_pr[:, j * 4:(j + 1) * 4],
                        apx(r_sb, o16 + i1, [(4, 4)]),
                        apx(r_sb, o16 + i2, [(4, 4)]), ALU.add),
                    reads=[sbrg], writes=[R(r_pr, j * 4, (j + 1) * 4)])
            vop(lambda e: e.tensor_max(r_ta[:, :], r_pr[:, 0:4], r_pr[:, 4:8]),
                reads=[R(r_pr, 0, 8)], writes=[R(r_ta)])
            vop(lambda e: e.tensor_max(r_tb[:, :], r_pr[:, 8:12], r_pr[:, 12:16]),
                reads=[R(r_pr, 8, 16)], writes=[R(r_tb)])
            vop(lambda e: e.tensor_max(r_ta[:, :], r_ta[:, :], r_tb[:, :]),
                reads=[R(r_ta), R(r_tb)], writes=[R(r_ta)])
            vop(lambda e: e.tensor_max(r_tb[:, :], r_pr[:, 16:20], r_pr[:, 20:24]),
                reads=[R(r_pr, 16, 24)], writes=[R(r_tb)])
            vop(lambda e: e.tensor_max(r_gs[:, :], r_ta[:, :], r_tb[:, :]),
                reads=[R(r_ta), R(r_tb)], writes=[R(r_gs)])
            for i in range(NG):
                vop(lambda e, i=i: e.tensor_scalar(
                        RT[0][:, i * 4:(i + 1) * 4], r_gs[:, :],
                        0.0, r_gs[:, i:i + 1], ALU.mult, ALU.add),
                    reads=[R(r_gs)], writes=[RT[1]])
                vop(lambda e, i=i: e.tensor_copy(
                        RS[0][:, i * 4:(i + 1) * 4], r_gs[:, :]),
                    reads=[R(r_gs)], writes=[RS[1]])
            vop(lambda e: e.tensor_tensor(C1[0][:, 0:16], RS[0][:, 0:16],
                                          RT[0][:, 0:16], ALU.is_gt),
                reads=[RT[1], RS[1]], writes=[C1[1]])
            vop(lambda e: e.tensor_tensor(C2[0][:, 0:16], RS[0][:, 0:16],
                                          RT[0][:, 0:16], ALU.is_equal),
                reads=[RT[1], RS[1]], writes=[C2[1]])
            vop(lambda e: e.tensor_tensor(C2[0][:, 0:16], C2[0][:, 0:16],
                                          cp[:, LT4:LT4 + 16], ALU.mult),
                reads=[C2[1], CPRG], writes=[C2[1]])
            vop(lambda e: e.tensor_tensor(C1[0][:, 0:16], C1[0][:, 0:16],
                                          C2[0][:, 0:16], ALU.add),
                reads=[C1[1], C2[1]], writes=[C1[1]])
            vop(lambda e: e.reduce_sum(
                    r_rk[:, 0:4],
                    bass.AP(scr[2], 0, [[T, 128], [4, 4], [1, 4]]),
                    axis=AX.X),
                reads=[C1[1]], writes=[R(r_rk, 0, 4)])
            vop(lambda e: e.tensor_single_scalar(r_pn[:, :], r_rk[:, 0:4], 2.0,
                                                 ALU.is_lt),
                reads=[R(r_rk, 0, 4)], writes=[R(r_pn)])
            vop(lambda e: e.tensor_scalar(r_pn[:, :], r_pn[:, :], -1.0, 1e30,
                                          ALU.add, ALU.mult),
                reads=[R(r_pn)], writes=[R(r_pn)])
            for i in range(EPG):
                vop(lambda e, i=i, o16=o16: e.tensor_tensor(
                        apx(r_mb, i, [(4, 4)]),
                        apx(r_sb, o16 + i, [(4, 4)]),
                        r_pn[:, :], ALU.add),
                    reads=[sbrg, R(r_pn)], writes=[R(r_mb)])
            for i in range(NE):
                vop(lambda e, i=i: e.tensor_scalar(
                        RT[0][:, i * NE:(i + 1) * NE], r_mb[:, :],
                        0.0, r_mb[:, i:i + 1], ALU.mult, ALU.add),
                    reads=[R(r_mb)], writes=[RT[1]])
                vop(lambda e, i=i: e.tensor_copy(
                        RS[0][:, i * NE:(i + 1) * NE], r_mb[:, :]),
                    reads=[R(r_mb)], writes=[RS[1]])
            vop(lambda e: e.tensor_tensor(C1[0][:, :], RS[0][:, :], RT[0][:, :],
                                          ALU.is_gt),
                reads=[RT[1], RS[1]], writes=[C1[1]])
            vop(lambda e: e.tensor_tensor(C2[0][:, :], RS[0][:, :], RT[0][:, :],
                                          ALU.is_equal),
                reads=[RT[1], RS[1]], writes=[C2[1]])
            vop(lambda e: e.tensor_tensor(C2[0][:, :], C2[0][:, :],
                                          cp[:, LT16:LT16 + 256], ALU.mult),
                reads=[C2[1], CPRG], writes=[C2[1]])
            vop(lambda e: e.tensor_tensor(C1[0][:, :], C1[0][:, :], C2[0][:, :],
                                          ALU.add),
                reads=[C1[1], C2[1]], writes=[C1[1]])
            vop(lambda e: e.reduce_sum(
                    r_rk[:, :],
                    bass.AP(scr[2], 0, [[T, 128], [NE, NE], [1, NE]]),
                    axis=AX.X),
                reads=[C1[1]], writes=[R(r_rk)])
            vop(lambda e: e.tensor_single_scalar(r_kp[:, :], r_rk[:, :], 4.0,
                                                 ALU.is_lt),
                reads=[R(r_rk)], writes=[R(r_kp)])
            vop(lambda e, o16=o16: e.tensor_tensor(
                    r_kp[:, :], r_kp[:, :], r_sc[:, o16:o16 + NE], ALU.mult),
                reads=[R(r_kp), R(r_sc, o16, o16 + NE)], writes=[R(r_kp)])
            vop(lambda e: e.reduce_sum(r_dn[:, :], r_kp[:, :], axis=AX.X),
                reads=[R(r_kp)], writes=[R(r_dn)])
            vop(lambda e: e.tensor_scalar_add(r_dn[:, :], r_dn[:, :], 1e-20),
                reads=[R(r_dn)], writes=[R(r_dn)])
            vop(lambda e: e.reciprocal(r_dn[:, :], r_dn[:, :]),
                reads=[R(r_dn)], writes=[R(r_dn)])
            vop(lambda e, o16=o16: e.tensor_scalar_mul(
                    s_cmb[:, o16:o16 + NE], r_kp[:, :], r_dn[:, :]),
                reads=[R(r_kp), R(r_dn)], writes=[R(s_cmb, o16, o16 + NE)])

        xm = xsplit(XMH, t0)

        b.phase = "shared"
        prodp = []
        for j in range(SHC):
            sch, srg2 = wpair(sgw, j, D, cache=(sgp, "sg", j))
            pj = bank()
            mmg3(pj, [(sch(k)[0], sch(k)[1], srg2,
                       xm[k][0], xm[k][1], xm[k][2]) for k in range(DC)])
            pr_ap, pr_rg = PL(j * T)
            aop(lambda e, pj=pj, o=pr_ap: e.activation(o, pj[:, :], AF.Silu),
                reads=[PS(pj)], writes=[pr_rg])
        for j in range(SHC):
            sch, srg2 = wpair(suw, j, D, cache=(sup, "su", j))
            pj = bank()
            mmg3(pj, [(sch(k)[0], sch(k)[1], srg2,
                       xm[k][0], xm[k][1], xm[k][2]) for k in range(DC)])
            pr_ap, pr_rg = PL(j * T)
            sg_ = scr[j % 3]
            vop(lambda e, pj=pj, o=pr_ap, sg_=sg_: e.tensor_mul(
                    sg_[:, :], o, pj[:, :]),
                reads=[pr_rg, PS(pj)], writes=[R(sg_)])
            split_to_pool(sg_[:, :], R(sg_), j * T)
            ph = pool[:, j * T:j * T + T // 2].bitcast(F16)
            pl = pool[:, j * T + T // 2:(j + 1) * T].bitcast(F16)
            prodp.append((ph, pl, R(pool, j * T, (j + 1) * T)))

        for c in range(TSUB):
            pt = bank()
            b.op("tensor",
                 lambda e, pt=pt, c=c: e.transpose(
                     pt[0:16, c * 128:(c + 1) * 128],
                     s_cmb[:, c * NE:(c + 1) * NE], IDENT[0]),
                 reads=[R(s_cmb, c * NE, (c + 1) * NE), CPRG],
                 writes=[PS(pt)], inc="pe")
            vop(lambda e, pt=pt, c=c: e.tensor_copy(
                    s_cem[:, c * 128:(c + 1) * 128],
                    pt[0:16, c * 128:(c + 1) * 128]),
                reads=[PS(pt)], writes=[R(s_cem, c * 128, (c + 1) * 128)])

        b.phase = "experts"
        set_rot([pA, pB, pD])
        for wv in range(2):
            hwep = []
            for ei in range(8):
                e_ = wv * 8 + ei
                cr_ap, cr_rg = crow[e_ % 2]
                ones_e = ONESR32 if e_ % 2 == 0 else ONESR64
                dma("sync", cr_ap, s_cem[e_:e_ + 1, :],
                    reads=[R(s_cem)], writes=[cr_rg], sem=f"cr{e_ % 2}")
                mmg(pC, [(ones_e[0], ones_e[1], cr_ap, cr_rg)])
                rot4 = [(scr[0][:, :], R(scr[0])), (scr[1][:, :], R(scr[1])),
                        (scr[2][:, :], R(scr[2])), (rows_t[:, :], ("rows_t", 0, T))]
                chains = []
                for nb in range(EHC):
                    gch, grg = wpair(gw, e_ * EHC + nb, D,
                                     cache=(gwp, "g", e_ * EHC + nb))
                    pg = bank()
                    mmg3(pg, [(gch(k)[0], gch(k)[1], grg,
                               xm[k][0], xm[k][1], xm[k][2]) for k in range(DC)])
                    Xb = rot4[(e_ * EHC + nb) % 4]
                    chains.append(Xb)
                    aop(lambda e, pg=pg, Xb=Xb: e.activation(
                            Xb[0], pg[:, :], AF.Silu),
                        reads=[PS(pg)], writes=[Xb[1]])
                for nb in range(EHC):
                    uch, urg = wpair(uw, e_ * EHC + nb, D,
                                     cache=(uwp, "u", e_ * EHC + nb))
                    pu = bank()
                    mmg3(pu, [(uch(k)[0], uch(k)[1], urg,
                               xm[k][0], xm[k][1], xm[k][2]) for k in range(DC)])
                    Xb = chains[nb]
                    vop(lambda e, pu=pu, Xb=Xb: e.tensor_tensor(
                            Xb[0], Xb[0], pu[:, :], ALU.mult),
                        reads=[Xb[1], PS(pu)], writes=[Xb[1]])
                    vop(lambda e, Xb=Xb: e.tensor_tensor(
                            Xb[0], Xb[0], pC[:, :], ALU.mult),
                        reads=[Xb[1], PS(pC)], writes=[Xb[1]])
                    woff = HWB + (ei * EHC + nb) * T
                    split_to_pool(Xb[0], Xb[1], woff)
                    hh = pool[:, woff:woff + T // 2].bitcast(F16)
                    hl = pool[:, woff + T // 2:woff + T].bitcast(F16)
                    hwep.append((hh, hl, R(pool, woff, woff + T)))
            for n in range(DC):
                gv = b.semv["pe"] + 1
                items = []
                if wv == 0:
                    sch, srg2 = wpair(sdw, n, 2048, cache=(sdp, "sd", n))
                    for k in range(SHC):
                        wh, wl = sch(k)
                        items.append((wh, wl, srg2,
                                      prodp[k][0], prodp[k][1], prodp[k][2]))
                dch, drg = wpair(dw, n, 2048, colslice=(wv * 2048, (wv + 1) * 2048),
                                 cache=(dwp, "d", n * 2 + wv))
                for k in range(16):
                    wh, wl = dch(k)
                    items.append((wh, wl, drg,
                                  hwep[k][0], hwep[k][1], hwep[k][2]))
                mmg3(pD, items)
                if wv == 0:
                    fa_ap, fa_rg = PL(FAB + n * T)
                    vop(lambda e, o=fa_ap: e.tensor_copy(o, pD[:, :]),
                        reads=[PS(pD)], writes=[fa_rg])
                else:
                    fa_ap, fa_rg = PL(FAB + n * T)
                    xa4 = x_ap(n, t0, t0 + T)
                    vop(lambda e, fa=fa_ap: e.tensor_add(
                            scr[0][:, :], fa, pD[:, :]),
                        reads=[fa_rg, PS(pD)], writes=[R(scr[0])])
                    vop(lambda e, xa4=xa4: e.tensor_add(xa4, xa4, scr[0][:, :]),
                        reads=[x_rg(n, t0, t0 + T), R(scr[0])],
                        writes=[x_rg(n, t0, t0 + T)])
        set_rot([pA, pB, pD, pC])
        ych = [(x_ap(n, t0, t0 + T), x_rg(n, t0, t0 + T)) for n in range(DC)]
        zs = [(scr[0], R(scr[0])), (scr[1], R(scr[1]))]
        outs = [(zs[n % 2][0][:, :], zs[n % 2][1]) for n in range(DC)]

        def _store(n, t0=t0):
            dma("scalar", yout[:, n * SC + t0:n * SC + t0 + T],
                zs[n % 2][0][:, :],
                reads=[zs[n % 2][1]],
                writes=[R(yout, n * SC + t0, n * SC + t0 + T)],
                sem=f"st{n % 2}")
        layernorm(ych, EPS_OUT, outs, store_after=_store, sbase=0)

    for qeng, sems in b.dma_sems.items():
        for s in sorted(sems):
            b.wait(qeng, s, b.semv[s])
    b.realize()
    return nc


def _blockify(W):
    K, N = W.shape
    return np.ascontiguousarray(
        W.reshape(K // 128, 128, N // 128, 128).transpose(2, 1, 0, 3)
    ).reshape(N // 128, 128, K)


def _prep(ins):
    f = np.float32
    out = {}
    wqs, wks, wvs, wos, w1s, w2s = [], [], [], [], [], []
    for L in range(NL):
        wqs.append(_blockify(np.asarray(ins["L0_attn_Wq"][L], f)))
        wks.append(_blockify(np.asarray(ins["L0_attn_Wk"][L], f)))
        wvs.append(_blockify(np.asarray(ins["L0_attn_Wv"][L], f)))
        wos.append(_blockify(np.asarray(ins["L0_attn_Wo"][L], f)))
        w1s.append(_blockify(np.asarray(ins["L0_attn_W1"][L], f)))
        w2s.append(_blockify(np.asarray(ins["L0_attn_W2"][L], f)))
    out["wq"] = np.concatenate(wqs, 0)
    out["wk"] = np.concatenate(wks, 0)
    out["wv"] = np.concatenate(wvs, 0)
    out["wo"] = np.concatenate(wos, 0)
    out["w1"] = np.concatenate(w1s, 0)
    out["w2"] = np.concatenate(w2s, 0)
    out["gw"] = np.concatenate([_blockify(np.asarray(ins["L0_ffn_gate_w"][e], f))
                                for e in range(NE)], 0)
    out["uw"] = np.concatenate([_blockify(np.asarray(ins["L0_ffn_up_w"][e], f))
                                for e in range(NE)], 0)
    dwa = np.asarray(ins["L0_ffn_down_w"], f)
    dwt = dwa.reshape(NE, 2, 128, DC, 128).transpose(3, 2, 0, 1, 4)
    out["dw"] = np.ascontiguousarray(dwt).reshape(DC, 128, NE * 256)
    out["sgw"] = _blockify(np.ascontiguousarray(np.asarray(ins["L0_ffn_shared_gate_w"], f).T))
    out["suw"] = _blockify(np.ascontiguousarray(np.asarray(ins["L0_ffn_shared_up_w"], f).T))
    out["sdw"] = _blockify(np.ascontiguousarray(np.asarray(ins["L0_ffn_shared_down_w"], f).T))
    rwt = np.asarray(ins["L0_ffn_router_w"], f).T
    out["rwt"] = np.ascontiguousarray(rwt.reshape(DC, 128, NE).transpose(1, 0, 2)).reshape(128, DC * NE)
    cpk = np.zeros((128, CPW), f)

    def park(name, vec, per=8):
        arr = np.asarray(vec, f)
        for L in range(arr.shape[0]):
            cpk[:, CPOFF[name] + L * per:CPOFF[name] + (L + 1) * per] = \
                arr[L].reshape(per, 128).T

    park("bq", ins["L0_attn_bq"])
    park("bk", ins["L0_attn_bk"])
    park("bv", ins["L0_attn_bv"])
    park("bo", ins["L0_attn_bo"])
    park("l1s", ins["L0_attn_ln1_s"])
    park("l1b", ins["L0_attn_ln1_b"])
    park("l2s", ins["L0_attn_ln2_s"])
    park("l2b", ins["L0_attn_ln2_b"])
    park("b2", ins["L0_attn_b2"])
    park("b1", ins["L0_attn_b1"], per=32)
    cpk[:, CPOFF["cbb"]:CPOFF["cbb"] + NE] = np.asarray(
        ins["L0_ffn_correction_bias"], f)[None, :]
    i4 = np.arange(4)
    cpk[:, CPOFF["lt4"]:CPOFF["lt4"] + 16] = (
        i4[:, None] > i4[None, :]).astype(f).reshape(1, 16)
    i16 = np.arange(16)
    cpk[:, CPOFF["lt16"]:CPOFF["lt16"] + 256] = (
        i16[:, None] > i16[None, :]).astype(f).reshape(1, 256)
    cpk[:, CPOFF["ident"]:CPOFF["ident"] + 128] = np.eye(128, dtype=f)
    cpk[:, CPOFF["onec"]] = 1.0
    out["cpk"] = cpk
    out["onr"] = np.ones((128, 128), f)
    return out


def kernel(**inputs):
    n_cores = 8
    x = np.asarray(inputs["L0_attn_x"], np.float32)
    S = x.shape[1]
    SC = S // n_cores
    shared = _prep(inputs)
    nc = build(n_cores, SC)
    in_maps = []
    for c in range(n_cores):
        xc = np.ascontiguousarray(x[0, c * SC:(c + 1) * SC, :].T)
        xc = np.ascontiguousarray(
            xc.reshape(DC, 128, SC).transpose(1, 0, 2)).reshape(128, DC * SC)
        m = dict(shared)
        m["xin"] = xc
        in_maps.append(m)
    res = run_bass_kernel_spmd(nc, in_maps, core_ids=list(range(n_cores)))
    outs = []
    for c in range(n_cores):
        yc = np.asarray(res.results[c]["yout"])
        yc = yc.reshape(128, DC, SC).transpose(1, 0, 2).reshape(D, SC)
        outs.append(yc.T)
    return np.concatenate(outs, 0)[None, :, :].astype(np.float32)


# version 17
# speedup vs baseline: 1.0633x; 1.0633x over previous
import numpy as np
import concourse.bass as bass
import concourse.mybir as mybir
from concourse.bass_utils import run_bass_kernel_spmd

F32 = mybir.dt.float32
F16 = mybir.dt.float16
AF = mybir.ActivationFunctionType
ALU = mybir.AluOpType
AX = mybir.AxisListType
GELU = AF.Gelu_apprx_tanh

D, DC = 1024, 8
FF, FFC = 4096, 32
NL = 4
NE, EHC = 16, 2
SHC = 16
NG, EPG = 4, 4
EPS_IN, EPS_OUT = 1e-6, 1e-5
T = 512
TSUB = T // 128
WSLOT = 1024
NSLOT = 8

ENGINES = ("sync", "gpsimd", "tensor", "vector", "scalar")

_CP = [("bq", 32), ("bk", 32), ("bv", 32), ("bo", 32), ("l1s", 32), ("l1b", 32),
       ("l2s", 32), ("l2b", 32), ("b2", 32), ("b1", 128), ("cbb", 16),
       ("lt4", 16), ("lt16", 256), ("ident", 128), ("onec", 1)]
CPOFF = {}
_o = 0
for _n, _w in _CP:
    CPOFF[_n] = _o
    _o += _w
CPW = _o


class Bld:

    def __init__(self, nc):
        self.nc = nc
        self.q = {e: [] for e in ENGINES}
        self.sem = {}
        self.semv = {}
        for s in ("cc", "pe", "dv", "ac", "gq"):
            self.getsem(s)
        self.wm = {e: {} for e in ENGINES}
        self.dma_sems = {e: set() for e in ENGINES}
        self.tw = {}
        self.tr = {}
        self.phase = "init"
        self.oplog = []

    def getsem(self, s):
        if s not in self.sem:
            self.sem[s] = self.nc.alloc_semaphore(name="sem_" + s)
            self.semv[s] = 0
        return self.sem[s]

    def _deps(self, regions, writes):
        deps = []
        for (nm, lo, hi) in regions:
            for rec in self.tw.get(nm, ()):
                if rec[0] < hi and lo < rec[1]:
                    deps.append(rec)
            if writes:
                for rec in self.tr.get(nm, ()):
                    if rec[0] < hi and lo < rec[1]:
                        deps.append(rec)
        return deps

    def _record(self, table, regions, eng, sem, val, reads=False):
        for (nm, lo, hi) in regions:
            lst = table.setdefault(nm, [])
            if reads:
                lst[:] = [r for r in lst
                          if not (lo <= r[0] and r[1] <= hi and r[3] == sem)]
            else:
                lst[:] = [r for r in lst if not (lo <= r[0] and r[1] <= hi)]
            lst.append([lo, hi, eng, sem, val])

    def op(self, eng, fn, reads=(), writes=(), inc=None, amt=1, recsem=None,
           skipval=None):
        deps = self._deps(reads, False) + self._deps(writes, True)
        want = {}
        for (_, _, e2, s2, v2) in deps:
            assert s2 is not None, f"untracked producer consumed ({eng})"
            if skipval is not None and s2 == "pe" and v2 == skipval:
                continue
            if v2 > want.get(s2, -1):
                want[s2] = v2
        wl = []
        for s2, v2 in want.items():
            if self.wm[eng].get(s2, 0) >= v2:
                continue
            self.wm[eng][s2] = v2
            wl.append((s2, v2))
        if inc is not None:
            self.getsem(inc)
            self.semv[inc] += amt
            rs, rv = (inc, self.semv[inc]) if recsem is None else recsem
        else:
            assert recsem is not None, "op without inc needs recsem"
            rs, rv = recsem
        self._record(self.tr, reads, eng, rs, rv, reads=True)
        self._record(self.tw, writes, eng, rs, rv)
        for (nm, lo, hi) in writes:
            lst = self.tr.get(nm)
            if lst:
                lst[:] = [r for r in lst if not (lo <= r[0] and r[1] <= hi)]
        self.oplog.append((self.phase, eng, tuple(s for s, _ in wl)))
        self.q[eng].append((wl, fn, self.sem[inc] if inc else None, amt))

    def wait(self, eng, sem, val):
        if self.wm[eng].get(sem, 0) >= val:
            return
        self.wm[eng][sem] = val
        self.q[eng].append(([(sem, val)], None, None, 0))

    def realize(self):
        nc = self.nc
        with nc.Block() as block:
            for ename in ENGINES:
                ops = self.q[ename]

                def body(q, ops=ops):
                    for wl, fn, isem, amt in ops:
                        for (s, v) in wl:
                            q.wait_ge(self.sem[s], v)
                        if fn is not None:
                            inst = fn(q)
                            if isem is not None:
                                inst.then_inc(isem, amt)

                getattr(block, ename)(body)


def apx(t, off, dims):
    f = t.shape[1]
    return bass.AP(t, off, [[f, t.shape[0]]] + [list(d) for d in dims])


def build(n_cores, SC, nl=NL, do_outer=True, do_moe=True):
    NT = SC // T
    nc = bass.Bass("TRN2", target_bir_lowering=False)

    xin = nc.dram_tensor("xin", [128, DC * SC], F32, kind="ExternalInput")
    wq = nc.dram_tensor("wq", [NL * DC, 128, D], F32, kind="ExternalInput")
    wk = nc.dram_tensor("wk", [NL * DC, 128, D], F32, kind="ExternalInput")
    wv = nc.dram_tensor("wv", [NL * DC, 128, D], F32, kind="ExternalInput")
    wo = nc.dram_tensor("wo", [NL * DC, 128, D], F32, kind="ExternalInput")
    w1 = nc.dram_tensor("w1", [NL * FFC, 128, D], F32, kind="ExternalInput")
    w2 = nc.dram_tensor("w2", [NL * DC, 128, FF], F32, kind="ExternalInput")
    gw = nc.dram_tensor("gw", [NE * EHC, 128, D], F32, kind="ExternalInput")
    uw = nc.dram_tensor("uw", [NE * EHC, 128, D], F32, kind="ExternalInput")
    dw = nc.dram_tensor("dw", [DC, 128, NE * 256], F32, kind="ExternalInput")
    sgw = nc.dram_tensor("sgw", [SHC, 128, D], F32, kind="ExternalInput")
    suw = nc.dram_tensor("suw", [SHC, 128, D], F32, kind="ExternalInput")
    sdw = nc.dram_tensor("sdw", [DC, 128, 2048], F32, kind="ExternalInput")
    rwt = nc.dram_tensor("rwt", [128, DC * NE], F32, kind="ExternalInput")
    cpk = nc.dram_tensor("cpk", [128, CPW], F32, kind="ExternalInput")
    onr = nc.dram_tensor("onr", [128, 128], F32, kind="ExternalInput")
    yout = nc.dram_tensor("yout", [128, DC * SC], F32, kind="ExternalOutput")
    cci = nc.dram_tensor("cci", [2 * D], F32)
    cco = nc.dram_tensor("cco", [2 * D], F32)
    wqp = nc.dram_tensor("wqp", [NL * DC, 128, D], F32)
    wkp = nc.dram_tensor("wkp", [NL * DC, 128, D], F32)
    wvp = nc.dram_tensor("wvp", [NL * DC, 128, D], F32)
    wop = nc.dram_tensor("wop", [NL * DC, 128, D], F32)
    w1p = nc.dram_tensor("w1p", [NL * FFC, 128, D], F32)
    w2p = nc.dram_tensor("w2p", [NL * DC, 128, FF], F32)
    gwp = nc.dram_tensor("gwp", [NE * EHC, 128, D], F32)
    uwp = nc.dram_tensor("uwp", [NE * EHC, 128, D], F32)
    sgp = nc.dram_tensor("sgp", [SHC, 128, D], F32)
    sup = nc.dram_tensor("sup", [SHC, 128, D], F32)
    dwp = nc.dram_tensor("dwp", [DC, 128, NE * 256], F32)
    sdp = nc.dram_tensor("sdp", [DC, 128, 2048], F32)

    x_sb = nc.alloc_sbuf_tensor("x_sb", [128, DC * SC], F32)
    pool = nc.alloc_sbuf_tensor("pool", [128, 8192 + FFC * T], F32)
    wbuf = nc.alloc_sbuf_tensor("wbuf", [128, NSLOT * WSLOT], F32)
    cp = nc.alloc_sbuf_tensor("cp", [128, CPW], F32)
    onesr = nc.alloc_sbuf_tensor("onesr", [128, 128], F32)
    s_rwt = nc.alloc_sbuf_tensor("s_rwt", [128, DC * NE], F32)
    scr = [nc.alloc_sbuf_tensor(f"scr{i}", [128, T], F32) for i in range(3)]
    rows_t = nc.alloc_sbuf_tensor("rows_t", [128, T], F32)
    crow = [(rows_t[32:33, :], ("rows_t", 0, T)),
            (rows_t[64:65, :], ("rows_t", 0, T))]
    kvp4 = nc.alloc_sbuf_tensor("kvp4", [128, DC * NT], F32)
    ksp4 = nc.alloc_sbuf_tensor("ksp4", [128, DC * NT], F32)
    kvr = nc.alloc_sbuf_tensor("kvr", [128, DC], F32)
    ksr = nc.alloc_sbuf_tensor("ksr", [128, DC], F32)
    kvg = nc.alloc_sbuf_tensor("kvg", [128, DC], F32)
    ksg = nc.alloc_sbuf_tensor("ksg", [128, DC], F32)
    r_sc = nc.alloc_sbuf_tensor("r_sc", [128, NE * TSUB], F32)
    r_sb = nc.alloc_sbuf_tensor("r_sb", [128, NE * TSUB], F32)
    r_pr = nc.alloc_sbuf_tensor("r_pr", [128, 24], F32)
    r_gs = nc.alloc_sbuf_tensor("r_gs", [128, NG], F32)
    r_ta = nc.alloc_sbuf_tensor("r_ta", [128, NG], F32)
    r_tb = nc.alloc_sbuf_tensor("r_tb", [128, NG], F32)
    r_rk = nc.alloc_sbuf_tensor("r_rk", [128, NE], F32)
    r_pn = nc.alloc_sbuf_tensor("r_pn", [128, NG], F32)
    r_mb = nc.alloc_sbuf_tensor("r_mb", [128, NE], F32)
    r_kp = nc.alloc_sbuf_tensor("r_kp", [128, NE], F32)
    r_dn = nc.alloc_sbuf_tensor("r_dn", [128, 1], F32)
    s_cmb = nc.alloc_sbuf_tensor("s_cmb", [128, NE * TSUB], F32)
    s_cem = nc.alloc_sbuf_tensor("s_cem", [16, T], F32)

    pA = nc.alloc_psum_tensor("pA", [128, T], F32)
    pB = nc.alloc_psum_tensor("pB", [128, T], F32)
    pD = nc.alloc_psum_tensor("pD", [128, T], F32)
    pC = nc.alloc_psum_tensor("pC", [128, T], F32)
    pS1 = nc.alloc_psum_tensor("pS1", [1, T], F32)
    pS2 = nc.alloc_psum_tensor("pS2", [1, T], F32)
    pM = nc.alloc_psum_tensor("pM", [128, T], F32)
    pR = nc.alloc_psum_tensor("pR", [128, T], F32)

    b = Bld(nc)
    BIG = 1 << 30

    def R(t, lo=0, hi=BIG):
        return (t.name, lo, hi)

    def PS(t):
        return (t.name, 0, BIG)

    def dma(qeng, out_ap, in_ap, reads, writes, sem, recfinal=None):
        b.getsem(sem)
        b.dma_sems[qeng].add(sem)
        rs = (sem, recfinal) if recfinal is not None else None
        b.op(qeng, lambda e, o=out_ap, i=in_ap: e.dma_start(out=o, in_=i),
             reads=reads, writes=writes, inc=sem, amt=16, recsem=rs)

    def vop(fn, reads, writes):
        b.op("vector", fn, reads, writes, inc="dv")

    def aop(fn, reads, writes):
        b.op("scalar", fn, reads, writes, inc="ac")

    def gop(fn, reads, writes):
        b.op("gpsimd", fn, reads, writes, inc="gq")

    def pop_(par, fn, reads, writes):
        if par % 2 == 0:
            vop(fn, reads, writes)
        else:
            gop(fn, reads, writes)

    rot = [0]
    rot_list = [[None]]

    def bank():
        lst = rot_list[0]
        rot[0] = (rot[0] + 1) % len(lst)
        return lst[rot[0]]

    def set_rot(lst):
        rot_list[0] = lst
        rot[0] = 0

    def mmg(pt, pairs, ap_out=None):
        gv = b.semv["pe"] + 1
        n = len(pairs)
        out = ap_out if ap_out is not None else pt[:, :]
        for i, (la, lrg, ra, rrg) in enumerate(pairs):
            last = i == n - 1
            b.op("tensor",
                 lambda e, la=la, ra=ra, s=(i == 0), z=last, o=out:
                 e.matmul(o, la, ra, start=s, stop=z),
                 reads=[lrg, rrg], writes=[PS(pt)],
                 inc=("pe" if last else None), amt=1, recsem=("pe", gv),
                 skipval=gv)
        assert b.semv["pe"] == gv

    def mmg3(pt, items, ap_out=None):
        gv = b.semv["pe"] + 1
        out = ap_out if ap_out is not None else pt[:, :]
        n3 = len(items) * 3
        i = 0
        for (wh, wl, wrg, xh, xl, xrg) in items:
            for (la, ra) in ((wh, xh), (wl, xh), (wh, xl)):
                last = i == n3 - 1
                b.op("tensor",
                     lambda e, la=la, ra=ra, s=(i == 0), z=last, o=out:
                     e.matmul(o, la, ra, start=s, stop=z),
                     reads=[wrg, xrg], writes=[PS(pt)],
                     inc=("pe" if last else None), amt=1, recsem=("pe", gv),
                     skipval=gv)
                i += 1
        assert b.semv["pe"] == gv

    ring = [0]

    def wslot(units):
        if ring[0] + units > NSLOT:
            ring[0] = 0
        off = ring[0] * WSLOT
        ring[0] += units
        return off

    split_done = set()
    presplit_jobs = []

    def wpair(dram_t, blk, cols, colslice=None, cache=None, stage_q="sync"):
        u = cols // WSLOT
        if cache is not None:
            pdram, fam, idx = cache[0], cache[1], cache[2]
            first = (fam, idx) not in split_done
            if first:
                split_done.add((fam, idx))
        else:
            first = True
        if not first:
            po = wslot(u)
            prg = R(wbuf, po, po + cols)
            src = pdram[blk, :, :] if colslice is None else \
                pdram[blk, :, colslice[0]:colslice[1]]
            dma("sync", wbuf[:, po:po + cols], src,
                reads=[(fam, idx, idx + 1)], writes=[prg],
                sem=f"ws{po // WSLOT}")
        else:
            so = wslot(2 * u)
            po = so + cols
            srg = R(wbuf, so, so + cols)
            src = dram_t[blk, :, :] if colslice is None else \
                dram_t[blk, :, colslice[0]:colslice[1]]
            dma(stage_q, wbuf[:, so:so + cols], src, reads=[], writes=[srg],
                sem=f"w{'s' if stage_q == 'sync' else 'p'}{so // WSLOT}")
            prg = R(wbuf, po, po + cols)
            half = cols // 2
            hi_full = wbuf[:, po:po + half].bitcast(F16)
            lo_full = wbuf[:, po + half:po + cols].bitcast(F16)
            gop(lambda e, o=hi_full, i=wbuf[:, so:so + cols]: e.tensor_copy(o, i),
                reads=[srg], writes=[prg])
            vop(lambda e, o=lo_full, i=wbuf[:, so:so + cols], h=hi_full:
                e.tensor_tensor(o, i, h, ALU.subtract),
                reads=[srg, prg], writes=[prg])
            if cache is not None:
                dst = pdram[blk, :, :] if colslice is None else \
                    pdram[blk, :, colslice[0]:colslice[1]]
                dma("gpsimd", dst, wbuf[:, po:po + cols],
                    reads=[prg], writes=[(fam, idx, idx + 1)],
                    sem=f"wb{po // WSLOT}")
        half = cols // 2

        def chunk(k, po=po, half=half):
            whk = wbuf[:, po + k * 64:po + (k + 1) * 64].bitcast(F16)
            wlk = wbuf[:, po + half + k * 64:po + half + (k + 1) * 64].bitcast(F16)
            return whk, wlk

        return chunk, prg

    def presplit(k=1):
        while k > 0 and presplit_jobs:
            dram_t, blk, cols, colslice, cache = presplit_jobs.pop(0)
            fam, idx = cache[1], cache[2]
            if (fam, idx) in split_done:
                continue
            wpair(dram_t, blk, cols, colslice=colslice, cache=cache)
            k -= 1

    def queue_layer_presplit(L):
        for n in range(DC):
            presplit_jobs.append((wk, L * DC + n, D, None, (wkp, "k", L * DC + n)))
            presplit_jobs.append((wv, L * DC + n, D, None, (wvp, "v", L * DC + n)))
        for n in range(DC):
            presplit_jobs.append((wq, L * DC + n, D, None, (wqp, "q", L * DC + n)))
        for n in range(DC):
            presplit_jobs.append((wo, L * DC + n, D, None, (wop, "o", L * DC + n)))
        for ff in range(FFC):
            presplit_jobs.append((w1, L * FFC + ff, D, None, (w1p, "1", L * FFC + ff)))
        for n in range(DC):
            for quarter in range(4):
                presplit_jobs.append((w2, L * DC + n, WSLOT,
                                      (quarter * WSLOT, (quarter + 1) * WSLOT),
                                      (w2p, "2", (L * DC + n) * 4 + quarter)))

    def queue_moe_presplit():
        for j in range(SHC):
            presplit_jobs.append((sgw, j, D, None, (sgp, "sg", j)))
        for j in range(SHC):
            presplit_jobs.append((suw, j, D, None, (sup, "su", j)))
        for e_ in range(NE):
            for nb in range(EHC):
                presplit_jobs.append((gw, e_ * EHC + nb, D, None,
                                      (gwp, "g", e_ * EHC + nb)))
                presplit_jobs.append((uw, e_ * EHC + nb, D, None,
                                      (uwp, "u", e_ * EHC + nb)))
        for n in range(DC):
            presplit_jobs.append((sdw, n, 2048, None, (sdp, "sd", n)))
            for wv_ in range(2):
                presplit_jobs.append((dw, n, 2048, (wv_ * 2048, (wv_ + 1) * 2048),
                                      (dwp, "d", n * 2 + wv_)))

    def cpcol(name, i):
        o = CPOFF[name] + i
        return cp[:, o:o + 1]

    CPRG = R(cp)
    x_rg = lambda n, t0, t1: R(x_sb, n * SC + t0, n * SC + t1)
    x_ap = lambda n, t0, t1: x_sb[:, n * SC + t0:n * SC + t1]

    def PL(off, cols=T):
        return pool[:, off:off + cols], R(pool, off, off + cols)

    def PLH(off, cols):
        return pool[:, off:off + cols].bitcast(F16), R(pool, off, off + cols)

    def split_to_pool(src_ap, src_rg, off, caster="scalar", subeng=0):
        hi_ap, hi_rg = PLH(off, T // 2)
        lo_ap, lo_rg = PLH(off + T // 2, T // 2)
        if caster == "gpsimd":
            gop(lambda e, o=hi_ap, i=src_ap: e.tensor_copy(o, i),
                reads=[src_rg], writes=[hi_rg])
        else:
            aop(lambda e, o=hi_ap, i=src_ap: e.activation(o, i, AF.Copy),
                reads=[src_rg], writes=[hi_rg])
        pop_(subeng, lambda e, o=lo_ap, i=src_ap, h=hi_ap:
             e.tensor_tensor(o, i, h, ALU.subtract),
             [src_rg, hi_rg], [lo_rg])

    def xsplit(base, t0):
        res = []
        for k in range(DC):
            s_ap, s_rg = x_ap(k, t0, t0 + T), x_rg(k, t0, t0 + T)
            off = base + k * T
            split_to_pool(s_ap, s_rg, off)
            hi_ap = pool[:, off:off + T // 2].bitcast(F16)
            lo_ap = pool[:, off + T // 2:off + T].bitcast(F16)
            res.append((hi_ap, lo_ap, R(pool, off, off + T)))
        return res

    set_rot([pA, pB, pD, pC])

    dma("sync", cp[:, :], cpk[:, :], reads=[], writes=[CPRG],
        sem="scst", recfinal=48)
    dma("sync", onesr[:, :], onr[:, :], reads=[], writes=[R(onesr)],
        sem="scst", recfinal=48)
    dma("sync", s_rwt[:, :], rwt[:, :], reads=[], writes=[R(s_rwt)],
        sem="scst", recfinal=48)
    for n in range(DC):
        dma("sync", x_ap(n, 0, SC), xin[:, n * SC:(n + 1) * SC],
            reads=[], writes=[x_rg(n, 0, SC)], sem="sx", recfinal=16 * DC)

    ONEC = (cp[:, CPOFF["onec"]:CPOFF["onec"] + 1], CPRG)
    ONESR = (onesr[0:1, :], R(onesr))
    ONESR32 = (onesr[32:33, :], R(onesr))
    ONESR64 = (onesr[64:65, :], R(onesr))
    IDENT = (cp[:, CPOFF["ident"]:CPOFF["ident"] + 128], CPRG)

    def eluplus1_head(psum_t, bcol, sA, sB):
        vop(lambda e, sA=sA: e.tensor_scalar(sA[0], psum_t[:, :], bcol, 0.0,
                                             ALU.add, ALU.min),
            reads=[PS(psum_t), CPRG], writes=[sA[1]])
        vop(lambda e, sB=sB: e.tensor_scalar(sB[0], psum_t[:, :], bcol, 0.0,
                                             ALU.add, ALU.max),
            reads=[PS(psum_t), CPRG], writes=[sB[1]])
        aop(lambda e, sA=sA: e.activation(sA[0], sA[0], AF.Exp),
            reads=[sA[1]], writes=[sA[1]])

    def eluplus1_tail(out_ap, out_rg, sA, sB, acc_ap=None, acc_rg=None):
        if acc_ap is not None:
            vop(lambda e, sA=sA, sB=sB: e.scalar_tensor_tensor(
                    out_ap, sA[0], 1.0, sB[0], ALU.mult, ALU.add,
                    accum_out=acc_ap),
                reads=[sA[1], sB[1]], writes=[out_rg, acc_rg])
        else:
            vop(lambda e, sA=sA, sB=sB: e.scalar_tensor_tensor(
                    out_ap, sA[0], 1.0, sB[0], ALU.mult, ALU.add),
                reads=[sA[1], sB[1]], writes=[out_rg])

    def layernorm(y, eps, out, scale_name=None, bias_name=None, Lix=0,
                  store_after=None, sbase=4096):
        sD_ap, sD_rg = PL(sbase)
        sE_ap, sE_rg = PL(sbase + T)
        sC_ap, sC_rg = PL(sbase + 2 * T)
        vop(lambda e: e.tensor_add(sD_ap, y[0][0], y[1][0]),
            reads=[y[0][1], y[1][1]], writes=[sD_rg])
        for n in range(2, DC):
            vop(lambda e, n=n: e.tensor_add(sD_ap, sD_ap, y[n][0]),
                reads=[sD_rg, y[n][1]], writes=[sD_rg])
        vop(lambda e: e.tensor_mul(sE_ap, y[0][0], y[0][0]),
            reads=[y[0][1]], writes=[sE_rg])
        for n in range(1, DC):
            vop(lambda e, n=n: e.tensor_mul(sC_ap, y[n][0], y[n][0]),
                reads=[y[n][1]], writes=[sC_rg])
            vop(lambda e: e.tensor_add(sE_ap, sE_ap, sC_ap),
                reads=[sE_rg, sC_rg], writes=[sE_rg])
        mmg(pS1, [(ONEC[0], ONEC[1], sD_ap, sD_rg)])
        mmg(pS2, [(ONEC[0], ONEC[1], sE_ap, sE_rg)])
        rm_ap = pool[0:1, sbase + 3 * T:sbase + 4 * T]
        rm_rg = R(pool, sbase + 3 * T, sbase + 4 * T)
        rv_ap = pool[0:1, sbase + 4 * T:sbase + 5 * T]
        rv_rg = R(pool, sbase + 4 * T, sbase + 5 * T)
        vop(lambda e: e.tensor_scalar_mul(rm_ap, pS1[:, :], 1.0 / D),
            reads=[PS(pS1)], writes=[rm_rg])
        vop(lambda e: e.tensor_scalar_mul(rv_ap, pS2[:, :], 1.0 / D),
            reads=[PS(pS2)], writes=[rv_rg])
        mmg(pM, [(ONESR[0], ONESR[1], rm_ap, rm_rg)])
        vop(lambda e: e.tensor_mul(rm_ap, rm_ap, rm_ap),
            reads=[rm_rg], writes=[rm_rg])
        vop(lambda e: e.tensor_sub(rv_ap, rv_ap, rm_ap),
            reads=[rv_rg, rm_rg], writes=[rv_rg])
        vop(lambda e: e.tensor_scalar_add(rv_ap, rv_ap, float(eps)),
            reads=[rv_rg], writes=[rv_rg])
        aop(lambda e: e.activation(rv_ap, rv_ap, AF.Ln),
            reads=[rv_rg], writes=[rv_rg])
        aop(lambda e: e.activation(rv_ap, rv_ap, AF.Exp, scale=-0.5),
            reads=[rv_rg], writes=[rv_rg])
        mmg(pR, [(ONESR[0], ONESR[1], rv_ap, rv_rg)])
        for n in range(DC):
            vop(lambda e, n=n: e.tensor_sub(y[n][0], y[n][0], pM[:, :]),
                reads=[y[n][1], PS(pM)], writes=[y[n][1]])
            if scale_name is None:
                pop_(n, lambda e, n=n: e.tensor_mul(out[n][0], y[n][0], pR[:, :]),
                     [y[n][1], PS(pR)], [out[n][1]]) if False else \
                    vop(lambda e, n=n: e.tensor_mul(out[n][0], y[n][0], pR[:, :]),
                        reads=[y[n][1], PS(pR)], writes=[out[n][1]])
                if store_after is not None:
                    store_after(n)
            else:
                vop(lambda e, n=n: e.tensor_mul(y[n][0], y[n][0], pR[:, :]),
                    reads=[y[n][1], PS(pR)], writes=[y[n][1]])
                sc_ = cpcol(scale_name, Lix * DC + n)
                bc_ = cpcol(bias_name, Lix * DC + n)
                vop(lambda e, n=n, sc_=sc_, bc_=bc_: e.tensor_scalar(
                        out[n][0], y[n][0], sc_, bc_, ALU.mult, ALU.add),
                    reads=[y[n][1], CPRG], writes=[out[n][1]])

    for L in range(nl):
        if L + 1 < nl:
            queue_layer_presplit(L + 1)
        elif do_moe:
            queue_moe_presplit()
        for t in range(NT):
            t0 = t * T
            b.phase = "passA"
            xp = xsplit(4096, t0)
            pend_a = [None]
            for n in range(DC):
                kch, krg = wpair(wk, L * DC + n, D,
                                 cache=(wkp, "k", L * DC + n))
                vch, vrg2 = wpair(wv, L * DC + n, D,
                                  cache=(wvp, "v", L * DC + n))
                pk = bank()
                mmg3(pk, [(kch(k)[0], kch(k)[1], krg,
                           xp[k][0], xp[k][1], xp[k][2]) for k in range(DC)])
                pv = bank()
                mmg3(pv, [(vch(k)[0], vch(k)[1], vrg2,
                           xp[k][0], xp[k][1], xp[k][2]) for k in range(DC)])
                col = n * NT + t
                rr = (n % 2) * 4
                sk_ap, sk_rg = PL((rr + 2) * T)
                sv_ap, sv_rg = PL((rr + 3) * T)
                sA, sB = PL(rr * T), PL((rr + 1) * T)
                eluplus1_head(pk, cpcol("bk", L * DC + n), sA, sB)
                bvc = cpcol("bv", L * DC + n)
                vop(lambda e, pv=pv, o=sv_ap, bvc=bvc: e.tensor_scalar_add(
                        o, pv[:, :], bvc),
                    reads=[PS(pv), CPRG], writes=[sv_rg])
                if pend_a[0] is not None:
                    pend_a[0]()
                smul = (scr[2][:, :], R(scr[2])) if n % 2 == 0 else \
                    (rows_t[:, :], ("rows_t", 0, T))

                def tail_a(col=col, sA=sA, sB=sB, sk_ap=sk_ap, sk_rg=sk_rg,
                           sv_ap=sv_ap, sv_rg=sv_rg, smul=smul):
                    eluplus1_tail(sk_ap, sk_rg, sA, sB,
                                  acc_ap=ksp4[:, col:col + 1],
                                  acc_rg=R(ksp4, col, col + 1))
                    vop(lambda e: e.tensor_mul(smul[0], sk_ap, sv_ap),
                        reads=[sk_rg, sv_rg], writes=[smul[1]])
                    vop(lambda e: e.reduce_sum(
                            kvp4[:, col:col + 1], smul[0], axis=AX.X),
                        reads=[smul[1]], writes=[R(kvp4, col, col + 1)])
                pend_a[0] = tail_a
            pend_a[0]()
            pend_a[0] = None
        for (p4, red) in ((kvp4, kvr), (ksp4, ksr)):
            if NT == 1:
                vop(lambda e, p4=p4, red=red: e.tensor_copy(red[:, :], p4[:, :]),
                    reads=[R(p4)], writes=[R(red)])
            else:
                v0 = apx(p4, 0, [(NT, DC)])
                v1 = apx(p4, 1, [(NT, DC)])
                vop(lambda e, p4=p4, red=red, v0=v0, v1=v1: e.tensor_tensor(
                        red[:, :], v0, v1, ALU.add),
                    reads=[R(p4)], writes=[R(red)])
                for t in range(2, NT):
                    vt = apx(p4, t, [(NT, DC)])
                    vop(lambda e, red=red, vt=vt, p4=p4: e.tensor_tensor(
                            red[:, :], red[:, :], vt, ALU.add),
                        reads=[R(red), R(p4)], writes=[R(red)])
        gfin = 16 * 2 * (L + 1)
        dma("gpsimd", bass.AP(cci, 0, [[DC, 128], [1, DC]]), kvr[:, :],
            reads=[R(kvr)], writes=[R(cci, 0, D)], sem="gpi", recfinal=gfin)
        dma("gpsimd", bass.AP(cci, D, [[DC, 128], [1, DC]]), ksr[:, :],
            reads=[R(ksr)], writes=[R(cci, D, 2 * D)], sem="gpi", recfinal=gfin)
        b.op("gpsimd",
             lambda e: e.collective_compute(
                 "AllReduce", ALU.add, replica_groups=[list(range(n_cores))],
                 ins=[cci.ap().opt()], outs=[cco.ap().opt()]),
             reads=[R(cci, 0, 2 * D)], writes=[R(cco, 0, 2 * D)], inc="cc")
        dma("gpsimd", kvg[:, :], bass.AP(cco, 0, [[DC, 128], [1, DC]]),
            reads=[R(cco, 0, D)], writes=[R(kvg)], sem="gpo", recfinal=gfin)
        dma("gpsimd", ksg[:, :], bass.AP(cco, D, [[DC, 128], [1, DC]]),
            reads=[R(cco, D, 2 * D)], writes=[R(ksg)], sem="gpo", recfinal=gfin)

        for t in range(NT):
            t0 = t * T
            b.phase = "q"
            xp = xsplit(4096, t0)
            pend_q = [None]
            for n in range(DC):
                qch, qrg = wpair(wq, L * DC + n, D,
                                 cache=(wqp, "q", L * DC + n))
                pq = bank()
                mmg3(pq, [(qch(k)[0], qch(k)[1], qrg,
                           xp[k][0], xp[k][1], xp[k][2]) for k in range(DC)])
                q_ap, q_rg = PL(n * T)
                rr = n % 4
                sA, sB = PL(8192 + (2 * rr) * T), PL(8192 + (2 * rr + 1) * T)
                eluplus1_head(pq, cpcol("bq", L * DC + n), sA, sB)
                if pend_q[0] is not None:
                    pend_q[0]()
                pend_q[0] = (lambda q_ap=q_ap, q_rg=q_rg, sA=sA, sB=sB:
                             eluplus1_tail(q_ap, q_rg, sA, sB))
                presplit(1)
            pend_q[0]()
            b.phase = "V"
            vhl = []
            for n in range(DC):
                q_ap, q_rg = PL(n * T)
                sden = scr[0] if n % 2 == 0 else scr[2]
                vop(lambda e, q=q_ap, n=n, sd=sden: e.tensor_scalar(
                        sd[:, :], q, ksg[:, n:n + 1], EPS_IN,
                        ALU.mult, ALU.add),
                    reads=[q_rg, R(ksg)], writes=[R(sden)])
                vop(lambda e, sd=sden: e.reciprocal(sd[:, :], sd[:, :]),
                    reads=[R(sden)], writes=[R(sden)])
                sv2 = (scr[1][:, :], R(scr[1])) if n % 2 == 0 else \
                    (rows_t[:, :], ("rows_t", 0, T))
                vop(lambda e, q=q_ap, n=n, sd=sden, sv2=sv2: e.scalar_tensor_tensor(
                        sv2[0], q, kvg[:, n:n + 1], sd[:, :],
                        ALU.mult, ALU.mult),
                    reads=[q_rg, R(kvg), R(sden)], writes=[sv2[1]])
                split_to_pool(sv2[0], sv2[1], n * T)
                vh = pool[:, n * T:n * T + T // 2].bitcast(F16)
                vl = pool[:, n * T + T // 2:(n + 1) * T].bitcast(F16)
                vhl.append((vh, vl, R(pool, n * T, (n + 1) * T)))
            b.phase = "o"
            ych = []
            for n in range(DC):
                och, org = wpair(wo, L * DC + n, D,
                                 cache=(wop, "o", L * DC + n))
                po = bank()
                mmg3(po, [(och(k)[0], och(k)[1], org,
                           vhl[k][0], vhl[k][1], vhl[k][2])
                          for k in range(DC)])
                a_ap, a_rg = PL(4096 + n * T)
                boc = cpcol("bo", L * DC + n)
                aop(lambda e, po=po, o=a_ap, boc=boc: e.activation(
                        o, po[:, :], GELU, bias=boc),
                    reads=[PS(po), CPRG], writes=[a_rg])
                aop(lambda e, o=a_ap: e.activation(o, o, GELU),
                    reads=[a_rg], writes=[a_rg])
                ych.append((a_ap, a_rg))
                presplit(1)
            for n in range(DC):
                ya, yr, xa = ych[n][0], ych[n][1], x_ap(n, t0, t0 + T)
                vop(lambda e, ya=ya, xa=xa: e.tensor_add(ya, ya, xa),
                    reads=[yr, x_rg(n, t0, t0 + T)], writes=[yr])
            xout = [(x_ap(n, t0, t0 + T), x_rg(n, t0, t0 + T)) for n in range(DC)]
            layernorm(ych, EPS_IN, xout, "l1s", "l1b", L, sbase=0)
            b.phase = "W1"
            xp1 = xsplit(4096, t0)
            hph = []
            for ff in range(FFC):
                fch, frg = wpair(w1, L * FFC + ff, D,
                                 cache=(w1p, "1", L * FFC + ff))
                pf = bank()
                mmg3(pf, [(fch(k)[0], fch(k)[1], frg,
                           xp1[k][0], xp1[k][1], xp1[k][2]) for k in range(DC)])
                b1c = cpcol("b1", L * 32 + ff)
                sg_ = scr[ff % 3]
                aop(lambda e, pf=pf, b1c=b1c, sg_=sg_: e.activation(
                        sg_[:, :], pf[:, :], GELU, bias=b1c),
                    reads=[PS(pf), CPRG], writes=[R(sg_)])
                hoff = 8192 + ff * T
                split_to_pool(sg_[:, :], R(sg_), hoff)
                presplit(1)
                hh = pool[:, hoff:hoff + T // 2].bitcast(F16)
                hl = pool[:, hoff + T // 2:hoff + T].bitcast(F16)
                hph.append((hh, hl, R(pool, hoff, hoff + T)))
            b.phase = "W2"
            ych2 = []
            for n in range(DC):
                items = []
                for quarter in range(4):
                    qch, qrg = wpair(w2, L * DC + n, WSLOT,
                                     colslice=(quarter * WSLOT, (quarter + 1) * WSLOT),
                                     cache=(w2p, "2", (L * DC + n) * 4 + quarter))
                    for k in range(8):
                        kk = quarter * 8 + k
                        wh, wl = qch(k)
                        items.append((wh, wl, qrg,
                                      hph[kk][0], hph[kk][1], hph[kk][2]))
                pf2 = bank()
                mmg3(pf2, items)
                y2_ap, y2_rg = PL(n * T)
                b2c = cpcol("b2", L * DC + n)
                sg_ = scr[n % 3]
                aop(lambda e, pf2=pf2, b2c=b2c, sg_=sg_: e.activation(
                        sg_[:, :], pf2[:, :], GELU, bias=b2c),
                    reads=[PS(pf2), CPRG], writes=[R(sg_)])
                xa2 = x_ap(n, t0, t0 + T)
                vop(lambda e, o=y2_ap, xa2=xa2, sg_=sg_: e.tensor_add(
                        o, sg_[:, :], xa2),
                    reads=[R(sg_), x_rg(n, t0, t0 + T)], writes=[y2_rg])
                ych2.append((y2_ap, y2_rg))
                presplit(1)
            layernorm(ych2, EPS_IN, xout, "l2s", "l2b", L, sbase=4096)

    for t in range(NT if do_outer else 0):
        t0 = t * T
        ych = []
        for n in range(DC):
            xo_ap, xo_rg = PL(n * T)
            dma("sync", xo_ap, xin[:, n * SC + t0:n * SC + t0 + T],
                reads=[], writes=[xo_rg], sem="so",
                recfinal=16 * DC * (t + 1))
            xa3 = x_ap(n, t0, t0 + T)
            vop(lambda e, o=xo_ap, xa3=xa3: e.tensor_add(o, o, xa3),
                reads=[xo_rg, x_rg(n, t0, t0 + T)], writes=[xo_rg])
            ych.append((xo_ap, xo_rg))
        xout = [(x_ap(n, t0, t0 + T), x_rg(n, t0, t0 + T)) for n in range(DC)]
        layernorm(ych, EPS_OUT, xout, sbase=4096)

    LT16 = CPOFF["lt16"]
    LT4 = CPOFF["lt4"]
    XMH = 8192
    HWB = 12288
    FAB = 20480
    if not do_moe:
        for t in range(NT):
            t0 = t * T
            for n in range(DC):
                dma("scalar", yout[:, n * SC + t0:n * SC + t0 + T],
                    x_ap(n, t0, t0 + T), reads=[x_rg(n, t0, t0 + T)],
                    writes=[R(yout, n * SC + t0, n * SC + t0 + T)],
                    sem=f"st{n % 2}")
    for t in range(NT if do_moe else 0):
        t0 = t * T
        b.phase = "router"
        for c in range(TSUB):
            pr = bank()
            mmg(pr, [(x_ap(k, t0 + c * 128, t0 + (c + 1) * 128),
                      x_rg(k, t0 + c * 128, t0 + (c + 1) * 128),
                      s_rwt[:, k * NE:(k + 1) * NE], R(s_rwt))
                     for k in range(DC)],
                ap_out=pr[:, 0:NE])
            aop(lambda e, pr=pr, c=c: e.activation(
                    r_sc[:, c * NE:(c + 1) * NE], pr[:, 0:NE], AF.Sigmoid),
                reads=[PS(pr)], writes=[R(r_sc, c * NE, (c + 1) * NE)])
        for c in range(TSUB):
            vop(lambda e, c=c: e.tensor_tensor(
                    r_sb[:, c * NE:(c + 1) * NE], r_sc[:, c * NE:(c + 1) * NE],
                    cp[:, CPOFF["cbb"]:CPOFF["cbb"] + NE], ALU.add),
                reads=[R(r_sc, c * NE, (c + 1) * NE), CPRG],
                writes=[R(r_sb, c * NE, (c + 1) * NE)])
        RT = (scr[0][:, 0:256], R(scr[0]))
        RS = (scr[1][:, 0:256], R(scr[1]))
        C1 = (scr[2][:, 0:256], R(scr[2]))
        C2 = (rows_t[:, 0:256], ("rows_t", 0, T))
        for c in range(TSUB):
            o16 = c * NE
            sbrg = R(r_sb, o16, o16 + NE)
            pairs6 = [(0, 1), (0, 2), (0, 3), (1, 2), (1, 3), (2, 3)]
            for j, (i1, i2) in enumerate(pairs6):
                vop(lambda e, j=j, i1=i1, i2=i2, o16=o16: e.tensor_tensor(
                        r_pr[:, j * 4:(j + 1) * 4],
                        apx(r_sb, o16 + i1, [(4, 4)]),
                        apx(r_sb, o16 + i2, [(4, 4)]), ALU.add),
                    reads=[sbrg], writes=[R(r_pr, j * 4, (j + 1) * 4)])
            vop(lambda e: e.tensor_max(r_ta[:, :], r_pr[:, 0:4], r_pr[:, 4:8]),
                reads=[R(r_pr, 0, 8)], writes=[R(r_ta)])
            vop(lambda e: e.tensor_max(r_tb[:, :], r_pr[:, 8:12], r_pr[:, 12:16]),
                reads=[R(r_pr, 8, 16)], writes=[R(r_tb)])
            vop(lambda e: e.tensor_max(r_ta[:, :], r_ta[:, :], r_tb[:, :]),
                reads=[R(r_ta), R(r_tb)], writes=[R(r_ta)])
            vop(lambda e: e.tensor_max(r_tb[:, :], r_pr[:, 16:20], r_pr[:, 20:24]),
                reads=[R(r_pr, 16, 24)], writes=[R(r_tb)])
            vop(lambda e: e.tensor_max(r_gs[:, :], r_ta[:, :], r_tb[:, :]),
                reads=[R(r_ta), R(r_tb)], writes=[R(r_gs)])
            for i in range(NG):
                vop(lambda e, i=i: e.tensor_scalar(
                        RT[0][:, i * 4:(i + 1) * 4], r_gs[:, :],
                        0.0, r_gs[:, i:i + 1], ALU.mult, ALU.add),
                    reads=[R(r_gs)], writes=[RT[1]])
                vop(lambda e, i=i: e.tensor_copy(
                        RS[0][:, i * 4:(i + 1) * 4], r_gs[:, :]),
                    reads=[R(r_gs)], writes=[RS[1]])
            vop(lambda e: e.tensor_tensor(C1[0][:, 0:16], RS[0][:, 0:16],
                                          RT[0][:, 0:16], ALU.is_gt),
                reads=[RT[1], RS[1]], writes=[C1[1]])
            vop(lambda e: e.tensor_tensor(C2[0][:, 0:16], RS[0][:, 0:16],
                                          RT[0][:, 0:16], ALU.is_equal),
                reads=[RT[1], RS[1]], writes=[C2[1]])
            vop(lambda e: e.tensor_tensor(C2[0][:, 0:16], C2[0][:, 0:16],
                                          cp[:, LT4:LT4 + 16], ALU.mult),
                reads=[C2[1], CPRG], writes=[C2[1]])
            vop(lambda e: e.tensor_tensor(C1[0][:, 0:16], C1[0][:, 0:16],
                                          C2[0][:, 0:16], ALU.add),
                reads=[C1[1], C2[1]], writes=[C1[1]])
            vop(lambda e: e.reduce_sum(
                    r_rk[:, 0:4],
                    bass.AP(scr[2], 0, [[T, 128], [4, 4], [1, 4]]),
                    axis=AX.X),
                reads=[C1[1]], writes=[R(r_rk, 0, 4)])
            vop(lambda e: e.tensor_single_scalar(r_pn[:, :], r_rk[:, 0:4], 2.0,
                                                 ALU.is_lt),
                reads=[R(r_rk, 0, 4)], writes=[R(r_pn)])
            vop(lambda e: e.tensor_scalar(r_pn[:, :], r_pn[:, :], -1.0, 1e30,
                                          ALU.add, ALU.mult),
                reads=[R(r_pn)], writes=[R(r_pn)])
            for i in range(EPG):
                vop(lambda e, i=i, o16=o16: e.tensor_tensor(
                        apx(r_mb, i, [(4, 4)]),
                        apx(r_sb, o16 + i, [(4, 4)]),
                        r_pn[:, :], ALU.add),
                    reads=[sbrg, R(r_pn)], writes=[R(r_mb)])
            for i in range(NE):
                vop(lambda e, i=i: e.tensor_scalar(
                        RT[0][:, i * NE:(i + 1) * NE], r_mb[:, :],
                        0.0, r_mb[:, i:i + 1], ALU.mult, ALU.add),
                    reads=[R(r_mb)], writes=[RT[1]])
                vop(lambda e, i=i: e.tensor_copy(
                        RS[0][:, i * NE:(i + 1) * NE], r_mb[:, :]),
                    reads=[R(r_mb)], writes=[RS[1]])
            vop(lambda e: e.tensor_tensor(C1[0][:, :], RS[0][:, :], RT[0][:, :],
                                          ALU.is_gt),
                reads=[RT[1], RS[1]], writes=[C1[1]])
            vop(lambda e: e.tensor_tensor(C2[0][:, :], RS[0][:, :], RT[0][:, :],
                                          ALU.is_equal),
                reads=[RT[1], RS[1]], writes=[C2[1]])
            vop(lambda e: e.tensor_tensor(C2[0][:, :], C2[0][:, :],
                                          cp[:, LT16:LT16 + 256], ALU.mult),
                reads=[C2[1], CPRG], writes=[C2[1]])
            vop(lambda e: e.tensor_tensor(C1[0][:, :], C1[0][:, :], C2[0][:, :],
                                          ALU.add),
                reads=[C1[1], C2[1]], writes=[C1[1]])
            vop(lambda e: e.reduce_sum(
                    r_rk[:, :],
                    bass.AP(scr[2], 0, [[T, 128], [NE, NE], [1, NE]]),
                    axis=AX.X),
                reads=[C1[1]], writes=[R(r_rk)])
            vop(lambda e: e.tensor_single_scalar(r_kp[:, :], r_rk[:, :], 4.0,
                                                 ALU.is_lt),
                reads=[R(r_rk)], writes=[R(r_kp)])
            vop(lambda e, o16=o16: e.tensor_tensor(
                    r_kp[:, :], r_kp[:, :], r_sc[:, o16:o16 + NE], ALU.mult),
                reads=[R(r_kp), R(r_sc, o16, o16 + NE)], writes=[R(r_kp)])
            vop(lambda e: e.reduce_sum(r_dn[:, :], r_kp[:, :], axis=AX.X),
                reads=[R(r_kp)], writes=[R(r_dn)])
            vop(lambda e: e.tensor_scalar_add(r_dn[:, :], r_dn[:, :], 1e-20),
                reads=[R(r_dn)], writes=[R(r_dn)])
            vop(lambda e: e.reciprocal(r_dn[:, :], r_dn[:, :]),
                reads=[R(r_dn)], writes=[R(r_dn)])
            vop(lambda e, o16=o16: e.tensor_scalar_mul(
                    s_cmb[:, o16:o16 + NE], r_kp[:, :], r_dn[:, :]),
                reads=[R(r_kp), R(r_dn)], writes=[R(s_cmb, o16, o16 + NE)])

        xm = xsplit(XMH, t0)

        b.phase = "shared"
        prodp = []
        for j in range(SHC):
            sch, srg2 = wpair(sgw, j, D, cache=(sgp, "sg", j))
            pj = bank()
            mmg3(pj, [(sch(k)[0], sch(k)[1], srg2,
                       xm[k][0], xm[k][1], xm[k][2]) for k in range(DC)])
            pr_ap, pr_rg = PL(j * T)
            aop(lambda e, pj=pj, o=pr_ap: e.activation(o, pj[:, :], AF.Silu),
                reads=[PS(pj)], writes=[pr_rg])
        for j in range(SHC):
            sch, srg2 = wpair(suw, j, D, cache=(sup, "su", j))
            pj = bank()
            mmg3(pj, [(sch(k)[0], sch(k)[1], srg2,
                       xm[k][0], xm[k][1], xm[k][2]) for k in range(DC)])
            pr_ap, pr_rg = PL(j * T)
            sg_ = scr[j % 3]
            vop(lambda e, pj=pj, o=pr_ap, sg_=sg_: e.tensor_mul(
                    sg_[:, :], o, pj[:, :]),
                reads=[pr_rg, PS(pj)], writes=[R(sg_)])
            split_to_pool(sg_[:, :], R(sg_), j * T)
            ph = pool[:, j * T:j * T + T // 2].bitcast(F16)
            pl = pool[:, j * T + T // 2:(j + 1) * T].bitcast(F16)
            prodp.append((ph, pl, R(pool, j * T, (j + 1) * T)))

        for c in range(TSUB):
            pt = bank()
            b.op("tensor",
                 lambda e, pt=pt, c=c: e.transpose(
                     pt[0:16, c * 128:(c + 1) * 128],
                     s_cmb[:, c * NE:(c + 1) * NE], IDENT[0]),
                 reads=[R(s_cmb, c * NE, (c + 1) * NE), CPRG],
                 writes=[PS(pt)], inc="pe")
            vop(lambda e, pt=pt, c=c: e.tensor_copy(
                    s_cem[:, c * 128:(c + 1) * 128],
                    pt[0:16, c * 128:(c + 1) * 128]),
                reads=[PS(pt)], writes=[R(s_cem, c * 128, (c + 1) * 128)])

        b.phase = "experts"
        set_rot([pA, pB, pD])
        for wv in range(2):
            hwep = []
            for ei in range(8):
                e_ = wv * 8 + ei
                cr_ap, cr_rg = crow[e_ % 2]
                ones_e = ONESR32 if e_ % 2 == 0 else ONESR64
                dma("sync", cr_ap, s_cem[e_:e_ + 1, :],
                    reads=[R(s_cem)], writes=[cr_rg], sem=f"cr{e_ % 2}")
                mmg(pC, [(ones_e[0], ones_e[1], cr_ap, cr_rg)])
                rot4 = [(scr[0][:, :], R(scr[0])), (scr[1][:, :], R(scr[1])),
                        (scr[2][:, :], R(scr[2])), (rows_t[:, :], ("rows_t", 0, T))]
                chains = []
                for nb in range(EHC):
                    gch, grg = wpair(gw, e_ * EHC + nb, D,
                                     cache=(gwp, "g", e_ * EHC + nb))
                    pg = bank()
                    mmg3(pg, [(gch(k)[0], gch(k)[1], grg,
                               xm[k][0], xm[k][1], xm[k][2]) for k in range(DC)])
                    Xb = rot4[(e_ * EHC + nb) % 4]
                    chains.append(Xb)
                    aop(lambda e, pg=pg, Xb=Xb: e.activation(
                            Xb[0], pg[:, :], AF.Silu),
                        reads=[PS(pg)], writes=[Xb[1]])
                for nb in range(EHC):
                    uch, urg = wpair(uw, e_ * EHC + nb, D,
                                     cache=(uwp, "u", e_ * EHC + nb))
                    pu = bank()
                    mmg3(pu, [(uch(k)[0], uch(k)[1], urg,
                               xm[k][0], xm[k][1], xm[k][2]) for k in range(DC)])
                    Xb = chains[nb]
                    vop(lambda e, pu=pu, Xb=Xb: e.tensor_tensor(
                            Xb[0], Xb[0], pu[:, :], ALU.mult),
                        reads=[Xb[1], PS(pu)], writes=[Xb[1]])
                    vop(lambda e, Xb=Xb: e.tensor_tensor(
                            Xb[0], Xb[0], pC[:, :], ALU.mult),
                        reads=[Xb[1], PS(pC)], writes=[Xb[1]])
                    woff = HWB + (ei * EHC + nb) * T
                    split_to_pool(Xb[0], Xb[1], woff)
                    hh = pool[:, woff:woff + T // 2].bitcast(F16)
                    hl = pool[:, woff + T // 2:woff + T].bitcast(F16)
                    hwep.append((hh, hl, R(pool, woff, woff + T)))
            for n in range(DC):
                gv = b.semv["pe"] + 1
                items = []
                if wv == 0:
                    sch, srg2 = wpair(sdw, n, 2048, cache=(sdp, "sd", n))
                    for k in range(SHC):
                        wh, wl = sch(k)
                        items.append((wh, wl, srg2,
                                      prodp[k][0], prodp[k][1], prodp[k][2]))
                dch, drg = wpair(dw, n, 2048, colslice=(wv * 2048, (wv + 1) * 2048),
                                 cache=(dwp, "d", n * 2 + wv))
                for k in range(16):
                    wh, wl = dch(k)
                    items.append((wh, wl, drg,
                                  hwep[k][0], hwep[k][1], hwep[k][2]))
                mmg3(pD, items)
                if wv == 0:
                    fa_ap, fa_rg = PL(FAB + n * T)
                    vop(lambda e, o=fa_ap: e.tensor_copy(o, pD[:, :]),
                        reads=[PS(pD)], writes=[fa_rg])
                else:
                    fa_ap, fa_rg = PL(FAB + n * T)
                    xa4 = x_ap(n, t0, t0 + T)
                    vop(lambda e, fa=fa_ap: e.tensor_add(
                            scr[0][:, :], fa, pD[:, :]),
                        reads=[fa_rg, PS(pD)], writes=[R(scr[0])])
                    vop(lambda e, xa4=xa4: e.tensor_add(xa4, xa4, scr[0][:, :]),
                        reads=[x_rg(n, t0, t0 + T), R(scr[0])],
                        writes=[x_rg(n, t0, t0 + T)])
        set_rot([pA, pB, pD, pC])
        ych = [(x_ap(n, t0, t0 + T), x_rg(n, t0, t0 + T)) for n in range(DC)]
        zs = [(scr[0], R(scr[0])), (scr[1], R(scr[1]))]
        outs = [(zs[n % 2][0][:, :], zs[n % 2][1]) for n in range(DC)]

        def _store(n, t0=t0):
            dma("scalar", yout[:, n * SC + t0:n * SC + t0 + T],
                zs[n % 2][0][:, :],
                reads=[zs[n % 2][1]],
                writes=[R(yout, n * SC + t0, n * SC + t0 + T)],
                sem=f"st{n % 2}")
        layernorm(ych, EPS_OUT, outs, store_after=_store, sbase=0)

    for qeng, sems in b.dma_sems.items():
        for s in sorted(sems):
            b.wait(qeng, s, b.semv[s])
    b.realize()
    return nc


def _blockify(W):
    K, N = W.shape
    return np.ascontiguousarray(
        W.reshape(K // 128, 128, N // 128, 128).transpose(2, 1, 0, 3)
    ).reshape(N // 128, 128, K)


def _prep(ins):
    f = np.float32
    out = {}
    wqs, wks, wvs, wos, w1s, w2s = [], [], [], [], [], []
    for L in range(NL):
        wqs.append(_blockify(np.asarray(ins["L0_attn_Wq"][L], f)))
        wks.append(_blockify(np.asarray(ins["L0_attn_Wk"][L], f)))
        wvs.append(_blockify(np.asarray(ins["L0_attn_Wv"][L], f)))
        wos.append(_blockify(np.asarray(ins["L0_attn_Wo"][L], f)))
        w1s.append(_blockify(np.asarray(ins["L0_attn_W1"][L], f)))
        w2s.append(_blockify(np.asarray(ins["L0_attn_W2"][L], f)))
    out["wq"] = np.concatenate(wqs, 0)
    out["wk"] = np.concatenate(wks, 0)
    out["wv"] = np.concatenate(wvs, 0)
    out["wo"] = np.concatenate(wos, 0)
    out["w1"] = np.concatenate(w1s, 0)
    out["w2"] = np.concatenate(w2s, 0)
    out["gw"] = np.concatenate([_blockify(np.asarray(ins["L0_ffn_gate_w"][e], f))
                                for e in range(NE)], 0)
    out["uw"] = np.concatenate([_blockify(np.asarray(ins["L0_ffn_up_w"][e], f))
                                for e in range(NE)], 0)
    dwa = np.asarray(ins["L0_ffn_down_w"], f)
    dwt = dwa.reshape(NE, 2, 128, DC, 128).transpose(3, 2, 0, 1, 4)
    out["dw"] = np.ascontiguousarray(dwt).reshape(DC, 128, NE * 256)
    out["sgw"] = _blockify(np.ascontiguousarray(np.asarray(ins["L0_ffn_shared_gate_w"], f).T))
    out["suw"] = _blockify(np.ascontiguousarray(np.asarray(ins["L0_ffn_shared_up_w"], f).T))
    out["sdw"] = _blockify(np.ascontiguousarray(np.asarray(ins["L0_ffn_shared_down_w"], f).T))
    rwt = np.asarray(ins["L0_ffn_router_w"], f).T
    out["rwt"] = np.ascontiguousarray(rwt.reshape(DC, 128, NE).transpose(1, 0, 2)).reshape(128, DC * NE)
    cpk = np.zeros((128, CPW), f)

    def park(name, vec, per=8):
        arr = np.asarray(vec, f)
        for L in range(arr.shape[0]):
            cpk[:, CPOFF[name] + L * per:CPOFF[name] + (L + 1) * per] = \
                arr[L].reshape(per, 128).T

    park("bq", ins["L0_attn_bq"])
    park("bk", ins["L0_attn_bk"])
    park("bv", ins["L0_attn_bv"])
    park("bo", ins["L0_attn_bo"])
    park("l1s", ins["L0_attn_ln1_s"])
    park("l1b", ins["L0_attn_ln1_b"])
    park("l2s", ins["L0_attn_ln2_s"])
    park("l2b", ins["L0_attn_ln2_b"])
    park("b2", ins["L0_attn_b2"])
    park("b1", ins["L0_attn_b1"], per=32)
    cpk[:, CPOFF["cbb"]:CPOFF["cbb"] + NE] = np.asarray(
        ins["L0_ffn_correction_bias"], f)[None, :]
    i4 = np.arange(4)
    cpk[:, CPOFF["lt4"]:CPOFF["lt4"] + 16] = (
        i4[:, None] > i4[None, :]).astype(f).reshape(1, 16)
    i16 = np.arange(16)
    cpk[:, CPOFF["lt16"]:CPOFF["lt16"] + 256] = (
        i16[:, None] > i16[None, :]).astype(f).reshape(1, 256)
    cpk[:, CPOFF["ident"]:CPOFF["ident"] + 128] = np.eye(128, dtype=f)
    cpk[:, CPOFF["onec"]] = 1.0
    out["cpk"] = cpk
    out["onr"] = np.ones((128, 128), f)
    return out


def kernel(**inputs):
    n_cores = 8
    x = np.asarray(inputs["L0_attn_x"], np.float32)
    S = x.shape[1]
    SC = S // n_cores
    shared = _prep(inputs)
    nc = build(n_cores, SC)
    in_maps = []
    for c in range(n_cores):
        xc = np.ascontiguousarray(x[0, c * SC:(c + 1) * SC, :].T)
        xc = np.ascontiguousarray(
            xc.reshape(DC, 128, SC).transpose(1, 0, 2)).reshape(128, DC * SC)
        m = dict(shared)
        m["xin"] = xc
        in_maps.append(m)
    res = run_bass_kernel_spmd(nc, in_maps, core_ids=list(range(n_cores)))
    outs = []
    for c in range(n_cores):
        yc = np.asarray(res.results[c]["yout"])
        yc = yc.reshape(128, DC, SC).transpose(1, 0, 2).reshape(D, SC)
        outs.append(yc.T)
    return np.concatenate(outs, 0)[None, :, :].astype(np.float32)


# version 18
# speedup vs baseline: 1.0809x; 1.0166x over previous
import numpy as np
import concourse.bass as bass
import concourse.mybir as mybir
from concourse.bass_utils import run_bass_kernel_spmd

F32 = mybir.dt.float32
F16 = mybir.dt.float16
AF = mybir.ActivationFunctionType
ALU = mybir.AluOpType
AX = mybir.AxisListType
GELU = AF.Gelu_apprx_tanh

D, DC = 1024, 8
FF, FFC = 4096, 32
NL = 4
NE, EHC = 16, 2
SHC = 16
NG, EPG = 4, 4
EPS_IN, EPS_OUT = 1e-6, 1e-5
T = 512
TSUB = T // 128
WSLOT = 1024
NSLOT = 8

ENGINES = ("sync", "gpsimd", "tensor", "vector", "scalar")

_CP = [("bq", 32), ("bk", 32), ("bv", 32), ("bo", 32), ("l1s", 32), ("l1b", 32),
       ("l2s", 32), ("l2b", 32), ("b2", 32), ("b1", 128), ("cbb", 16),
       ("lt4", 16), ("lt16", 256), ("ident", 128), ("onec", 1)]
CPOFF = {}
_o = 0
for _n, _w in _CP:
    CPOFF[_n] = _o
    _o += _w
CPW = _o


class Bld:

    def __init__(self, nc):
        self.nc = nc
        self.q = {e: [] for e in ENGINES}
        self.sem = {}
        self.semv = {}
        for s in ("cc", "pe", "dv", "ac", "gq"):
            self.getsem(s)
        self.wm = {e: {} for e in ENGINES}
        self.dma_sems = {e: set() for e in ENGINES}
        self.tw = {}
        self.tr = {}
        self.phase = "init"
        self.oplog = []

    def getsem(self, s):
        if s not in self.sem:
            self.sem[s] = self.nc.alloc_semaphore(name="sem_" + s)
            self.semv[s] = 0
        return self.sem[s]

    def _deps(self, regions, writes):
        deps = []
        for (nm, lo, hi) in regions:
            for rec in self.tw.get(nm, ()):
                if rec[0] < hi and lo < rec[1]:
                    deps.append(rec)
            if writes:
                for rec in self.tr.get(nm, ()):
                    if rec[0] < hi and lo < rec[1]:
                        deps.append(rec)
        return deps

    def _record(self, table, regions, eng, sem, val, reads=False):
        for (nm, lo, hi) in regions:
            lst = table.setdefault(nm, [])
            if reads:
                lst[:] = [r for r in lst
                          if not (lo <= r[0] and r[1] <= hi and r[3] == sem)]
            else:
                lst[:] = [r for r in lst if not (lo <= r[0] and r[1] <= hi)]
            lst.append([lo, hi, eng, sem, val])

    def op(self, eng, fn, reads=(), writes=(), inc=None, amt=1, recsem=None,
           skipval=None):
        deps = self._deps(reads, False) + self._deps(writes, True)
        want = {}
        for (_, _, e2, s2, v2) in deps:
            assert s2 is not None, f"untracked producer consumed ({eng})"
            if skipval is not None and s2 == "pe" and v2 == skipval:
                continue
            if v2 > want.get(s2, -1):
                want[s2] = v2
        wl = []
        for s2, v2 in want.items():
            if self.wm[eng].get(s2, 0) >= v2:
                continue
            self.wm[eng][s2] = v2
            wl.append((s2, v2))
        if inc is not None:
            self.getsem(inc)
            self.semv[inc] += amt
            rs, rv = (inc, self.semv[inc]) if recsem is None else recsem
        else:
            assert recsem is not None, "op without inc needs recsem"
            rs, rv = recsem
        self._record(self.tr, reads, eng, rs, rv, reads=True)
        self._record(self.tw, writes, eng, rs, rv)
        for (nm, lo, hi) in writes:
            lst = self.tr.get(nm)
            if lst:
                lst[:] = [r for r in lst if not (lo <= r[0] and r[1] <= hi)]
        self.oplog.append((self.phase, eng, tuple(s for s, _ in wl)))
        self.q[eng].append((wl, fn, self.sem[inc] if inc else None, amt))

    def wait(self, eng, sem, val):
        if self.wm[eng].get(sem, 0) >= val:
            return
        self.wm[eng][sem] = val
        self.q[eng].append(([(sem, val)], None, None, 0))

    def realize(self):
        nc = self.nc
        with nc.Block() as block:
            for ename in ENGINES:
                ops = self.q[ename]

                def body(q, ops=ops):
                    for wl, fn, isem, amt in ops:
                        for (s, v) in wl:
                            q.wait_ge(self.sem[s], v)
                        if fn is not None:
                            inst = fn(q)
                            if isem is not None:
                                inst.then_inc(isem, amt)

                getattr(block, ename)(body)


def apx(t, off, dims):
    f = t.shape[1]
    return bass.AP(t, off, [[f, t.shape[0]]] + [list(d) for d in dims])


def build(n_cores, SC, nl=NL, do_outer=True, do_moe=True):
    NT = SC // T
    nc = bass.Bass("TRN2", target_bir_lowering=False)

    xin = nc.dram_tensor("xin", [128, DC * SC], F32, kind="ExternalInput")
    wq = nc.dram_tensor("wq", [NL * DC, 128, D], F32, kind="ExternalInput")
    wk = nc.dram_tensor("wk", [NL * DC, 128, D], F32, kind="ExternalInput")
    wv = nc.dram_tensor("wv", [NL * DC, 128, D], F32, kind="ExternalInput")
    wo = nc.dram_tensor("wo", [NL * DC, 128, D], F32, kind="ExternalInput")
    w1 = nc.dram_tensor("w1", [NL * FFC, 128, D], F32, kind="ExternalInput")
    w2 = nc.dram_tensor("w2", [NL * DC, 128, FF], F32, kind="ExternalInput")
    gw = nc.dram_tensor("gw", [NE * EHC, 128, D], F32, kind="ExternalInput")
    uw = nc.dram_tensor("uw", [NE * EHC, 128, D], F32, kind="ExternalInput")
    dw = nc.dram_tensor("dw", [DC, 128, NE * 256], F32, kind="ExternalInput")
    sgw = nc.dram_tensor("sgw", [SHC, 128, D], F32, kind="ExternalInput")
    suw = nc.dram_tensor("suw", [SHC, 128, D], F32, kind="ExternalInput")
    sdw = nc.dram_tensor("sdw", [DC, 128, 2048], F32, kind="ExternalInput")
    rwt = nc.dram_tensor("rwt", [128, DC * NE], F32, kind="ExternalInput")
    cpk = nc.dram_tensor("cpk", [128, CPW], F32, kind="ExternalInput")
    onr = nc.dram_tensor("onr", [128, 128], F32, kind="ExternalInput")
    yout = nc.dram_tensor("yout", [128, DC * SC], F32, kind="ExternalOutput")
    cci = nc.dram_tensor("cci", [2 * D], F32)
    cco = nc.dram_tensor("cco", [2 * D], F32)
    wqp = nc.dram_tensor("wqp", [NL * DC, 128, D], F32)
    wkp = nc.dram_tensor("wkp", [NL * DC, 128, D], F32)
    wvp = nc.dram_tensor("wvp", [NL * DC, 128, D], F32)
    wop = nc.dram_tensor("wop", [NL * DC, 128, D], F32)
    w1p = nc.dram_tensor("w1p", [NL * FFC, 128, D], F32)
    w2p = nc.dram_tensor("w2p", [NL * DC, 128, FF], F32)
    gwp = nc.dram_tensor("gwp", [NE * EHC, 128, D], F32)
    uwp = nc.dram_tensor("uwp", [NE * EHC, 128, D], F32)
    sgp = nc.dram_tensor("sgp", [SHC, 128, D], F32)
    sup = nc.dram_tensor("sup", [SHC, 128, D], F32)
    dwp = nc.dram_tensor("dwp", [DC, 128, NE * 256], F32)
    sdp = nc.dram_tensor("sdp", [DC, 128, 2048], F32)

    x_sb = nc.alloc_sbuf_tensor("x_sb", [128, DC * SC], F32)
    pool = nc.alloc_sbuf_tensor("pool", [128, 8192 + FFC * T], F32)
    wbuf = nc.alloc_sbuf_tensor("wbuf", [128, NSLOT * WSLOT], F32)
    cp = nc.alloc_sbuf_tensor("cp", [128, CPW], F32)
    onesr = nc.alloc_sbuf_tensor("onesr", [128, 128], F32)
    s_rwt = nc.alloc_sbuf_tensor("s_rwt", [128, DC * NE], F32)
    scr = [nc.alloc_sbuf_tensor(f"scr{i}", [128, T], F32) for i in range(3)]
    rows_t = nc.alloc_sbuf_tensor("rows_t", [128, T], F32)
    crow = [(rows_t[32:33, :], ("rows_t", 0, T)),
            (rows_t[64:65, :], ("rows_t", 0, T))]
    kvp4 = nc.alloc_sbuf_tensor("kvp4", [128, DC * NT], F32)
    ksp4 = nc.alloc_sbuf_tensor("ksp4", [128, DC * NT], F32)
    kvr = nc.alloc_sbuf_tensor("kvr", [128, DC], F32)
    ksr = nc.alloc_sbuf_tensor("ksr", [128, DC], F32)
    kvg = nc.alloc_sbuf_tensor("kvg", [128, DC], F32)
    ksg = nc.alloc_sbuf_tensor("ksg", [128, DC], F32)
    r_sc = nc.alloc_sbuf_tensor("r_sc", [128, NE * TSUB], F32)
    r_sb = nc.alloc_sbuf_tensor("r_sb", [128, NE * TSUB], F32)
    r_pr = nc.alloc_sbuf_tensor("r_pr", [128, 24], F32)
    r_gs = nc.alloc_sbuf_tensor("r_gs", [128, NG], F32)
    r_ta = nc.alloc_sbuf_tensor("r_ta", [128, NG], F32)
    r_tb = nc.alloc_sbuf_tensor("r_tb", [128, NG], F32)
    r_rk = nc.alloc_sbuf_tensor("r_rk", [128, NE], F32)
    r_pn = nc.alloc_sbuf_tensor("r_pn", [128, NG], F32)
    r_mb = nc.alloc_sbuf_tensor("r_mb", [128, NE], F32)
    r_kp = nc.alloc_sbuf_tensor("r_kp", [128, NE], F32)
    r_dn = nc.alloc_sbuf_tensor("r_dn", [128, 1], F32)
    s_cmb = nc.alloc_sbuf_tensor("s_cmb", [128, NE * TSUB], F32)
    s_cem = nc.alloc_sbuf_tensor("s_cem", [16, T], F32)

    pA = nc.alloc_psum_tensor("pA", [128, T], F32)
    pB = nc.alloc_psum_tensor("pB", [128, T], F32)
    pD = nc.alloc_psum_tensor("pD", [128, T], F32)
    pC = nc.alloc_psum_tensor("pC", [128, T], F32)
    pS1 = nc.alloc_psum_tensor("pS1", [1, T], F32)
    pS2 = nc.alloc_psum_tensor("pS2", [1, T], F32)
    pM = nc.alloc_psum_tensor("pM", [128, T], F32)
    pR = nc.alloc_psum_tensor("pR", [128, T], F32)

    b = Bld(nc)
    BIG = 1 << 30

    def R(t, lo=0, hi=BIG):
        return (t.name, lo, hi)

    def PS(t):
        return (t.name, 0, BIG)

    def dma(qeng, out_ap, in_ap, reads, writes, sem, recfinal=None):
        b.getsem(sem)
        b.dma_sems[qeng].add(sem)
        rs = (sem, recfinal) if recfinal is not None else None
        b.op(qeng, lambda e, o=out_ap, i=in_ap: e.dma_start(out=o, in_=i),
             reads=reads, writes=writes, inc=sem, amt=16, recsem=rs)

    def vop(fn, reads, writes):
        b.op("vector", fn, reads, writes, inc="dv")

    def aop(fn, reads, writes):
        b.op("scalar", fn, reads, writes, inc="ac")

    def gop(fn, reads, writes):
        b.op("gpsimd", fn, reads, writes, inc="gq")

    def pop_(par, fn, reads, writes):
        if par % 2 == 0:
            vop(fn, reads, writes)
        else:
            gop(fn, reads, writes)

    rot = [0]
    rot_list = [[None]]

    def bank():
        lst = rot_list[0]
        rot[0] = (rot[0] + 1) % len(lst)
        return lst[rot[0]]

    def set_rot(lst):
        rot_list[0] = lst
        rot[0] = 0

    def mmg(pt, pairs, ap_out=None):
        gv = b.semv["pe"] + 1
        n = len(pairs)
        out = ap_out if ap_out is not None else pt[:, :]
        for i, (la, lrg, ra, rrg) in enumerate(pairs):
            last = i == n - 1
            b.op("tensor",
                 lambda e, la=la, ra=ra, s=(i == 0), z=last, o=out:
                 e.matmul(o, la, ra, start=s, stop=z),
                 reads=[lrg, rrg], writes=[PS(pt)],
                 inc=("pe" if last else None), amt=1, recsem=("pe", gv),
                 skipval=gv)
        assert b.semv["pe"] == gv

    def mmg3(pt, items, ap_out=None):
        gv = b.semv["pe"] + 1
        out = ap_out if ap_out is not None else pt[:, :]
        n3 = len(items) * 3
        i = 0
        for (wh, wl, wrg, xh, xl, xrg) in items:
            for (la, ra) in ((wh, xh), (wl, xh), (wh, xl)):
                last = i == n3 - 1
                b.op("tensor",
                     lambda e, la=la, ra=ra, s=(i == 0), z=last, o=out:
                     e.matmul(o, la, ra, start=s, stop=z),
                     reads=[wrg, xrg], writes=[PS(pt)],
                     inc=("pe" if last else None), amt=1, recsem=("pe", gv),
                     skipval=gv)
                i += 1
        assert b.semv["pe"] == gv

    ring = [0]

    def wslot(units):
        if ring[0] + units > NSLOT:
            ring[0] = 0
        off = ring[0] * WSLOT
        ring[0] += units
        return off

    split_done = set()
    presplit_jobs = []

    def wpair(dram_t, blk, cols, colslice=None, cache=None, stage_q="sync"):
        u = cols // WSLOT
        if cache is not None:
            pdram, fam, idx = cache[0], cache[1], cache[2]
            first = (fam, idx) not in split_done
            if first:
                split_done.add((fam, idx))
        else:
            first = True
        if not first:
            po = wslot(u)
            prg = R(wbuf, po, po + cols)
            src = pdram[blk, :, :] if colslice is None else \
                pdram[blk, :, colslice[0]:colslice[1]]
            dma("sync", wbuf[:, po:po + cols], src,
                reads=[(fam, idx, idx + 1)], writes=[prg],
                sem=f"ws{po // WSLOT}")
        else:
            so = wslot(2 * u)
            po = so + cols
            srg = R(wbuf, so, so + cols)
            src = dram_t[blk, :, :] if colslice is None else \
                dram_t[blk, :, colslice[0]:colslice[1]]
            dma(stage_q, wbuf[:, so:so + cols], src, reads=[], writes=[srg],
                sem=f"w{'s' if stage_q == 'sync' else 'p'}{so // WSLOT}")
            prg = R(wbuf, po, po + cols)
            half = cols // 2
            hi_full = wbuf[:, po:po + half].bitcast(F16)
            lo_full = wbuf[:, po + half:po + cols].bitcast(F16)
            gop(lambda e, o=hi_full, i=wbuf[:, so:so + cols]: e.tensor_copy(o, i),
                reads=[srg], writes=[prg])
            vop(lambda e, o=lo_full, i=wbuf[:, so:so + cols], h=hi_full:
                e.tensor_tensor(o, i, h, ALU.subtract),
                reads=[srg, prg], writes=[prg])
            if cache is not None:
                dst = pdram[blk, :, :] if colslice is None else \
                    pdram[blk, :, colslice[0]:colslice[1]]
                dma("gpsimd", dst, wbuf[:, po:po + cols],
                    reads=[prg], writes=[(fam, idx, idx + 1)],
                    sem=f"wb{po // WSLOT}")
        half = cols // 2

        def chunk(k, po=po, half=half):
            whk = wbuf[:, po + k * 64:po + (k + 1) * 64].bitcast(F16)
            wlk = wbuf[:, po + half + k * 64:po + half + (k + 1) * 64].bitcast(F16)
            return whk, wlk

        return chunk, prg

    def presplit(k=1):
        while k > 0 and presplit_jobs:
            dram_t, blk, cols, colslice, cache = presplit_jobs.pop(0)
            fam, idx = cache[1], cache[2]
            if (fam, idx) in split_done:
                continue
            wpair(dram_t, blk, cols, colslice=colslice, cache=cache)
            k -= 1

    def queue_layer_presplit(L):
        for n in range(DC):
            presplit_jobs.append((wk, L * DC + n, D, None, (wkp, "k", L * DC + n)))
            presplit_jobs.append((wv, L * DC + n, D, None, (wvp, "v", L * DC + n)))
        for n in range(DC):
            presplit_jobs.append((wq, L * DC + n, D, None, (wqp, "q", L * DC + n)))
        for n in range(DC):
            presplit_jobs.append((wo, L * DC + n, D, None, (wop, "o", L * DC + n)))
        for ff in range(FFC):
            presplit_jobs.append((w1, L * FFC + ff, D, None, (w1p, "1", L * FFC + ff)))
        for n in range(DC):
            for quarter in range(4):
                presplit_jobs.append((w2, L * DC + n, WSLOT,
                                      (quarter * WSLOT, (quarter + 1) * WSLOT),
                                      (w2p, "2", (L * DC + n) * 4 + quarter)))

    def queue_moe_presplit():
        for j in range(SHC):
            presplit_jobs.append((sgw, j, D, None, (sgp, "sg", j)))
        for j in range(SHC):
            presplit_jobs.append((suw, j, D, None, (sup, "su", j)))
        for e_ in range(NE):
            for nb in range(EHC):
                presplit_jobs.append((gw, e_ * EHC + nb, D, None,
                                      (gwp, "g", e_ * EHC + nb)))
                presplit_jobs.append((uw, e_ * EHC + nb, D, None,
                                      (uwp, "u", e_ * EHC + nb)))
        for n in range(DC):
            presplit_jobs.append((sdw, n, 2048, None, (sdp, "sd", n)))
            for wv_ in range(2):
                presplit_jobs.append((dw, n, 2048, (wv_ * 2048, (wv_ + 1) * 2048),
                                      (dwp, "d", n * 2 + wv_)))

    def cpcol(name, i):
        o = CPOFF[name] + i
        return cp[:, o:o + 1]

    CPRG = R(cp)
    x_rg = lambda n, t0, t1: R(x_sb, n * SC + t0, n * SC + t1)
    x_ap = lambda n, t0, t1: x_sb[:, n * SC + t0:n * SC + t1]

    def PL(off, cols=T):
        return pool[:, off:off + cols], R(pool, off, off + cols)

    def PLH(off, cols):
        return pool[:, off:off + cols].bitcast(F16), R(pool, off, off + cols)

    def split_to_pool(src_ap, src_rg, off, caster="scalar", subeng=0):
        hi_ap, hi_rg = PLH(off, T // 2)
        lo_ap, lo_rg = PLH(off + T // 2, T // 2)
        if caster == "gpsimd":
            gop(lambda e, o=hi_ap, i=src_ap: e.tensor_copy(o, i),
                reads=[src_rg], writes=[hi_rg])
        else:
            aop(lambda e, o=hi_ap, i=src_ap: e.activation(o, i, AF.Copy),
                reads=[src_rg], writes=[hi_rg])
        pop_(subeng, lambda e, o=lo_ap, i=src_ap, h=hi_ap:
             e.tensor_tensor(o, i, h, ALU.subtract),
             [src_rg, hi_rg], [lo_rg])

    def xsplit(base, t0):
        res = []
        for k in range(DC):
            s_ap, s_rg = x_ap(k, t0, t0 + T), x_rg(k, t0, t0 + T)
            off = base + k * T
            split_to_pool(s_ap, s_rg, off)
            hi_ap = pool[:, off:off + T // 2].bitcast(F16)
            lo_ap = pool[:, off + T // 2:off + T].bitcast(F16)
            res.append((hi_ap, lo_ap, R(pool, off, off + T)))
        return res

    set_rot([pA, pB, pD, pC])

    dma("sync", cp[:, :], cpk[:, :], reads=[], writes=[CPRG],
        sem="scst", recfinal=48)
    dma("sync", onesr[:, :], onr[:, :], reads=[], writes=[R(onesr)],
        sem="scst", recfinal=48)
    dma("sync", s_rwt[:, :], rwt[:, :], reads=[], writes=[R(s_rwt)],
        sem="scst", recfinal=48)
    for n in range(DC):
        dma("sync", x_ap(n, 0, SC), xin[:, n * SC:(n + 1) * SC],
            reads=[], writes=[x_rg(n, 0, SC)], sem="sx", recfinal=16 * DC)

    ONEC = (cp[:, CPOFF["onec"]:CPOFF["onec"] + 1], CPRG)
    ONESR = (onesr[0:1, :], R(onesr))
    ONESR32 = (onesr[32:33, :], R(onesr))
    ONESR64 = (onesr[64:65, :], R(onesr))
    IDENT = (cp[:, CPOFF["ident"]:CPOFF["ident"] + 128], CPRG)

    def eluplus1_head(psum_t, bcol, sA, sB):
        vop(lambda e, sA=sA: e.tensor_scalar(sA[0], psum_t[:, :], bcol, 0.0,
                                             ALU.add, ALU.min),
            reads=[PS(psum_t), CPRG], writes=[sA[1]])
        vop(lambda e, sB=sB: e.tensor_scalar(sB[0], psum_t[:, :], bcol, 0.0,
                                             ALU.add, ALU.max),
            reads=[PS(psum_t), CPRG], writes=[sB[1]])
        aop(lambda e, sA=sA: e.activation(sA[0], sA[0], AF.Exp),
            reads=[sA[1]], writes=[sA[1]])

    def eluplus1_tail(out_ap, out_rg, sA, sB, acc_ap=None, acc_rg=None):
        if acc_ap is not None:
            vop(lambda e, sA=sA, sB=sB: e.scalar_tensor_tensor(
                    out_ap, sA[0], 1.0, sB[0], ALU.mult, ALU.add,
                    accum_out=acc_ap),
                reads=[sA[1], sB[1]], writes=[out_rg, acc_rg])
        else:
            vop(lambda e, sA=sA, sB=sB: e.scalar_tensor_tensor(
                    out_ap, sA[0], 1.0, sB[0], ALU.mult, ALU.add),
                reads=[sA[1], sB[1]], writes=[out_rg])

    def layernorm(y, eps, out, scale_name=None, bias_name=None, Lix=0,
                  store_after=None, sbase=4096):
        sD_ap, sD_rg = PL(sbase)
        sE_ap, sE_rg = PL(sbase + T)
        sC_ap, sC_rg = PL(sbase + 2 * T)
        vop(lambda e: e.tensor_add(sD_ap, y[0][0], y[1][0]),
            reads=[y[0][1], y[1][1]], writes=[sD_rg])
        for n in range(2, DC):
            vop(lambda e, n=n: e.tensor_add(sD_ap, sD_ap, y[n][0]),
                reads=[sD_rg, y[n][1]], writes=[sD_rg])
        vop(lambda e: e.tensor_mul(sE_ap, y[0][0], y[0][0]),
            reads=[y[0][1]], writes=[sE_rg])
        for n in range(1, DC):
            vop(lambda e, n=n: e.tensor_mul(sC_ap, y[n][0], y[n][0]),
                reads=[y[n][1]], writes=[sC_rg])
            vop(lambda e: e.tensor_add(sE_ap, sE_ap, sC_ap),
                reads=[sE_rg, sC_rg], writes=[sE_rg])
        mmg(pS1, [(ONEC[0], ONEC[1], sD_ap, sD_rg)])
        mmg(pS2, [(ONEC[0], ONEC[1], sE_ap, sE_rg)])
        rm_ap = pool[0:1, sbase + 3 * T:sbase + 4 * T]
        rm_rg = R(pool, sbase + 3 * T, sbase + 4 * T)
        rv_ap = pool[0:1, sbase + 4 * T:sbase + 5 * T]
        rv_rg = R(pool, sbase + 4 * T, sbase + 5 * T)
        vop(lambda e: e.tensor_scalar_mul(rm_ap, pS1[:, :], 1.0 / D),
            reads=[PS(pS1)], writes=[rm_rg])
        vop(lambda e: e.tensor_scalar_mul(rv_ap, pS2[:, :], 1.0 / D),
            reads=[PS(pS2)], writes=[rv_rg])
        mmg(pM, [(ONESR[0], ONESR[1], rm_ap, rm_rg)])
        vop(lambda e: e.tensor_mul(rm_ap, rm_ap, rm_ap),
            reads=[rm_rg], writes=[rm_rg])
        vop(lambda e: e.tensor_sub(rv_ap, rv_ap, rm_ap),
            reads=[rv_rg, rm_rg], writes=[rv_rg])
        vop(lambda e: e.tensor_scalar_add(rv_ap, rv_ap, float(eps)),
            reads=[rv_rg], writes=[rv_rg])
        aop(lambda e: e.activation(rv_ap, rv_ap, AF.Ln),
            reads=[rv_rg], writes=[rv_rg])
        aop(lambda e: e.activation(rv_ap, rv_ap, AF.Exp, scale=-0.5),
            reads=[rv_rg], writes=[rv_rg])
        mmg(pR, [(ONESR[0], ONESR[1], rv_ap, rv_rg)])
        for n in range(DC):
            vop(lambda e, n=n: e.tensor_sub(y[n][0], y[n][0], pM[:, :]),
                reads=[y[n][1], PS(pM)], writes=[y[n][1]])
            if scale_name is None:
                pop_(n, lambda e, n=n: e.tensor_mul(out[n][0], y[n][0], pR[:, :]),
                     [y[n][1], PS(pR)], [out[n][1]]) if False else \
                    vop(lambda e, n=n: e.tensor_mul(out[n][0], y[n][0], pR[:, :]),
                        reads=[y[n][1], PS(pR)], writes=[out[n][1]])
                if store_after is not None:
                    store_after(n)
            else:
                vop(lambda e, n=n: e.tensor_mul(y[n][0], y[n][0], pR[:, :]),
                    reads=[y[n][1], PS(pR)], writes=[y[n][1]])
                sc_ = cpcol(scale_name, Lix * DC + n)
                bc_ = cpcol(bias_name, Lix * DC + n)
                vop(lambda e, n=n, sc_=sc_, bc_=bc_: e.tensor_scalar(
                        out[n][0], y[n][0], sc_, bc_, ALU.mult, ALU.add),
                    reads=[y[n][1], CPRG], writes=[out[n][1]])

    for L in range(nl):
        if L + 1 < nl:
            queue_layer_presplit(L + 1)
        elif do_moe:
            queue_moe_presplit()
        xpa_next = [None]
        for t in range(NT):
            t0 = t * T
            b.phase = "passA"
            if xpa_next[0] is None:
                xp = xsplit(4096 if t % 2 == 0 else 8192, t0)
            else:
                xp = xpa_next[0]
            pend_a = [None]
            for n in range(DC):
                kch, krg = wpair(wk, L * DC + n, D,
                                 cache=(wkp, "k", L * DC + n))
                vch, vrg2 = wpair(wv, L * DC + n, D,
                                  cache=(wvp, "v", L * DC + n))
                pk = bank()
                mmg3(pk, [(kch(k)[0], kch(k)[1], krg,
                           xp[k][0], xp[k][1], xp[k][2]) for k in range(DC)])
                pv = bank()
                mmg3(pv, [(vch(k)[0], vch(k)[1], vrg2,
                           xp[k][0], xp[k][1], xp[k][2]) for k in range(DC)])
                col = n * NT + t
                rr = (n % 2) * 4
                sk_ap, sk_rg = PL((rr + 2) * T)
                sv_ap, sv_rg = PL((rr + 3) * T)
                sA, sB = PL(rr * T), PL((rr + 1) * T)
                eluplus1_head(pk, cpcol("bk", L * DC + n), sA, sB)
                bvc = cpcol("bv", L * DC + n)
                vop(lambda e, pv=pv, o=sv_ap, bvc=bvc: e.tensor_scalar_add(
                        o, pv[:, :], bvc),
                    reads=[PS(pv), CPRG], writes=[sv_rg])
                if pend_a[0] is not None:
                    pend_a[0]()
                smul = (scr[2][:, :], R(scr[2])) if n % 2 == 0 else \
                    (rows_t[:, :], ("rows_t", 0, T))

                def tail_a(col=col, sA=sA, sB=sB, sk_ap=sk_ap, sk_rg=sk_rg,
                           sv_ap=sv_ap, sv_rg=sv_rg, smul=smul):
                    eluplus1_tail(sk_ap, sk_rg, sA, sB,
                                  acc_ap=ksp4[:, col:col + 1],
                                  acc_rg=R(ksp4, col, col + 1))
                    vop(lambda e: e.tensor_mul(smul[0], sk_ap, sv_ap),
                        reads=[sk_rg, sv_rg], writes=[smul[1]])
                    vop(lambda e: e.reduce_sum(
                            kvp4[:, col:col + 1], smul[0], axis=AX.X),
                        reads=[smul[1]], writes=[R(kvp4, col, col + 1)])
                pend_a[0] = tail_a
                if n == 0 and t + 1 < NT:
                    xpa_next[0] = xsplit(4096 if (t + 1) % 2 == 0 else 8192,
                                         t0 + T)
                elif n == 0:
                    xpa_next[0] = None
            pend_a[0]()
            pend_a[0] = None
        for (p4, red) in ((kvp4, kvr), (ksp4, ksr)):
            if NT == 1:
                vop(lambda e, p4=p4, red=red: e.tensor_copy(red[:, :], p4[:, :]),
                    reads=[R(p4)], writes=[R(red)])
            else:
                v0 = apx(p4, 0, [(NT, DC)])
                v1 = apx(p4, 1, [(NT, DC)])
                vop(lambda e, p4=p4, red=red, v0=v0, v1=v1: e.tensor_tensor(
                        red[:, :], v0, v1, ALU.add),
                    reads=[R(p4)], writes=[R(red)])
                for t in range(2, NT):
                    vt = apx(p4, t, [(NT, DC)])
                    vop(lambda e, red=red, vt=vt, p4=p4: e.tensor_tensor(
                            red[:, :], red[:, :], vt, ALU.add),
                        reads=[R(red), R(p4)], writes=[R(red)])
        gfin = 16 * 2 * (L + 1)
        dma("gpsimd", bass.AP(cci, 0, [[DC, 128], [1, DC]]), kvr[:, :],
            reads=[R(kvr)], writes=[R(cci, 0, D)], sem="gpi", recfinal=gfin)
        dma("gpsimd", bass.AP(cci, D, [[DC, 128], [1, DC]]), ksr[:, :],
            reads=[R(ksr)], writes=[R(cci, D, 2 * D)], sem="gpi", recfinal=gfin)
        b.op("gpsimd",
             lambda e: e.collective_compute(
                 "AllReduce", ALU.add, replica_groups=[list(range(n_cores))],
                 ins=[cci.ap().opt()], outs=[cco.ap().opt()]),
             reads=[R(cci, 0, 2 * D)], writes=[R(cco, 0, 2 * D)], inc="cc")
        dma("gpsimd", kvg[:, :], bass.AP(cco, 0, [[DC, 128], [1, DC]]),
            reads=[R(cco, 0, D)], writes=[R(kvg)], sem="gpo", recfinal=gfin)
        dma("gpsimd", ksg[:, :], bass.AP(cco, D, [[DC, 128], [1, DC]]),
            reads=[R(cco, D, 2 * D)], writes=[R(ksg)], sem="gpo", recfinal=gfin)

        xpb_next = [None]
        for t in range(NT):
            t0 = t * T
            b.phase = "q"
            xp = xpb_next[0] if xpb_next[0] is not None else xsplit(4096, t0)
            pend_q = [None]
            for n in range(DC):
                qch, qrg = wpair(wq, L * DC + n, D,
                                 cache=(wqp, "q", L * DC + n))
                pq = bank()
                mmg3(pq, [(qch(k)[0], qch(k)[1], qrg,
                           xp[k][0], xp[k][1], xp[k][2]) for k in range(DC)])
                q_ap, q_rg = PL(n * T)
                rr = n % 4
                sA, sB = PL(8192 + (2 * rr) * T), PL(8192 + (2 * rr + 1) * T)
                eluplus1_head(pq, cpcol("bq", L * DC + n), sA, sB)
                if pend_q[0] is not None:
                    pend_q[0]()
                pend_q[0] = (lambda q_ap=q_ap, q_rg=q_rg, sA=sA, sB=sB:
                             eluplus1_tail(q_ap, q_rg, sA, sB))
                presplit(1)
            pend_q[0]()
            b.phase = "V"
            vhl = []
            for n in range(DC):
                q_ap, q_rg = PL(n * T)
                sden = scr[0] if n % 2 == 0 else scr[2]
                vop(lambda e, q=q_ap, n=n, sd=sden: e.tensor_scalar(
                        sd[:, :], q, ksg[:, n:n + 1], EPS_IN,
                        ALU.mult, ALU.add),
                    reads=[q_rg, R(ksg)], writes=[R(sden)])
                vop(lambda e, sd=sden: e.reciprocal(sd[:, :], sd[:, :]),
                    reads=[R(sden)], writes=[R(sden)])
                sv2 = (scr[1][:, :], R(scr[1])) if n % 2 == 0 else \
                    (rows_t[:, :], ("rows_t", 0, T))
                vop(lambda e, q=q_ap, n=n, sd=sden, sv2=sv2: e.scalar_tensor_tensor(
                        sv2[0], q, kvg[:, n:n + 1], sd[:, :],
                        ALU.mult, ALU.mult),
                    reads=[q_rg, R(kvg), R(sden)], writes=[sv2[1]])
                split_to_pool(sv2[0], sv2[1], n * T)
                vh = pool[:, n * T:n * T + T // 2].bitcast(F16)
                vl = pool[:, n * T + T // 2:(n + 1) * T].bitcast(F16)
                vhl.append((vh, vl, R(pool, n * T, (n + 1) * T)))
            b.phase = "o"
            ych = []
            for n in range(DC):
                och, org = wpair(wo, L * DC + n, D,
                                 cache=(wop, "o", L * DC + n))
                po = bank()
                mmg3(po, [(och(k)[0], och(k)[1], org,
                           vhl[k][0], vhl[k][1], vhl[k][2])
                          for k in range(DC)])
                a_ap, a_rg = PL(4096 + n * T)
                boc = cpcol("bo", L * DC + n)
                aop(lambda e, po=po, o=a_ap, boc=boc: e.activation(
                        o, po[:, :], GELU, bias=boc),
                    reads=[PS(po), CPRG], writes=[a_rg])
                aop(lambda e, o=a_ap: e.activation(o, o, GELU),
                    reads=[a_rg], writes=[a_rg])
                ych.append((a_ap, a_rg))
                presplit(1)
            for n in range(DC):
                ya, yr, xa = ych[n][0], ych[n][1], x_ap(n, t0, t0 + T)
                vop(lambda e, ya=ya, xa=xa: e.tensor_add(ya, ya, xa),
                    reads=[yr, x_rg(n, t0, t0 + T)], writes=[yr])
            xout = [(x_ap(n, t0, t0 + T), x_rg(n, t0, t0 + T)) for n in range(DC)]
            layernorm(ych, EPS_IN, xout, "l1s", "l1b", L, sbase=0)
            b.phase = "W1"
            xp1 = xsplit(4096, t0)
            hph = []
            for ff in range(FFC):
                fch, frg = wpair(w1, L * FFC + ff, D,
                                 cache=(w1p, "1", L * FFC + ff))
                pf = bank()
                mmg3(pf, [(fch(k)[0], fch(k)[1], frg,
                           xp1[k][0], xp1[k][1], xp1[k][2]) for k in range(DC)])
                b1c = cpcol("b1", L * 32 + ff)
                sg_ = scr[ff % 3]
                aop(lambda e, pf=pf, b1c=b1c, sg_=sg_: e.activation(
                        sg_[:, :], pf[:, :], GELU, bias=b1c),
                    reads=[PS(pf), CPRG], writes=[R(sg_)])
                hoff = 8192 + ff * T
                split_to_pool(sg_[:, :], R(sg_), hoff)
                presplit(1)
                hh = pool[:, hoff:hoff + T // 2].bitcast(F16)
                hl = pool[:, hoff + T // 2:hoff + T].bitcast(F16)
                hph.append((hh, hl, R(pool, hoff, hoff + T)))
            b.phase = "W2"
            ych2 = []
            for n in range(DC):
                items = []
                for quarter in range(4):
                    qch, qrg = wpair(w2, L * DC + n, WSLOT,
                                     colslice=(quarter * WSLOT, (quarter + 1) * WSLOT),
                                     cache=(w2p, "2", (L * DC + n) * 4 + quarter))
                    for k in range(8):
                        kk = quarter * 8 + k
                        wh, wl = qch(k)
                        items.append((wh, wl, qrg,
                                      hph[kk][0], hph[kk][1], hph[kk][2]))
                pf2 = bank()
                mmg3(pf2, items)
                y2_ap, y2_rg = PL(n * T)
                b2c = cpcol("b2", L * DC + n)
                sg_ = scr[n % 3]
                aop(lambda e, pf2=pf2, b2c=b2c, sg_=sg_: e.activation(
                        sg_[:, :], pf2[:, :], GELU, bias=b2c),
                    reads=[PS(pf2), CPRG], writes=[R(sg_)])
                xa2 = x_ap(n, t0, t0 + T)
                vop(lambda e, o=y2_ap, xa2=xa2, sg_=sg_: e.tensor_add(
                        o, sg_[:, :], xa2),
                    reads=[R(sg_), x_rg(n, t0, t0 + T)], writes=[y2_rg])
                ych2.append((y2_ap, y2_rg))
                presplit(1)
            xpb_next[0] = xsplit(4096, t0 + T) if t + 1 < NT else None
            layernorm(ych2, EPS_IN, xout, "l2s", "l2b", L, sbase=8192 + 8 * T)

    for t in range(NT if do_outer else 0):
        t0 = t * T
        ych = []
        for n in range(DC):
            xo_ap, xo_rg = PL(n * T)
            dma("sync", xo_ap, xin[:, n * SC + t0:n * SC + t0 + T],
                reads=[], writes=[xo_rg], sem="so",
                recfinal=16 * DC * (t + 1))
            xa3 = x_ap(n, t0, t0 + T)
            vop(lambda e, o=xo_ap, xa3=xa3: e.tensor_add(o, o, xa3),
                reads=[xo_rg, x_rg(n, t0, t0 + T)], writes=[xo_rg])
            ych.append((xo_ap, xo_rg))
        xout = [(x_ap(n, t0, t0 + T), x_rg(n, t0, t0 + T)) for n in range(DC)]
        layernorm(ych, EPS_OUT, xout, sbase=4096)

    LT16 = CPOFF["lt16"]
    LT4 = CPOFF["lt4"]
    XMH = 8192
    HWB = 12288
    FAB = 20480
    if not do_moe:
        for t in range(NT):
            t0 = t * T
            for n in range(DC):
                dma("scalar", yout[:, n * SC + t0:n * SC + t0 + T],
                    x_ap(n, t0, t0 + T), reads=[x_rg(n, t0, t0 + T)],
                    writes=[R(yout, n * SC + t0, n * SC + t0 + T)],
                    sem=f"st{n % 2}")
    for t in range(NT if do_moe else 0):
        t0 = t * T
        b.phase = "router"
        for c in range(TSUB):
            pr = bank()
            mmg(pr, [(x_ap(k, t0 + c * 128, t0 + (c + 1) * 128),
                      x_rg(k, t0 + c * 128, t0 + (c + 1) * 128),
                      s_rwt[:, k * NE:(k + 1) * NE], R(s_rwt))
                     for k in range(DC)],
                ap_out=pr[:, 0:NE])
            aop(lambda e, pr=pr, c=c: e.activation(
                    r_sc[:, c * NE:(c + 1) * NE], pr[:, 0:NE], AF.Sigmoid),
                reads=[PS(pr)], writes=[R(r_sc, c * NE, (c + 1) * NE)])
        for c in range(TSUB):
            vop(lambda e, c=c: e.tensor_tensor(
                    r_sb[:, c * NE:(c + 1) * NE], r_sc[:, c * NE:(c + 1) * NE],
                    cp[:, CPOFF["cbb"]:CPOFF["cbb"] + NE], ALU.add),
                reads=[R(r_sc, c * NE, (c + 1) * NE), CPRG],
                writes=[R(r_sb, c * NE, (c + 1) * NE)])
        RT = (scr[0][:, 0:256], R(scr[0]))
        RS = (scr[1][:, 0:256], R(scr[1]))
        C1 = (scr[2][:, 0:256], R(scr[2]))
        C2 = (rows_t[:, 0:256], ("rows_t", 0, T))
        for c in range(TSUB):
            o16 = c * NE
            sbrg = R(r_sb, o16, o16 + NE)
            pairs6 = [(0, 1), (0, 2), (0, 3), (1, 2), (1, 3), (2, 3)]
            for j, (i1, i2) in enumerate(pairs6):
                vop(lambda e, j=j, i1=i1, i2=i2, o16=o16: e.tensor_tensor(
                        r_pr[:, j * 4:(j + 1) * 4],
                        apx(r_sb, o16 + i1, [(4, 4)]),
                        apx(r_sb, o16 + i2, [(4, 4)]), ALU.add),
                    reads=[sbrg], writes=[R(r_pr, j * 4, (j + 1) * 4)])
            vop(lambda e: e.tensor_max(r_ta[:, :], r_pr[:, 0:4], r_pr[:, 4:8]),
                reads=[R(r_pr, 0, 8)], writes=[R(r_ta)])
            vop(lambda e: e.tensor_max(r_tb[:, :], r_pr[:, 8:12], r_pr[:, 12:16]),
                reads=[R(r_pr, 8, 16)], writes=[R(r_tb)])
            vop(lambda e: e.tensor_max(r_ta[:, :], r_ta[:, :], r_tb[:, :]),
                reads=[R(r_ta), R(r_tb)], writes=[R(r_ta)])
            vop(lambda e: e.tensor_max(r_tb[:, :], r_pr[:, 16:20], r_pr[:, 20:24]),
                reads=[R(r_pr, 16, 24)], writes=[R(r_tb)])
            vop(lambda e: e.tensor_max(r_gs[:, :], r_ta[:, :], r_tb[:, :]),
                reads=[R(r_ta), R(r_tb)], writes=[R(r_gs)])
            for i in range(NG):
                vop(lambda e, i=i: e.tensor_scalar(
                        RT[0][:, i * 4:(i + 1) * 4], r_gs[:, :],
                        0.0, r_gs[:, i:i + 1], ALU.mult, ALU.add),
                    reads=[R(r_gs)], writes=[RT[1]])
                vop(lambda e, i=i: e.tensor_copy(
                        RS[0][:, i * 4:(i + 1) * 4], r_gs[:, :]),
                    reads=[R(r_gs)], writes=[RS[1]])
            vop(lambda e: e.tensor_tensor(C1[0][:, 0:16], RS[0][:, 0:16],
                                          RT[0][:, 0:16], ALU.is_gt),
                reads=[RT[1], RS[1]], writes=[C1[1]])
            vop(lambda e: e.tensor_tensor(C2[0][:, 0:16], RS[0][:, 0:16],
                                          RT[0][:, 0:16], ALU.is_equal),
                reads=[RT[1], RS[1]], writes=[C2[1]])
            vop(lambda e: e.tensor_tensor(C2[0][:, 0:16], C2[0][:, 0:16],
                                          cp[:, LT4:LT4 + 16], ALU.mult),
                reads=[C2[1], CPRG], writes=[C2[1]])
            vop(lambda e: e.tensor_tensor(C1[0][:, 0:16], C1[0][:, 0:16],
                                          C2[0][:, 0:16], ALU.add),
                reads=[C1[1], C2[1]], writes=[C1[1]])
            vop(lambda e: e.reduce_sum(
                    r_rk[:, 0:4],
                    bass.AP(scr[2], 0, [[T, 128], [4, 4], [1, 4]]),
                    axis=AX.X),
                reads=[C1[1]], writes=[R(r_rk, 0, 4)])
            vop(lambda e: e.tensor_single_scalar(r_pn[:, :], r_rk[:, 0:4], 2.0,
                                                 ALU.is_lt),
                reads=[R(r_rk, 0, 4)], writes=[R(r_pn)])
            vop(lambda e: e.tensor_scalar(r_pn[:, :], r_pn[:, :], -1.0, 1e30,
                                          ALU.add, ALU.mult),
                reads=[R(r_pn)], writes=[R(r_pn)])
            for i in range(EPG):
                vop(lambda e, i=i, o16=o16: e.tensor_tensor(
                        apx(r_mb, i, [(4, 4)]),
                        apx(r_sb, o16 + i, [(4, 4)]),
                        r_pn[:, :], ALU.add),
                    reads=[sbrg, R(r_pn)], writes=[R(r_mb)])
            for i in range(NE):
                vop(lambda e, i=i: e.tensor_scalar(
                        RT[0][:, i * NE:(i + 1) * NE], r_mb[:, :],
                        0.0, r_mb[:, i:i + 1], ALU.mult, ALU.add),
                    reads=[R(r_mb)], writes=[RT[1]])
                vop(lambda e, i=i: e.tensor_copy(
                        RS[0][:, i * NE:(i + 1) * NE], r_mb[:, :]),
                    reads=[R(r_mb)], writes=[RS[1]])
            vop(lambda e: e.tensor_tensor(C1[0][:, :], RS[0][:, :], RT[0][:, :],
                                          ALU.is_gt),
                reads=[RT[1], RS[1]], writes=[C1[1]])
            vop(lambda e: e.tensor_tensor(C2[0][:, :], RS[0][:, :], RT[0][:, :],
                                          ALU.is_equal),
                reads=[RT[1], RS[1]], writes=[C2[1]])
            vop(lambda e: e.tensor_tensor(C2[0][:, :], C2[0][:, :],
                                          cp[:, LT16:LT16 + 256], ALU.mult),
                reads=[C2[1], CPRG], writes=[C2[1]])
            vop(lambda e: e.tensor_tensor(C1[0][:, :], C1[0][:, :], C2[0][:, :],
                                          ALU.add),
                reads=[C1[1], C2[1]], writes=[C1[1]])
            vop(lambda e: e.reduce_sum(
                    r_rk[:, :],
                    bass.AP(scr[2], 0, [[T, 128], [NE, NE], [1, NE]]),
                    axis=AX.X),
                reads=[C1[1]], writes=[R(r_rk)])
            vop(lambda e: e.tensor_single_scalar(r_kp[:, :], r_rk[:, :], 4.0,
                                                 ALU.is_lt),
                reads=[R(r_rk)], writes=[R(r_kp)])
            vop(lambda e, o16=o16: e.tensor_tensor(
                    r_kp[:, :], r_kp[:, :], r_sc[:, o16:o16 + NE], ALU.mult),
                reads=[R(r_kp), R(r_sc, o16, o16 + NE)], writes=[R(r_kp)])
            vop(lambda e: e.reduce_sum(r_dn[:, :], r_kp[:, :], axis=AX.X),
                reads=[R(r_kp)], writes=[R(r_dn)])
            vop(lambda e: e.tensor_scalar_add(r_dn[:, :], r_dn[:, :], 1e-20),
                reads=[R(r_dn)], writes=[R(r_dn)])
            vop(lambda e: e.reciprocal(r_dn[:, :], r_dn[:, :]),
                reads=[R(r_dn)], writes=[R(r_dn)])
            vop(lambda e, o16=o16: e.tensor_scalar_mul(
                    s_cmb[:, o16:o16 + NE], r_kp[:, :], r_dn[:, :]),
                reads=[R(r_kp), R(r_dn)], writes=[R(s_cmb, o16, o16 + NE)])

        xm = xsplit(XMH, t0)

        b.phase = "shared"
        prodp = []
        for j in range(SHC):
            sch, srg2 = wpair(sgw, j, D, cache=(sgp, "sg", j))
            pj = bank()
            mmg3(pj, [(sch(k)[0], sch(k)[1], srg2,
                       xm[k][0], xm[k][1], xm[k][2]) for k in range(DC)])
            pr_ap, pr_rg = PL(j * T)
            aop(lambda e, pj=pj, o=pr_ap: e.activation(o, pj[:, :], AF.Silu),
                reads=[PS(pj)], writes=[pr_rg])
        for j in range(SHC):
            sch, srg2 = wpair(suw, j, D, cache=(sup, "su", j))
            pj = bank()
            mmg3(pj, [(sch(k)[0], sch(k)[1], srg2,
                       xm[k][0], xm[k][1], xm[k][2]) for k in range(DC)])
            pr_ap, pr_rg = PL(j * T)
            sg_ = scr[j % 3]
            vop(lambda e, pj=pj, o=pr_ap, sg_=sg_: e.tensor_mul(
                    sg_[:, :], o, pj[:, :]),
                reads=[pr_rg, PS(pj)], writes=[R(sg_)])
            split_to_pool(sg_[:, :], R(sg_), j * T)
            ph = pool[:, j * T:j * T + T // 2].bitcast(F16)
            pl = pool[:, j * T + T // 2:(j + 1) * T].bitcast(F16)
            prodp.append((ph, pl, R(pool, j * T, (j + 1) * T)))

        for c in range(TSUB):
            pt = bank()
            b.op("tensor",
                 lambda e, pt=pt, c=c: e.transpose(
                     pt[0:16, c * 128:(c + 1) * 128],
                     s_cmb[:, c * NE:(c + 1) * NE], IDENT[0]),
                 reads=[R(s_cmb, c * NE, (c + 1) * NE), CPRG],
                 writes=[PS(pt)], inc="pe")
            vop(lambda e, pt=pt, c=c: e.tensor_copy(
                    s_cem[:, c * 128:(c + 1) * 128],
                    pt[0:16, c * 128:(c + 1) * 128]),
                reads=[PS(pt)], writes=[R(s_cem, c * 128, (c + 1) * 128)])

        b.phase = "experts"
        set_rot([pA, pB, pD])
        for wv in range(2):
            hwep = []
            for ei in range(8):
                e_ = wv * 8 + ei
                cr_ap, cr_rg = crow[e_ % 2]
                ones_e = ONESR32 if e_ % 2 == 0 else ONESR64
                dma("sync", cr_ap, s_cem[e_:e_ + 1, :],
                    reads=[R(s_cem)], writes=[cr_rg], sem=f"cr{e_ % 2}")
                mmg(pC, [(ones_e[0], ones_e[1], cr_ap, cr_rg)])
                rot4 = [(scr[0][:, :], R(scr[0])), (scr[1][:, :], R(scr[1])),
                        (scr[2][:, :], R(scr[2])), (rows_t[:, :], ("rows_t", 0, T))]
                chains = []
                for nb in range(EHC):
                    gch, grg = wpair(gw, e_ * EHC + nb, D,
                                     cache=(gwp, "g", e_ * EHC + nb))
                    pg = bank()
                    mmg3(pg, [(gch(k)[0], gch(k)[1], grg,
                               xm[k][0], xm[k][1], xm[k][2]) for k in range(DC)])
                    Xb = rot4[(e_ * EHC + nb) % 4]
                    chains.append(Xb)
                    aop(lambda e, pg=pg, Xb=Xb: e.activation(
                            Xb[0], pg[:, :], AF.Silu),
                        reads=[PS(pg)], writes=[Xb[1]])
                for nb in range(EHC):
                    uch, urg = wpair(uw, e_ * EHC + nb, D,
                                     cache=(uwp, "u", e_ * EHC + nb))
                    pu = bank()
                    mmg3(pu, [(uch(k)[0], uch(k)[1], urg,
                               xm[k][0], xm[k][1], xm[k][2]) for k in range(DC)])
                    Xb = chains[nb]
                    vop(lambda e, pu=pu, Xb=Xb: e.tensor_tensor(
                            Xb[0], Xb[0], pu[:, :], ALU.mult),
                        reads=[Xb[1], PS(pu)], writes=[Xb[1]])
                    vop(lambda e, Xb=Xb: e.tensor_tensor(
                            Xb[0], Xb[0], pC[:, :], ALU.mult),
                        reads=[Xb[1], PS(pC)], writes=[Xb[1]])
                    woff = HWB + (ei * EHC + nb) * T
                    split_to_pool(Xb[0], Xb[1], woff)
                    hh = pool[:, woff:woff + T // 2].bitcast(F16)
                    hl = pool[:, woff + T // 2:woff + T].bitcast(F16)
                    hwep.append((hh, hl, R(pool, woff, woff + T)))
            for n in range(DC):
                gv = b.semv["pe"] + 1
                items = []
                if wv == 0:
                    sch, srg2 = wpair(sdw, n, 2048, cache=(sdp, "sd", n))
                    for k in range(SHC):
                        wh, wl = sch(k)
                        items.append((wh, wl, srg2,
                                      prodp[k][0], prodp[k][1], prodp[k][2]))
                dch, drg = wpair(dw, n, 2048, colslice=(wv * 2048, (wv + 1) * 2048),
                                 cache=(dwp, "d", n * 2 + wv))
                for k in range(16):
                    wh, wl = dch(k)
                    items.append((wh, wl, drg,
                                  hwep[k][0], hwep[k][1], hwep[k][2]))
                mmg3(pD, items)
                if wv == 0:
                    fa_ap, fa_rg = PL(FAB + n * T)
                    vop(lambda e, o=fa_ap: e.tensor_copy(o, pD[:, :]),
                        reads=[PS(pD)], writes=[fa_rg])
                else:
                    fa_ap, fa_rg = PL(FAB + n * T)
                    xa4 = x_ap(n, t0, t0 + T)
                    vop(lambda e, fa=fa_ap: e.tensor_add(
                            scr[0][:, :], fa, pD[:, :]),
                        reads=[fa_rg, PS(pD)], writes=[R(scr[0])])
                    vop(lambda e, xa4=xa4: e.tensor_add(xa4, xa4, scr[0][:, :]),
                        reads=[x_rg(n, t0, t0 + T), R(scr[0])],
                        writes=[x_rg(n, t0, t0 + T)])
        set_rot([pA, pB, pD, pC])
        ych = [(x_ap(n, t0, t0 + T), x_rg(n, t0, t0 + T)) for n in range(DC)]
        zs = [(scr[0], R(scr[0])), (scr[1], R(scr[1]))]
        outs = [(zs[n % 2][0][:, :], zs[n % 2][1]) for n in range(DC)]

        def _store(n, t0=t0):
            dma("scalar", yout[:, n * SC + t0:n * SC + t0 + T],
                zs[n % 2][0][:, :],
                reads=[zs[n % 2][1]],
                writes=[R(yout, n * SC + t0, n * SC + t0 + T)],
                sem=f"st{n % 2}")
        layernorm(ych, EPS_OUT, outs, store_after=_store, sbase=0)

    for qeng, sems in b.dma_sems.items():
        for s in sorted(sems):
            b.wait(qeng, s, b.semv[s])
    b.realize()
    return nc


def _blockify(W):
    K, N = W.shape
    return np.ascontiguousarray(
        W.reshape(K // 128, 128, N // 128, 128).transpose(2, 1, 0, 3)
    ).reshape(N // 128, 128, K)


def _prep(ins):
    f = np.float32
    out = {}
    wqs, wks, wvs, wos, w1s, w2s = [], [], [], [], [], []
    for L in range(NL):
        wqs.append(_blockify(np.asarray(ins["L0_attn_Wq"][L], f)))
        wks.append(_blockify(np.asarray(ins["L0_attn_Wk"][L], f)))
        wvs.append(_blockify(np.asarray(ins["L0_attn_Wv"][L], f)))
        wos.append(_blockify(np.asarray(ins["L0_attn_Wo"][L], f)))
        w1s.append(_blockify(np.asarray(ins["L0_attn_W1"][L], f)))
        w2s.append(_blockify(np.asarray(ins["L0_attn_W2"][L], f)))
    out["wq"] = np.concatenate(wqs, 0)
    out["wk"] = np.concatenate(wks, 0)
    out["wv"] = np.concatenate(wvs, 0)
    out["wo"] = np.concatenate(wos, 0)
    out["w1"] = np.concatenate(w1s, 0)
    out["w2"] = np.concatenate(w2s, 0)
    out["gw"] = np.concatenate([_blockify(np.asarray(ins["L0_ffn_gate_w"][e], f))
                                for e in range(NE)], 0)
    out["uw"] = np.concatenate([_blockify(np.asarray(ins["L0_ffn_up_w"][e], f))
                                for e in range(NE)], 0)
    dwa = np.asarray(ins["L0_ffn_down_w"], f)
    dwt = dwa.reshape(NE, 2, 128, DC, 128).transpose(3, 2, 0, 1, 4)
    out["dw"] = np.ascontiguousarray(dwt).reshape(DC, 128, NE * 256)
    out["sgw"] = _blockify(np.ascontiguousarray(np.asarray(ins["L0_ffn_shared_gate_w"], f).T))
    out["suw"] = _blockify(np.ascontiguousarray(np.asarray(ins["L0_ffn_shared_up_w"], f).T))
    out["sdw"] = _blockify(np.ascontiguousarray(np.asarray(ins["L0_ffn_shared_down_w"], f).T))
    rwt = np.asarray(ins["L0_ffn_router_w"], f).T
    out["rwt"] = np.ascontiguousarray(rwt.reshape(DC, 128, NE).transpose(1, 0, 2)).reshape(128, DC * NE)
    cpk = np.zeros((128, CPW), f)

    def park(name, vec, per=8):
        arr = np.asarray(vec, f)
        for L in range(arr.shape[0]):
            cpk[:, CPOFF[name] + L * per:CPOFF[name] + (L + 1) * per] = \
                arr[L].reshape(per, 128).T

    park("bq", ins["L0_attn_bq"])
    park("bk", ins["L0_attn_bk"])
    park("bv", ins["L0_attn_bv"])
    park("bo", ins["L0_attn_bo"])
    park("l1s", ins["L0_attn_ln1_s"])
    park("l1b", ins["L0_attn_ln1_b"])
    park("l2s", ins["L0_attn_ln2_s"])
    park("l2b", ins["L0_attn_ln2_b"])
    park("b2", ins["L0_attn_b2"])
    park("b1", ins["L0_attn_b1"], per=32)
    cpk[:, CPOFF["cbb"]:CPOFF["cbb"] + NE] = np.asarray(
        ins["L0_ffn_correction_bias"], f)[None, :]
    i4 = np.arange(4)
    cpk[:, CPOFF["lt4"]:CPOFF["lt4"] + 16] = (
        i4[:, None] > i4[None, :]).astype(f).reshape(1, 16)
    i16 = np.arange(16)
    cpk[:, CPOFF["lt16"]:CPOFF["lt16"] + 256] = (
        i16[:, None] > i16[None, :]).astype(f).reshape(1, 256)
    cpk[:, CPOFF["ident"]:CPOFF["ident"] + 128] = np.eye(128, dtype=f)
    cpk[:, CPOFF["onec"]] = 1.0
    out["cpk"] = cpk
    out["onr"] = np.ones((128, 128), f)
    return out


def kernel(**inputs):
    n_cores = 8
    x = np.asarray(inputs["L0_attn_x"], np.float32)
    S = x.shape[1]
    SC = S // n_cores
    shared = _prep(inputs)
    nc = build(n_cores, SC)
    in_maps = []
    for c in range(n_cores):
        xc = np.ascontiguousarray(x[0, c * SC:(c + 1) * SC, :].T)
        xc = np.ascontiguousarray(
            xc.reshape(DC, 128, SC).transpose(1, 0, 2)).reshape(128, DC * SC)
        m = dict(shared)
        m["xin"] = xc
        in_maps.append(m)
    res = run_bass_kernel_spmd(nc, in_maps, core_ids=list(range(n_cores)))
    outs = []
    for c in range(n_cores):
        yc = np.asarray(res.results[c]["yout"])
        yc = yc.reshape(128, DC, SC).transpose(1, 0, 2).reshape(D, SC)
        outs.append(yc.T)
    return np.concatenate(outs, 0)[None, :, :].astype(np.float32)


# version 19
# speedup vs baseline: 1.0947x; 1.0127x over previous
import numpy as np
import concourse.bass as bass
import concourse.mybir as mybir
from concourse.bass_utils import run_bass_kernel_spmd

F32 = mybir.dt.float32
F16 = mybir.dt.float16
AF = mybir.ActivationFunctionType
ALU = mybir.AluOpType
AX = mybir.AxisListType
GELU = AF.Gelu_apprx_tanh

D, DC = 1024, 8
FF, FFC = 4096, 32
NL = 4
NE, EHC = 16, 2
SHC = 16
NG, EPG = 4, 4
EPS_IN, EPS_OUT = 1e-6, 1e-5
T = 512
TSUB = T // 128
WSLOT = 1024
NSLOT = 8

ENGINES = ("sync", "gpsimd", "tensor", "vector", "scalar")

_CP = [("bq", 32), ("bk", 32), ("bv", 32), ("bo", 32), ("l1s", 32), ("l1b", 32),
       ("l2s", 32), ("l2b", 32), ("b2", 32), ("b1", 128), ("cbb", 16),
       ("lt4", 16), ("lt16", 256), ("ident", 128), ("onec", 1)]
CPOFF = {}
_o = 0
for _n, _w in _CP:
    CPOFF[_n] = _o
    _o += _w
CPW = _o


class Bld:

    def __init__(self, nc):
        self.nc = nc
        self.q = {e: [] for e in ENGINES}
        self.sem = {}
        self.semv = {}
        for s in ("cc", "pe", "dv", "ac", "gq"):
            self.getsem(s)
        self.wm = {e: {} for e in ENGINES}
        self.dma_sems = {e: set() for e in ENGINES}
        self.tw = {}
        self.tr = {}
        self.phase = "init"
        self.oplog = []

    def getsem(self, s):
        if s not in self.sem:
            self.sem[s] = self.nc.alloc_semaphore(name="sem_" + s)
            self.semv[s] = 0
        return self.sem[s]

    def _deps(self, regions, writes):
        deps = []
        for (nm, lo, hi) in regions:
            for rec in self.tw.get(nm, ()):
                if rec[0] < hi and lo < rec[1]:
                    deps.append(rec)
            if writes:
                for rec in self.tr.get(nm, ()):
                    if rec[0] < hi and lo < rec[1]:
                        deps.append(rec)
        return deps

    def _record(self, table, regions, eng, sem, val, reads=False):
        for (nm, lo, hi) in regions:
            lst = table.setdefault(nm, [])
            if reads:
                lst[:] = [r for r in lst
                          if not (lo <= r[0] and r[1] <= hi and r[3] == sem)]
            else:
                lst[:] = [r for r in lst if not (lo <= r[0] and r[1] <= hi)]
            lst.append([lo, hi, eng, sem, val])

    def op(self, eng, fn, reads=(), writes=(), inc=None, amt=1, recsem=None,
           skipval=None):
        deps = self._deps(reads, False) + self._deps(writes, True)
        want = {}
        for (_, _, e2, s2, v2) in deps:
            assert s2 is not None, f"untracked producer consumed ({eng})"
            if skipval is not None and s2 == "pe" and v2 == skipval:
                continue
            if v2 > want.get(s2, -1):
                want[s2] = v2
        wl = []
        for s2, v2 in want.items():
            if self.wm[eng].get(s2, 0) >= v2:
                continue
            self.wm[eng][s2] = v2
            wl.append((s2, v2))
        if inc is not None:
            self.getsem(inc)
            self.semv[inc] += amt
            rs, rv = (inc, self.semv[inc]) if recsem is None else recsem
        else:
            assert recsem is not None, "op without inc needs recsem"
            rs, rv = recsem
        self._record(self.tr, reads, eng, rs, rv, reads=True)
        self._record(self.tw, writes, eng, rs, rv)
        for (nm, lo, hi) in writes:
            lst = self.tr.get(nm)
            if lst:
                lst[:] = [r for r in lst if not (lo <= r[0] and r[1] <= hi)]
        self.oplog.append((self.phase, eng, tuple(s for s, _ in wl)))
        self.q[eng].append((wl, fn, self.sem[inc] if inc else None, amt))

    def wait(self, eng, sem, val):
        if self.wm[eng].get(sem, 0) >= val:
            return
        self.wm[eng][sem] = val
        self.q[eng].append(([(sem, val)], None, None, 0))

    def realize(self):
        nc = self.nc
        with nc.Block() as block:
            for ename in ENGINES:
                ops = self.q[ename]

                def body(q, ops=ops):
                    for wl, fn, isem, amt in ops:
                        for (s, v) in wl:
                            q.wait_ge(self.sem[s], v)
                        if fn is not None:
                            inst = fn(q)
                            if isem is not None:
                                inst.then_inc(isem, amt)

                getattr(block, ename)(body)


def apx(t, off, dims):
    f = t.shape[1]
    return bass.AP(t, off, [[f, t.shape[0]]] + [list(d) for d in dims])


def build(n_cores, SC, nl=NL, do_outer=True, do_moe=True):
    NT = SC // T
    nc = bass.Bass("TRN2", target_bir_lowering=False)

    xin = nc.dram_tensor("xin", [128, DC * SC], F32, kind="ExternalInput")
    wq = nc.dram_tensor("wq", [NL * DC, 128, D], F32, kind="ExternalInput")
    wk = nc.dram_tensor("wk", [NL * DC, 128, D], F32, kind="ExternalInput")
    wv = nc.dram_tensor("wv", [NL * DC, 128, D], F32, kind="ExternalInput")
    wo = nc.dram_tensor("wo", [NL * DC, 128, D], F32, kind="ExternalInput")
    w1 = nc.dram_tensor("w1", [NL * FFC, 128, D], F32, kind="ExternalInput")
    w2 = nc.dram_tensor("w2", [NL * DC, 128, FF], F32, kind="ExternalInput")
    gw = nc.dram_tensor("gw", [NE * EHC, 128, D], F32, kind="ExternalInput")
    uw = nc.dram_tensor("uw", [NE * EHC, 128, D], F32, kind="ExternalInput")
    dw = nc.dram_tensor("dw", [DC, 128, NE * 256], F32, kind="ExternalInput")
    sgw = nc.dram_tensor("sgw", [SHC, 128, D], F32, kind="ExternalInput")
    suw = nc.dram_tensor("suw", [SHC, 128, D], F32, kind="ExternalInput")
    sdw = nc.dram_tensor("sdw", [DC, 128, 2048], F32, kind="ExternalInput")
    rwt = nc.dram_tensor("rwt", [128, DC * NE], F32, kind="ExternalInput")
    cpk = nc.dram_tensor("cpk", [128, CPW], F32, kind="ExternalInput")
    onr = nc.dram_tensor("onr", [128, 128], F32, kind="ExternalInput")
    yout = nc.dram_tensor("yout", [128, DC * SC], F32, kind="ExternalOutput")
    cci = nc.dram_tensor("cci", [2 * D], F32)
    cco = nc.dram_tensor("cco", [2 * D], F32)
    wqp = nc.dram_tensor("wqp", [NL * DC, 128, D], F32)
    wkp = nc.dram_tensor("wkp", [NL * DC, 128, D], F32)
    wvp = nc.dram_tensor("wvp", [NL * DC, 128, D], F32)
    wop = nc.dram_tensor("wop", [NL * DC, 128, D], F32)
    w1p = nc.dram_tensor("w1p", [NL * FFC, 128, D], F32)
    w2p = nc.dram_tensor("w2p", [NL * DC, 128, FF], F32)
    gwp = nc.dram_tensor("gwp", [NE * EHC, 128, D], F32)
    uwp = nc.dram_tensor("uwp", [NE * EHC, 128, D], F32)
    sgp = nc.dram_tensor("sgp", [SHC, 128, D], F32)
    sup = nc.dram_tensor("sup", [SHC, 128, D], F32)
    dwp = nc.dram_tensor("dwp", [DC, 128, NE * 256], F32)
    sdp = nc.dram_tensor("sdp", [DC, 128, 2048], F32)

    x_sb = nc.alloc_sbuf_tensor("x_sb", [128, DC * SC], F32)
    pool = nc.alloc_sbuf_tensor("pool", [128, 8192 + FFC * T], F32)
    wbuf = nc.alloc_sbuf_tensor("wbuf", [128, NSLOT * WSLOT], F32)
    cp = nc.alloc_sbuf_tensor("cp", [128, CPW], F32)
    onesr = nc.alloc_sbuf_tensor("onesr", [128, 128], F32)
    s_rwt = nc.alloc_sbuf_tensor("s_rwt", [128, DC * NE], F32)
    scr = [nc.alloc_sbuf_tensor(f"scr{i}", [128, T], F32) for i in range(3)]
    rows_t = nc.alloc_sbuf_tensor("rows_t", [128, T], F32)
    crow = [(rows_t[32:33, :], ("rows_t", 0, T)),
            (rows_t[64:65, :], ("rows_t", 0, T))]
    kvp4 = nc.alloc_sbuf_tensor("kvp4", [128, DC * NT], F32)
    ksp4 = nc.alloc_sbuf_tensor("ksp4", [128, DC * NT], F32)
    kvr = nc.alloc_sbuf_tensor("kvr", [128, DC], F32)
    ksr = nc.alloc_sbuf_tensor("ksr", [128, DC], F32)
    kvg = nc.alloc_sbuf_tensor("kvg", [128, DC], F32)
    ksg = nc.alloc_sbuf_tensor("ksg", [128, DC], F32)
    r_sc = nc.alloc_sbuf_tensor("r_sc", [128, NE * TSUB], F32)
    r_sb = nc.alloc_sbuf_tensor("r_sb", [128, NE * TSUB], F32)
    r_pr = nc.alloc_sbuf_tensor("r_pr", [128, 24], F32)
    r_gs = nc.alloc_sbuf_tensor("r_gs", [128, NG], F32)
    r_ta = nc.alloc_sbuf_tensor("r_ta", [128, NG], F32)
    r_tb = nc.alloc_sbuf_tensor("r_tb", [128, NG], F32)
    r_rk = nc.alloc_sbuf_tensor("r_rk", [128, NE], F32)
    r_pn = nc.alloc_sbuf_tensor("r_pn", [128, NG], F32)
    r_mb = nc.alloc_sbuf_tensor("r_mb", [128, NE], F32)
    r_kp = nc.alloc_sbuf_tensor("r_kp", [128, NE], F32)
    r_dn = nc.alloc_sbuf_tensor("r_dn", [128, 1], F32)
    s_cmb = nc.alloc_sbuf_tensor("s_cmb", [128, NE * TSUB], F32)
    s_cem = nc.alloc_sbuf_tensor("s_cem", [16, T], F32)

    pA = nc.alloc_psum_tensor("pA", [128, T], F32)
    pB = nc.alloc_psum_tensor("pB", [128, T], F32)
    pD = nc.alloc_psum_tensor("pD", [128, T], F32)
    pC = nc.alloc_psum_tensor("pC", [128, T], F32)
    pS1 = nc.alloc_psum_tensor("pS1", [1, T], F32)
    pS2 = nc.alloc_psum_tensor("pS2", [1, T], F32)
    pM = nc.alloc_psum_tensor("pM", [128, T], F32)
    pR = nc.alloc_psum_tensor("pR", [128, T], F32)

    b = Bld(nc)
    BIG = 1 << 30

    def R(t, lo=0, hi=BIG):
        return (t.name, lo, hi)

    def PS(t):
        return (t.name, 0, BIG)

    def dma(qeng, out_ap, in_ap, reads, writes, sem, recfinal=None):
        b.getsem(sem)
        b.dma_sems[qeng].add(sem)
        rs = (sem, recfinal) if recfinal is not None else None
        b.op(qeng, lambda e, o=out_ap, i=in_ap: e.dma_start(out=o, in_=i),
             reads=reads, writes=writes, inc=sem, amt=16, recsem=rs)

    def vop(fn, reads, writes):
        b.op("vector", fn, reads, writes, inc="dv")

    def aop(fn, reads, writes):
        b.op("scalar", fn, reads, writes, inc="ac")

    def gop(fn, reads, writes):
        b.op("gpsimd", fn, reads, writes, inc="gq")

    def pop_(par, fn, reads, writes):
        if par % 2 == 0:
            vop(fn, reads, writes)
        else:
            gop(fn, reads, writes)

    rot = [0]
    rot_list = [[None]]

    def bank():
        lst = rot_list[0]
        rot[0] = (rot[0] + 1) % len(lst)
        return lst[rot[0]]

    def set_rot(lst):
        rot_list[0] = lst
        rot[0] = 0

    def mmg(pt, pairs, ap_out=None):
        gv = b.semv["pe"] + 1
        n = len(pairs)
        out = ap_out if ap_out is not None else pt[:, :]
        for i, (la, lrg, ra, rrg) in enumerate(pairs):
            last = i == n - 1
            b.op("tensor",
                 lambda e, la=la, ra=ra, s=(i == 0), z=last, o=out:
                 e.matmul(o, la, ra, start=s, stop=z),
                 reads=[lrg, rrg], writes=[PS(pt)],
                 inc=("pe" if last else None), amt=1, recsem=("pe", gv),
                 skipval=gv)
        assert b.semv["pe"] == gv

    def mmg3(pt, items, ap_out=None):
        gv = b.semv["pe"] + 1
        out = ap_out if ap_out is not None else pt[:, :]
        n3 = len(items) * 3
        i = 0
        for (wh, wl, wrg, xh, xl, xrg) in items:
            for (la, ra) in ((wh, xh), (wl, xh), (wh, xl)):
                last = i == n3 - 1
                b.op("tensor",
                     lambda e, la=la, ra=ra, s=(i == 0), z=last, o=out:
                     e.matmul(o, la, ra, start=s, stop=z),
                     reads=[wrg, xrg], writes=[PS(pt)],
                     inc=("pe" if last else None), amt=1, recsem=("pe", gv),
                     skipval=gv)
                i += 1
        assert b.semv["pe"] == gv

    ring = [0]

    def wslot(units):
        if ring[0] + units > NSLOT:
            ring[0] = 0
        off = ring[0] * WSLOT
        ring[0] += units
        return off

    split_done = set()
    presplit_jobs = []

    def wpair(dram_t, blk, cols, colslice=None, cache=None, stage_q="sync"):
        u = cols // WSLOT
        if cache is not None:
            pdram, fam, idx = cache[0], cache[1], cache[2]
            first = (fam, idx) not in split_done
            if first:
                split_done.add((fam, idx))
        else:
            first = True
        if not first:
            po = wslot(u)
            prg = R(wbuf, po, po + cols)
            src = pdram[blk, :, :] if colslice is None else \
                pdram[blk, :, colslice[0]:colslice[1]]
            dma("sync", wbuf[:, po:po + cols], src,
                reads=[(fam, idx, idx + 1)], writes=[prg],
                sem=f"ws{po // WSLOT}")
        else:
            so = wslot(2 * u)
            po = so + cols
            srg = R(wbuf, so, so + cols)
            src = dram_t[blk, :, :] if colslice is None else \
                dram_t[blk, :, colslice[0]:colslice[1]]
            dma(stage_q, wbuf[:, so:so + cols], src, reads=[], writes=[srg],
                sem=f"w{'s' if stage_q == 'sync' else 'p'}{so // WSLOT}")
            prg = R(wbuf, po, po + cols)
            half = cols // 2
            hi_full = wbuf[:, po:po + half].bitcast(F16)
            lo_full = wbuf[:, po + half:po + cols].bitcast(F16)
            gop(lambda e, o=hi_full, i=wbuf[:, so:so + cols]: e.tensor_copy(o, i),
                reads=[srg], writes=[prg])
            vop(lambda e, o=lo_full, i=wbuf[:, so:so + cols], h=hi_full:
                e.tensor_tensor(o, i, h, ALU.subtract),
                reads=[srg, prg], writes=[prg])
            if cache is not None:
                dst = pdram[blk, :, :] if colslice is None else \
                    pdram[blk, :, colslice[0]:colslice[1]]
                dma("gpsimd", dst, wbuf[:, po:po + cols],
                    reads=[prg], writes=[(fam, idx, idx + 1)],
                    sem=f"wb{po // WSLOT}")
        half = cols // 2

        def chunk(k, po=po, half=half):
            whk = wbuf[:, po + k * 64:po + (k + 1) * 64].bitcast(F16)
            wlk = wbuf[:, po + half + k * 64:po + half + (k + 1) * 64].bitcast(F16)
            return whk, wlk

        return chunk, prg

    def presplit(k=1):
        while k > 0 and presplit_jobs:
            dram_t, blk, cols, colslice, cache = presplit_jobs.pop(0)
            fam, idx = cache[1], cache[2]
            if (fam, idx) in split_done:
                continue
            wpair(dram_t, blk, cols, colslice=colslice, cache=cache)
            k -= 1

    def queue_layer_presplit(L):
        for n in range(DC):
            presplit_jobs.append((wk, L * DC + n, D, None, (wkp, "k", L * DC + n)))
            presplit_jobs.append((wv, L * DC + n, D, None, (wvp, "v", L * DC + n)))
        for n in range(DC):
            presplit_jobs.append((wq, L * DC + n, D, None, (wqp, "q", L * DC + n)))
        for n in range(DC):
            presplit_jobs.append((wo, L * DC + n, D, None, (wop, "o", L * DC + n)))
        for ff in range(FFC):
            presplit_jobs.append((w1, L * FFC + ff, D, None, (w1p, "1", L * FFC + ff)))
        for n in range(DC):
            for quarter in range(4):
                presplit_jobs.append((w2, L * DC + n, WSLOT,
                                      (quarter * WSLOT, (quarter + 1) * WSLOT),
                                      (w2p, "2", (L * DC + n) * 4 + quarter)))

    def queue_moe_presplit():
        for j in range(SHC):
            presplit_jobs.append((sgw, j, D, None, (sgp, "sg", j)))
        for j in range(SHC):
            presplit_jobs.append((suw, j, D, None, (sup, "su", j)))
        for e_ in range(NE):
            for nb in range(EHC):
                presplit_jobs.append((gw, e_ * EHC + nb, D, None,
                                      (gwp, "g", e_ * EHC + nb)))
                presplit_jobs.append((uw, e_ * EHC + nb, D, None,
                                      (uwp, "u", e_ * EHC + nb)))
        for n in range(DC):
            presplit_jobs.append((sdw, n, 2048, None, (sdp, "sd", n)))
            for wv_ in range(2):
                presplit_jobs.append((dw, n, 2048, (wv_ * 2048, (wv_ + 1) * 2048),
                                      (dwp, "d", n * 2 + wv_)))

    def cpcol(name, i):
        o = CPOFF[name] + i
        return cp[:, o:o + 1]

    CPRG = R(cp)
    x_rg = lambda n, t0, t1: R(x_sb, n * SC + t0, n * SC + t1)
    x_ap = lambda n, t0, t1: x_sb[:, n * SC + t0:n * SC + t1]

    def PL(off, cols=T):
        return pool[:, off:off + cols], R(pool, off, off + cols)

    def PLH(off, cols):
        return pool[:, off:off + cols].bitcast(F16), R(pool, off, off + cols)

    def split_to_pool(src_ap, src_rg, off, caster="scalar", subeng=0):
        hi_ap, hi_rg = PLH(off, T // 2)
        lo_ap, lo_rg = PLH(off + T // 2, T // 2)
        if caster == "gpsimd":
            gop(lambda e, o=hi_ap, i=src_ap: e.tensor_copy(o, i),
                reads=[src_rg], writes=[hi_rg])
        else:
            aop(lambda e, o=hi_ap, i=src_ap: e.activation(o, i, AF.Copy),
                reads=[src_rg], writes=[hi_rg])
        pop_(subeng, lambda e, o=lo_ap, i=src_ap, h=hi_ap:
             e.tensor_tensor(o, i, h, ALU.subtract),
             [src_rg, hi_rg], [lo_rg])

    def xsplit(base, t0):
        res = []
        for k in range(DC):
            s_ap, s_rg = x_ap(k, t0, t0 + T), x_rg(k, t0, t0 + T)
            off = base + k * T
            split_to_pool(s_ap, s_rg, off)
            hi_ap = pool[:, off:off + T // 2].bitcast(F16)
            lo_ap = pool[:, off + T // 2:off + T].bitcast(F16)
            res.append((hi_ap, lo_ap, R(pool, off, off + T)))
        return res

    set_rot([pA, pB, pD, pC])

    dma("sync", cp[:, :], cpk[:, :], reads=[], writes=[CPRG],
        sem="scst", recfinal=48)
    dma("sync", onesr[:, :], onr[:, :], reads=[], writes=[R(onesr)],
        sem="scst", recfinal=48)
    dma("sync", s_rwt[:, :], rwt[:, :], reads=[], writes=[R(s_rwt)],
        sem="scst", recfinal=48)
    for n in range(DC):
        dma("sync", x_ap(n, 0, SC), xin[:, n * SC:(n + 1) * SC],
            reads=[], writes=[x_rg(n, 0, SC)], sem="sx", recfinal=16 * DC)

    ONEC = (cp[:, CPOFF["onec"]:CPOFF["onec"] + 1], CPRG)
    ONESR = (onesr[0:1, :], R(onesr))
    ONESR32 = (onesr[32:33, :], R(onesr))
    ONESR64 = (onesr[64:65, :], R(onesr))
    IDENT = (cp[:, CPOFF["ident"]:CPOFF["ident"] + 128], CPRG)

    def eluplus1_head(psum_t, bcol, sA, sB):
        vop(lambda e, sA=sA: e.tensor_scalar(sA[0], psum_t[:, :], bcol, 0.0,
                                             ALU.add, ALU.min),
            reads=[PS(psum_t), CPRG], writes=[sA[1]])
        vop(lambda e, sB=sB: e.tensor_scalar(sB[0], psum_t[:, :], bcol, 0.0,
                                             ALU.add, ALU.max),
            reads=[PS(psum_t), CPRG], writes=[sB[1]])
        aop(lambda e, sA=sA: e.activation(sA[0], sA[0], AF.Exp),
            reads=[sA[1]], writes=[sA[1]])

    def eluplus1_tail(out_ap, out_rg, sA, sB, acc_ap=None, acc_rg=None):
        if acc_ap is not None:
            vop(lambda e, sA=sA, sB=sB: e.scalar_tensor_tensor(
                    out_ap, sA[0], 1.0, sB[0], ALU.mult, ALU.add,
                    accum_out=acc_ap),
                reads=[sA[1], sB[1]], writes=[out_rg, acc_rg])
        else:
            vop(lambda e, sA=sA, sB=sB: e.scalar_tensor_tensor(
                    out_ap, sA[0], 1.0, sB[0], ALU.mult, ALU.add),
                reads=[sA[1], sB[1]], writes=[out_rg])

    def layernorm(y, eps, out, scale_name=None, bias_name=None, Lix=0,
                  store_after=None, sbase=4096):
        sD_ap, sD_rg = PL(sbase)
        sE_ap, sE_rg = PL(sbase + T)
        sC_ap, sC_rg = PL(sbase + 2 * T)
        vop(lambda e: e.tensor_add(sD_ap, y[0][0], y[1][0]),
            reads=[y[0][1], y[1][1]], writes=[sD_rg])
        for n in range(2, DC):
            vop(lambda e, n=n: e.tensor_add(sD_ap, sD_ap, y[n][0]),
                reads=[sD_rg, y[n][1]], writes=[sD_rg])
        vop(lambda e: e.tensor_mul(sE_ap, y[0][0], y[0][0]),
            reads=[y[0][1]], writes=[sE_rg])
        for n in range(1, DC):
            vop(lambda e, n=n: e.tensor_mul(sC_ap, y[n][0], y[n][0]),
                reads=[y[n][1]], writes=[sC_rg])
            vop(lambda e: e.tensor_add(sE_ap, sE_ap, sC_ap),
                reads=[sE_rg, sC_rg], writes=[sE_rg])
        mmg(pS1, [(ONEC[0], ONEC[1], sD_ap, sD_rg)])
        mmg(pS2, [(ONEC[0], ONEC[1], sE_ap, sE_rg)])
        rm_ap = pool[0:1, sbase + 3 * T:sbase + 4 * T]
        rm_rg = R(pool, sbase + 3 * T, sbase + 4 * T)
        rv_ap = pool[0:1, sbase + 4 * T:sbase + 5 * T]
        rv_rg = R(pool, sbase + 4 * T, sbase + 5 * T)
        vop(lambda e: e.tensor_scalar_mul(rm_ap, pS1[:, :], 1.0 / D),
            reads=[PS(pS1)], writes=[rm_rg])
        vop(lambda e: e.tensor_scalar_mul(rv_ap, pS2[:, :], 1.0 / D),
            reads=[PS(pS2)], writes=[rv_rg])
        mmg(pM, [(ONESR[0], ONESR[1], rm_ap, rm_rg)])
        vop(lambda e: e.tensor_mul(rm_ap, rm_ap, rm_ap),
            reads=[rm_rg], writes=[rm_rg])
        vop(lambda e: e.tensor_sub(rv_ap, rv_ap, rm_ap),
            reads=[rv_rg, rm_rg], writes=[rv_rg])
        vop(lambda e: e.tensor_scalar_add(rv_ap, rv_ap, float(eps)),
            reads=[rv_rg], writes=[rv_rg])
        aop(lambda e: e.activation(rv_ap, rv_ap, AF.Ln),
            reads=[rv_rg], writes=[rv_rg])
        aop(lambda e: e.activation(rv_ap, rv_ap, AF.Exp, scale=-0.5),
            reads=[rv_rg], writes=[rv_rg])
        mmg(pR, [(ONESR[0], ONESR[1], rv_ap, rv_rg)])
        for n in range(DC):
            vop(lambda e, n=n: e.tensor_sub(y[n][0], y[n][0], pM[:, :]),
                reads=[y[n][1], PS(pM)], writes=[y[n][1]])
            if scale_name is None:
                pop_(n, lambda e, n=n: e.tensor_mul(out[n][0], y[n][0], pR[:, :]),
                     [y[n][1], PS(pR)], [out[n][1]]) if False else \
                    vop(lambda e, n=n: e.tensor_mul(out[n][0], y[n][0], pR[:, :]),
                        reads=[y[n][1], PS(pR)], writes=[out[n][1]])
                if store_after is not None:
                    store_after(n)
            else:
                vop(lambda e, n=n: e.tensor_mul(y[n][0], y[n][0], pR[:, :]),
                    reads=[y[n][1], PS(pR)], writes=[y[n][1]])
                sc_ = cpcol(scale_name, Lix * DC + n)
                bc_ = cpcol(bias_name, Lix * DC + n)
                vop(lambda e, n=n, sc_=sc_, bc_=bc_: e.tensor_scalar(
                        out[n][0], y[n][0], sc_, bc_, ALU.mult, ALU.add),
                    reads=[y[n][1], CPRG], writes=[out[n][1]])

    for L in range(nl):
        if L + 1 < nl:
            queue_layer_presplit(L + 1)
        elif do_moe:
            queue_moe_presplit()
        xpa_next = [None]
        for t in range(NT):
            t0 = t * T
            b.phase = "passA"
            if xpa_next[0] is None:
                xp = xsplit(4096 if t % 2 == 0 else 8192, t0)
            else:
                xp = xpa_next[0]
            pend_a = [None]
            for n in range(DC):
                kch, krg = wpair(wk, L * DC + n, D,
                                 cache=(wkp, "k", L * DC + n))
                vch, vrg2 = wpair(wv, L * DC + n, D,
                                  cache=(wvp, "v", L * DC + n))
                pk = bank()
                mmg3(pk, [(kch(k)[0], kch(k)[1], krg,
                           xp[k][0], xp[k][1], xp[k][2]) for k in range(DC)])
                pv = bank()
                mmg3(pv, [(vch(k)[0], vch(k)[1], vrg2,
                           xp[k][0], xp[k][1], xp[k][2]) for k in range(DC)])
                col = n * NT + t
                rr = (n % 2) * 4
                sk_ap, sk_rg = PL((rr + 2) * T)
                sv_ap, sv_rg = PL((rr + 3) * T)
                sA, sB = PL(rr * T), PL((rr + 1) * T)
                eluplus1_head(pk, cpcol("bk", L * DC + n), sA, sB)
                bvc = cpcol("bv", L * DC + n)
                vop(lambda e, pv=pv, o=sv_ap, bvc=bvc: e.tensor_scalar_add(
                        o, pv[:, :], bvc),
                    reads=[PS(pv), CPRG], writes=[sv_rg])
                if pend_a[0] is not None:
                    pend_a[0]()
                smul = (scr[2][:, :], R(scr[2])) if n % 2 == 0 else \
                    (rows_t[:, :], ("rows_t", 0, T))

                def tail_a(col=col, sA=sA, sB=sB, sk_ap=sk_ap, sk_rg=sk_rg,
                           sv_ap=sv_ap, sv_rg=sv_rg, smul=smul):
                    eluplus1_tail(sk_ap, sk_rg, sA, sB,
                                  acc_ap=ksp4[:, col:col + 1],
                                  acc_rg=R(ksp4, col, col + 1))
                    vop(lambda e: e.tensor_mul(smul[0], sk_ap, sv_ap),
                        reads=[sk_rg, sv_rg], writes=[smul[1]])
                    vop(lambda e: e.reduce_sum(
                            kvp4[:, col:col + 1], smul[0], axis=AX.X),
                        reads=[smul[1]], writes=[R(kvp4, col, col + 1)])
                pend_a[0] = tail_a
                if n == 0 and t + 1 < NT:
                    xpa_next[0] = xsplit(4096 if (t + 1) % 2 == 0 else 8192,
                                         t0 + T)
                elif n == 0:
                    xpa_next[0] = None
            pend_a[0]()
            pend_a[0] = None
        for (p4, red) in ((kvp4, kvr), (ksp4, ksr)):
            if NT == 1:
                vop(lambda e, p4=p4, red=red: e.tensor_copy(red[:, :], p4[:, :]),
                    reads=[R(p4)], writes=[R(red)])
            else:
                v0 = apx(p4, 0, [(NT, DC)])
                v1 = apx(p4, 1, [(NT, DC)])
                vop(lambda e, p4=p4, red=red, v0=v0, v1=v1: e.tensor_tensor(
                        red[:, :], v0, v1, ALU.add),
                    reads=[R(p4)], writes=[R(red)])
                for t in range(2, NT):
                    vt = apx(p4, t, [(NT, DC)])
                    vop(lambda e, red=red, vt=vt, p4=p4: e.tensor_tensor(
                            red[:, :], red[:, :], vt, ALU.add),
                        reads=[R(red), R(p4)], writes=[R(red)])
        gfin = 16 * 2 * (L + 1)
        dma("gpsimd", bass.AP(cci, 0, [[DC, 128], [1, DC]]), kvr[:, :],
            reads=[R(kvr)], writes=[R(cci, 0, D)], sem="gpi", recfinal=gfin)
        dma("gpsimd", bass.AP(cci, D, [[DC, 128], [1, DC]]), ksr[:, :],
            reads=[R(ksr)], writes=[R(cci, D, 2 * D)], sem="gpi", recfinal=gfin)
        b.op("gpsimd",
             lambda e: e.collective_compute(
                 "AllReduce", ALU.add, replica_groups=[list(range(n_cores))],
                 ins=[cci.ap().opt()], outs=[cco.ap().opt()]),
             reads=[R(cci, 0, 2 * D)], writes=[R(cco, 0, 2 * D)], inc="cc")
        dma("gpsimd", kvg[:, :], bass.AP(cco, 0, [[DC, 128], [1, DC]]),
            reads=[R(cco, 0, D)], writes=[R(kvg)], sem="gpo", recfinal=gfin)
        dma("gpsimd", ksg[:, :], bass.AP(cco, D, [[DC, 128], [1, DC]]),
            reads=[R(cco, D, 2 * D)], writes=[R(ksg)], sem="gpo", recfinal=gfin)

        xpb_next = [None]
        for t in range(NT):
            t0 = t * T
            b.phase = "q"
            xp = xpb_next[0] if xpb_next[0] is not None else xsplit(4096, t0)
            pend_q = [None]
            for n in range(DC):
                qch, qrg = wpair(wq, L * DC + n, D,
                                 cache=(wqp, "q", L * DC + n))
                pq = bank()
                mmg3(pq, [(qch(k)[0], qch(k)[1], qrg,
                           xp[k][0], xp[k][1], xp[k][2]) for k in range(DC)])
                q_ap, q_rg = PL(n * T)
                rr = n % 4
                sA, sB = PL(8192 + (2 * rr) * T), PL(8192 + (2 * rr + 1) * T)
                eluplus1_head(pq, cpcol("bq", L * DC + n), sA, sB)
                if pend_q[0] is not None:
                    pend_q[0]()
                pend_q[0] = (lambda q_ap=q_ap, q_rg=q_rg, sA=sA, sB=sB:
                             eluplus1_tail(q_ap, q_rg, sA, sB))
                presplit(1)
            pend_q[0]()
            b.phase = "V"
            vhl = []
            for n in range(DC):
                q_ap, q_rg = PL(n * T)
                sden = scr[0] if n % 2 == 0 else scr[2]
                vop(lambda e, q=q_ap, n=n, sd=sden: e.tensor_scalar(
                        sd[:, :], q, ksg[:, n:n + 1], EPS_IN,
                        ALU.mult, ALU.add),
                    reads=[q_rg, R(ksg)], writes=[R(sden)])
                vop(lambda e, sd=sden: e.reciprocal(sd[:, :], sd[:, :]),
                    reads=[R(sden)], writes=[R(sden)])
                sv2 = (scr[1][:, :], R(scr[1])) if n % 2 == 0 else \
                    (rows_t[:, :], ("rows_t", 0, T))
                vop(lambda e, q=q_ap, n=n, sd=sden, sv2=sv2: e.scalar_tensor_tensor(
                        sv2[0], q, kvg[:, n:n + 1], sd[:, :],
                        ALU.mult, ALU.mult),
                    reads=[q_rg, R(kvg), R(sden)], writes=[sv2[1]])
                split_to_pool(sv2[0], sv2[1], n * T)
                vh = pool[:, n * T:n * T + T // 2].bitcast(F16)
                vl = pool[:, n * T + T // 2:(n + 1) * T].bitcast(F16)
                vhl.append((vh, vl, R(pool, n * T, (n + 1) * T)))
            b.phase = "o"
            ych = []
            for n in range(DC):
                och, org = wpair(wo, L * DC + n, D,
                                 cache=(wop, "o", L * DC + n))
                po = bank()
                mmg3(po, [(och(k)[0], och(k)[1], org,
                           vhl[k][0], vhl[k][1], vhl[k][2])
                          for k in range(DC)])
                a_ap, a_rg = PL(4096 + n * T)
                boc = cpcol("bo", L * DC + n)
                aop(lambda e, po=po, o=a_ap, boc=boc: e.activation(
                        o, po[:, :], GELU, bias=boc),
                    reads=[PS(po), CPRG], writes=[a_rg])
                aop(lambda e, o=a_ap: e.activation(o, o, GELU),
                    reads=[a_rg], writes=[a_rg])
                ych.append((a_ap, a_rg))
                presplit(1)
            for n in range(DC):
                ya, yr, xa = ych[n][0], ych[n][1], x_ap(n, t0, t0 + T)
                vop(lambda e, ya=ya, xa=xa: e.tensor_add(ya, ya, xa),
                    reads=[yr, x_rg(n, t0, t0 + T)], writes=[yr])
            xout = [(x_ap(n, t0, t0 + T), x_rg(n, t0, t0 + T)) for n in range(DC)]
            layernorm(ych, EPS_IN, xout, "l1s", "l1b", L, sbase=0)
            b.phase = "W1"
            xp1 = xsplit(4096, t0)
            hph = []
            for ff in range(FFC):
                fch, frg = wpair(w1, L * FFC + ff, D,
                                 cache=(w1p, "1", L * FFC + ff))
                pf = bank()
                mmg3(pf, [(fch(k)[0], fch(k)[1], frg,
                           xp1[k][0], xp1[k][1], xp1[k][2]) for k in range(DC)])
                b1c = cpcol("b1", L * 32 + ff)
                sg_ = scr[ff % 3]
                aop(lambda e, pf=pf, b1c=b1c, sg_=sg_: e.activation(
                        sg_[:, :], pf[:, :], GELU, bias=b1c),
                    reads=[PS(pf), CPRG], writes=[R(sg_)])
                hoff = 8192 + ff * T
                split_to_pool(sg_[:, :], R(sg_), hoff)
                presplit(1)
                hh = pool[:, hoff:hoff + T // 2].bitcast(F16)
                hl = pool[:, hoff + T // 2:hoff + T].bitcast(F16)
                hph.append((hh, hl, R(pool, hoff, hoff + T)))
            b.phase = "W2"
            ych2 = []
            for n in range(DC):
                items = []
                for quarter in range(4):
                    qch, qrg = wpair(w2, L * DC + n, WSLOT,
                                     colslice=(quarter * WSLOT, (quarter + 1) * WSLOT),
                                     cache=(w2p, "2", (L * DC + n) * 4 + quarter))
                    for k in range(8):
                        kk = quarter * 8 + k
                        wh, wl = qch(k)
                        items.append((wh, wl, qrg,
                                      hph[kk][0], hph[kk][1], hph[kk][2]))
                pf2 = bank()
                mmg3(pf2, items)
                y2_ap, y2_rg = PL(n * T)
                b2c = cpcol("b2", L * DC + n)
                sg_ = scr[n % 3]
                aop(lambda e, pf2=pf2, b2c=b2c, sg_=sg_: e.activation(
                        sg_[:, :], pf2[:, :], GELU, bias=b2c),
                    reads=[PS(pf2), CPRG], writes=[R(sg_)])
                xa2 = x_ap(n, t0, t0 + T)
                vop(lambda e, o=y2_ap, xa2=xa2, sg_=sg_: e.tensor_add(
                        o, sg_[:, :], xa2),
                    reads=[R(sg_), x_rg(n, t0, t0 + T)], writes=[y2_rg])
                ych2.append((y2_ap, y2_rg))
                presplit(1)
            xpb_next[0] = xsplit(4096, t0 + T) if t + 1 < NT else None
            layernorm(ych2, EPS_IN, xout, "l2s", "l2b", L, sbase=8192 + 8 * T)

    for t in range(NT if do_outer else 0):
        t0 = t * T
        ych = []
        for n in range(DC):
            xo_ap, xo_rg = PL(n * T)
            dma("sync", xo_ap, xin[:, n * SC + t0:n * SC + t0 + T],
                reads=[], writes=[xo_rg], sem="so",
                recfinal=16 * DC * (t + 1))
            xa3 = x_ap(n, t0, t0 + T)
            vop(lambda e, o=xo_ap, xa3=xa3: e.tensor_add(o, o, xa3),
                reads=[xo_rg, x_rg(n, t0, t0 + T)], writes=[xo_rg])
            ych.append((xo_ap, xo_rg))
        xout = [(x_ap(n, t0, t0 + T), x_rg(n, t0, t0 + T)) for n in range(DC)]
        layernorm(ych, EPS_OUT, xout, sbase=4096)

    LT16 = CPOFF["lt16"]
    LT4 = CPOFF["lt4"]
    XMH = 8192
    HWB = 12288
    FAB = 20480
    if not do_moe:
        for t in range(NT):
            t0 = t * T
            for n in range(DC):
                dma("scalar", yout[:, n * SC + t0:n * SC + t0 + T],
                    x_ap(n, t0, t0 + T), reads=[x_rg(n, t0, t0 + T)],
                    writes=[R(yout, n * SC + t0, n * SC + t0 + T)],
                    sem=f"st{n % 2}")
    xm_next = [None]
    for t in range(NT if do_moe else 0):
        t0 = t * T
        b.phase = "router"
        for c in range(TSUB):
            pr = bank()
            mmg(pr, [(x_ap(k, t0 + c * 128, t0 + (c + 1) * 128),
                      x_rg(k, t0 + c * 128, t0 + (c + 1) * 128),
                      s_rwt[:, k * NE:(k + 1) * NE], R(s_rwt))
                     for k in range(DC)],
                ap_out=pr[:, 0:NE])
            aop(lambda e, pr=pr, c=c: e.activation(
                    r_sc[:, c * NE:(c + 1) * NE], pr[:, 0:NE], AF.Sigmoid),
                reads=[PS(pr)], writes=[R(r_sc, c * NE, (c + 1) * NE)])
        for c in range(TSUB):
            vop(lambda e, c=c: e.tensor_tensor(
                    r_sb[:, c * NE:(c + 1) * NE], r_sc[:, c * NE:(c + 1) * NE],
                    cp[:, CPOFF["cbb"]:CPOFF["cbb"] + NE], ALU.add),
                reads=[R(r_sc, c * NE, (c + 1) * NE), CPRG],
                writes=[R(r_sb, c * NE, (c + 1) * NE)])
        RT = (scr[0][:, 0:256], R(scr[0]))
        RS = (scr[1][:, 0:256], R(scr[1]))
        C1 = (scr[2][:, 0:256], R(scr[2]))
        C2 = (rows_t[:, 0:256], ("rows_t", 0, T))
        for c in range(TSUB):
            o16 = c * NE
            sbrg = R(r_sb, o16, o16 + NE)
            pairs6 = [(0, 1), (0, 2), (0, 3), (1, 2), (1, 3), (2, 3)]
            for j, (i1, i2) in enumerate(pairs6):
                vop(lambda e, j=j, i1=i1, i2=i2, o16=o16: e.tensor_tensor(
                        r_pr[:, j * 4:(j + 1) * 4],
                        apx(r_sb, o16 + i1, [(4, 4)]),
                        apx(r_sb, o16 + i2, [(4, 4)]), ALU.add),
                    reads=[sbrg], writes=[R(r_pr, j * 4, (j + 1) * 4)])
            vop(lambda e: e.tensor_max(r_ta[:, :], r_pr[:, 0:4], r_pr[:, 4:8]),
                reads=[R(r_pr, 0, 8)], writes=[R(r_ta)])
            vop(lambda e: e.tensor_max(r_tb[:, :], r_pr[:, 8:12], r_pr[:, 12:16]),
                reads=[R(r_pr, 8, 16)], writes=[R(r_tb)])
            vop(lambda e: e.tensor_max(r_ta[:, :], r_ta[:, :], r_tb[:, :]),
                reads=[R(r_ta), R(r_tb)], writes=[R(r_ta)])
            vop(lambda e: e.tensor_max(r_tb[:, :], r_pr[:, 16:20], r_pr[:, 20:24]),
                reads=[R(r_pr, 16, 24)], writes=[R(r_tb)])
            vop(lambda e: e.tensor_max(r_gs[:, :], r_ta[:, :], r_tb[:, :]),
                reads=[R(r_ta), R(r_tb)], writes=[R(r_gs)])
            for i in range(NG):
                vop(lambda e, i=i: e.tensor_scalar(
                        RT[0][:, i * 4:(i + 1) * 4], r_gs[:, :],
                        0.0, r_gs[:, i:i + 1], ALU.mult, ALU.add),
                    reads=[R(r_gs)], writes=[RT[1]])
                vop(lambda e, i=i: e.tensor_copy(
                        RS[0][:, i * 4:(i + 1) * 4], r_gs[:, :]),
                    reads=[R(r_gs)], writes=[RS[1]])
            vop(lambda e: e.tensor_tensor(C1[0][:, 0:16], RS[0][:, 0:16],
                                          RT[0][:, 0:16], ALU.is_gt),
                reads=[RT[1], RS[1]], writes=[C1[1]])
            vop(lambda e: e.tensor_tensor(C2[0][:, 0:16], RS[0][:, 0:16],
                                          RT[0][:, 0:16], ALU.is_equal),
                reads=[RT[1], RS[1]], writes=[C2[1]])
            vop(lambda e: e.tensor_tensor(C2[0][:, 0:16], C2[0][:, 0:16],
                                          cp[:, LT4:LT4 + 16], ALU.mult),
                reads=[C2[1], CPRG], writes=[C2[1]])
            vop(lambda e: e.tensor_tensor(C1[0][:, 0:16], C1[0][:, 0:16],
                                          C2[0][:, 0:16], ALU.add),
                reads=[C1[1], C2[1]], writes=[C1[1]])
            vop(lambda e: e.reduce_sum(
                    r_rk[:, 0:4],
                    bass.AP(scr[2], 0, [[T, 128], [4, 4], [1, 4]]),
                    axis=AX.X),
                reads=[C1[1]], writes=[R(r_rk, 0, 4)])
            vop(lambda e: e.tensor_single_scalar(r_pn[:, :], r_rk[:, 0:4], 2.0,
                                                 ALU.is_lt),
                reads=[R(r_rk, 0, 4)], writes=[R(r_pn)])
            vop(lambda e: e.tensor_scalar(r_pn[:, :], r_pn[:, :], -1.0, 1e30,
                                          ALU.add, ALU.mult),
                reads=[R(r_pn)], writes=[R(r_pn)])
            for i in range(EPG):
                vop(lambda e, i=i, o16=o16: e.tensor_tensor(
                        apx(r_mb, i, [(4, 4)]),
                        apx(r_sb, o16 + i, [(4, 4)]),
                        r_pn[:, :], ALU.add),
                    reads=[sbrg, R(r_pn)], writes=[R(r_mb)])
            for i in range(NE):
                vop(lambda e, i=i: e.tensor_scalar(
                        RT[0][:, i * NE:(i + 1) * NE], r_mb[:, :],
                        0.0, r_mb[:, i:i + 1], ALU.mult, ALU.add),
                    reads=[R(r_mb)], writes=[RT[1]])
                vop(lambda e, i=i: e.tensor_copy(
                        RS[0][:, i * NE:(i + 1) * NE], r_mb[:, :]),
                    reads=[R(r_mb)], writes=[RS[1]])
            vop(lambda e: e.tensor_tensor(C1[0][:, :], RS[0][:, :], RT[0][:, :],
                                          ALU.is_gt),
                reads=[RT[1], RS[1]], writes=[C1[1]])
            vop(lambda e: e.tensor_tensor(C2[0][:, :], RS[0][:, :], RT[0][:, :],
                                          ALU.is_equal),
                reads=[RT[1], RS[1]], writes=[C2[1]])
            vop(lambda e: e.tensor_tensor(C2[0][:, :], C2[0][:, :],
                                          cp[:, LT16:LT16 + 256], ALU.mult),
                reads=[C2[1], CPRG], writes=[C2[1]])
            vop(lambda e: e.tensor_tensor(C1[0][:, :], C1[0][:, :], C2[0][:, :],
                                          ALU.add),
                reads=[C1[1], C2[1]], writes=[C1[1]])
            vop(lambda e: e.reduce_sum(
                    r_rk[:, :],
                    bass.AP(scr[2], 0, [[T, 128], [NE, NE], [1, NE]]),
                    axis=AX.X),
                reads=[C1[1]], writes=[R(r_rk)])
            vop(lambda e: e.tensor_single_scalar(r_kp[:, :], r_rk[:, :], 4.0,
                                                 ALU.is_lt),
                reads=[R(r_rk)], writes=[R(r_kp)])
            vop(lambda e, o16=o16: e.tensor_tensor(
                    r_kp[:, :], r_kp[:, :], r_sc[:, o16:o16 + NE], ALU.mult),
                reads=[R(r_kp), R(r_sc, o16, o16 + NE)], writes=[R(r_kp)])
            vop(lambda e: e.reduce_sum(r_dn[:, :], r_kp[:, :], axis=AX.X),
                reads=[R(r_kp)], writes=[R(r_dn)])
            vop(lambda e: e.tensor_scalar_add(r_dn[:, :], r_dn[:, :], 1e-20),
                reads=[R(r_dn)], writes=[R(r_dn)])
            vop(lambda e: e.reciprocal(r_dn[:, :], r_dn[:, :]),
                reads=[R(r_dn)], writes=[R(r_dn)])
            vop(lambda e, o16=o16: e.tensor_scalar_mul(
                    s_cmb[:, o16:o16 + NE], r_kp[:, :], r_dn[:, :]),
                reads=[R(r_kp), R(r_dn)], writes=[R(s_cmb, o16, o16 + NE)])

        xm = xm_next[0] if xm_next[0] is not None else xsplit(XMH, t0)

        b.phase = "shared"
        prodp = []
        for j in range(SHC):
            sch, srg2 = wpair(sgw, j, D, cache=(sgp, "sg", j))
            pj = bank()
            mmg3(pj, [(sch(k)[0], sch(k)[1], srg2,
                       xm[k][0], xm[k][1], xm[k][2]) for k in range(DC)])
            pr_ap, pr_rg = PL(j * T)
            aop(lambda e, pj=pj, o=pr_ap: e.activation(o, pj[:, :], AF.Silu),
                reads=[PS(pj)], writes=[pr_rg])
        for j in range(SHC):
            sch, srg2 = wpair(suw, j, D, cache=(sup, "su", j))
            pj = bank()
            mmg3(pj, [(sch(k)[0], sch(k)[1], srg2,
                       xm[k][0], xm[k][1], xm[k][2]) for k in range(DC)])
            pr_ap, pr_rg = PL(j * T)
            sg_ = scr[j % 3]
            vop(lambda e, pj=pj, o=pr_ap, sg_=sg_: e.tensor_mul(
                    sg_[:, :], o, pj[:, :]),
                reads=[pr_rg, PS(pj)], writes=[R(sg_)])
            split_to_pool(sg_[:, :], R(sg_), j * T)
            ph = pool[:, j * T:j * T + T // 2].bitcast(F16)
            pl = pool[:, j * T + T // 2:(j + 1) * T].bitcast(F16)
            prodp.append((ph, pl, R(pool, j * T, (j + 1) * T)))

        for c in range(TSUB):
            pt = bank()
            b.op("tensor",
                 lambda e, pt=pt, c=c: e.transpose(
                     pt[0:16, c * 128:(c + 1) * 128],
                     s_cmb[:, c * NE:(c + 1) * NE], IDENT[0]),
                 reads=[R(s_cmb, c * NE, (c + 1) * NE), CPRG],
                 writes=[PS(pt)], inc="pe")
            vop(lambda e, pt=pt, c=c: e.tensor_copy(
                    s_cem[:, c * 128:(c + 1) * 128],
                    pt[0:16, c * 128:(c + 1) * 128]),
                reads=[PS(pt)], writes=[R(s_cem, c * 128, (c + 1) * 128)])

        b.phase = "experts"
        set_rot([pA, pB, pD])
        for wv in range(2):
            hwep = []
            for ei in range(8):
                e_ = wv * 8 + ei
                cr_ap, cr_rg = crow[e_ % 2]
                ones_e = ONESR32 if e_ % 2 == 0 else ONESR64
                dma("sync", cr_ap, s_cem[e_:e_ + 1, :],
                    reads=[R(s_cem)], writes=[cr_rg], sem=f"cr{e_ % 2}")
                mmg(pC, [(ones_e[0], ones_e[1], cr_ap, cr_rg)])
                rot4 = [(scr[0][:, :], R(scr[0])), (scr[1][:, :], R(scr[1])),
                        (scr[2][:, :], R(scr[2])), (rows_t[:, :], ("rows_t", 0, T))]
                chains = []
                for nb in range(EHC):
                    gch, grg = wpair(gw, e_ * EHC + nb, D,
                                     cache=(gwp, "g", e_ * EHC + nb))
                    pg = bank()
                    mmg3(pg, [(gch(k)[0], gch(k)[1], grg,
                               xm[k][0], xm[k][1], xm[k][2]) for k in range(DC)])
                    Xb = rot4[(e_ * EHC + nb) % 4]
                    chains.append(Xb)
                    aop(lambda e, pg=pg, Xb=Xb: e.activation(
                            Xb[0], pg[:, :], AF.Silu),
                        reads=[PS(pg)], writes=[Xb[1]])
                for nb in range(EHC):
                    uch, urg = wpair(uw, e_ * EHC + nb, D,
                                     cache=(uwp, "u", e_ * EHC + nb))
                    pu = bank()
                    mmg3(pu, [(uch(k)[0], uch(k)[1], urg,
                               xm[k][0], xm[k][1], xm[k][2]) for k in range(DC)])
                    Xb = chains[nb]
                    vop(lambda e, pu=pu, Xb=Xb: e.tensor_tensor(
                            Xb[0], Xb[0], pu[:, :], ALU.mult),
                        reads=[Xb[1], PS(pu)], writes=[Xb[1]])
                    vop(lambda e, Xb=Xb: e.tensor_tensor(
                            Xb[0], Xb[0], pC[:, :], ALU.mult),
                        reads=[Xb[1], PS(pC)], writes=[Xb[1]])
                    woff = HWB + (ei * EHC + nb) * T
                    split_to_pool(Xb[0], Xb[1], woff)
                    hh = pool[:, woff:woff + T // 2].bitcast(F16)
                    hl = pool[:, woff + T // 2:woff + T].bitcast(F16)
                    hwep.append((hh, hl, R(pool, woff, woff + T)))
            for n in range(DC):
                gv = b.semv["pe"] + 1
                items = []
                if wv == 0:
                    sch, srg2 = wpair(sdw, n, 2048, cache=(sdp, "sd", n))
                    for k in range(SHC):
                        wh, wl = sch(k)
                        items.append((wh, wl, srg2,
                                      prodp[k][0], prodp[k][1], prodp[k][2]))
                dch, drg = wpair(dw, n, 2048, colslice=(wv * 2048, (wv + 1) * 2048),
                                 cache=(dwp, "d", n * 2 + wv))
                for k in range(16):
                    wh, wl = dch(k)
                    items.append((wh, wl, drg,
                                  hwep[k][0], hwep[k][1], hwep[k][2]))
                mmg3(pD, items)
                if wv == 0:
                    fa_ap, fa_rg = PL(FAB + n * T)
                    vop(lambda e, o=fa_ap: e.tensor_copy(o, pD[:, :]),
                        reads=[PS(pD)], writes=[fa_rg])
                else:
                    fa_ap, fa_rg = PL(FAB + n * T)
                    xa4 = x_ap(n, t0, t0 + T)
                    vop(lambda e, fa=fa_ap: e.tensor_add(
                            scr[0][:, :], fa, pD[:, :]),
                        reads=[fa_rg, PS(pD)], writes=[R(scr[0])])
                    vop(lambda e, xa4=xa4: e.tensor_add(xa4, xa4, scr[0][:, :]),
                        reads=[x_rg(n, t0, t0 + T), R(scr[0])],
                        writes=[x_rg(n, t0, t0 + T)])
        set_rot([pA, pB, pD, pC])
        xm_next[0] = xsplit(XMH, t0 + T) if t + 1 < NT else None
        ych = [(x_ap(n, t0, t0 + T), x_rg(n, t0, t0 + T)) for n in range(DC)]
        zs = [(scr[0], R(scr[0])), (scr[1], R(scr[1]))]
        outs = [(zs[n % 2][0][:, :], zs[n % 2][1]) for n in range(DC)]

        def _store(n, t0=t0):
            dma("scalar", yout[:, n * SC + t0:n * SC + t0 + T],
                zs[n % 2][0][:, :],
                reads=[zs[n % 2][1]],
                writes=[R(yout, n * SC + t0, n * SC + t0 + T)],
                sem=f"st{n % 2}")
        layernorm(ych, EPS_OUT, outs, store_after=_store, sbase=0)

    for qeng, sems in b.dma_sems.items():
        for s in sorted(sems):
            b.wait(qeng, s, b.semv[s])
    b.realize()
    return nc


def _blockify(W):
    K, N = W.shape
    return np.ascontiguousarray(
        W.reshape(K // 128, 128, N // 128, 128).transpose(2, 1, 0, 3)
    ).reshape(N // 128, 128, K)


def _prep(ins):
    f = np.float32
    out = {}
    wqs, wks, wvs, wos, w1s, w2s = [], [], [], [], [], []
    for L in range(NL):
        wqs.append(_blockify(np.asarray(ins["L0_attn_Wq"][L], f)))
        wks.append(_blockify(np.asarray(ins["L0_attn_Wk"][L], f)))
        wvs.append(_blockify(np.asarray(ins["L0_attn_Wv"][L], f)))
        wos.append(_blockify(np.asarray(ins["L0_attn_Wo"][L], f)))
        w1s.append(_blockify(np.asarray(ins["L0_attn_W1"][L], f)))
        w2s.append(_blockify(np.asarray(ins["L0_attn_W2"][L], f)))
    out["wq"] = np.concatenate(wqs, 0)
    out["wk"] = np.concatenate(wks, 0)
    out["wv"] = np.concatenate(wvs, 0)
    out["wo"] = np.concatenate(wos, 0)
    out["w1"] = np.concatenate(w1s, 0)
    out["w2"] = np.concatenate(w2s, 0)
    out["gw"] = np.concatenate([_blockify(np.asarray(ins["L0_ffn_gate_w"][e], f))
                                for e in range(NE)], 0)
    out["uw"] = np.concatenate([_blockify(np.asarray(ins["L0_ffn_up_w"][e], f))
                                for e in range(NE)], 0)
    dwa = np.asarray(ins["L0_ffn_down_w"], f)
    dwt = dwa.reshape(NE, 2, 128, DC, 128).transpose(3, 2, 0, 1, 4)
    out["dw"] = np.ascontiguousarray(dwt).reshape(DC, 128, NE * 256)
    out["sgw"] = _blockify(np.ascontiguousarray(np.asarray(ins["L0_ffn_shared_gate_w"], f).T))
    out["suw"] = _blockify(np.ascontiguousarray(np.asarray(ins["L0_ffn_shared_up_w"], f).T))
    out["sdw"] = _blockify(np.ascontiguousarray(np.asarray(ins["L0_ffn_shared_down_w"], f).T))
    rwt = np.asarray(ins["L0_ffn_router_w"], f).T
    out["rwt"] = np.ascontiguousarray(rwt.reshape(DC, 128, NE).transpose(1, 0, 2)).reshape(128, DC * NE)
    cpk = np.zeros((128, CPW), f)

    def park(name, vec, per=8):
        arr = np.asarray(vec, f)
        for L in range(arr.shape[0]):
            cpk[:, CPOFF[name] + L * per:CPOFF[name] + (L + 1) * per] = \
                arr[L].reshape(per, 128).T

    park("bq", ins["L0_attn_bq"])
    park("bk", ins["L0_attn_bk"])
    park("bv", ins["L0_attn_bv"])
    park("bo", ins["L0_attn_bo"])
    park("l1s", ins["L0_attn_ln1_s"])
    park("l1b", ins["L0_attn_ln1_b"])
    park("l2s", ins["L0_attn_ln2_s"])
    park("l2b", ins["L0_attn_ln2_b"])
    park("b2", ins["L0_attn_b2"])
    park("b1", ins["L0_attn_b1"], per=32)
    cpk[:, CPOFF["cbb"]:CPOFF["cbb"] + NE] = np.asarray(
        ins["L0_ffn_correction_bias"], f)[None, :]
    i4 = np.arange(4)
    cpk[:, CPOFF["lt4"]:CPOFF["lt4"] + 16] = (
        i4[:, None] > i4[None, :]).astype(f).reshape(1, 16)
    i16 = np.arange(16)
    cpk[:, CPOFF["lt16"]:CPOFF["lt16"] + 256] = (
        i16[:, None] > i16[None, :]).astype(f).reshape(1, 256)
    cpk[:, CPOFF["ident"]:CPOFF["ident"] + 128] = np.eye(128, dtype=f)
    cpk[:, CPOFF["onec"]] = 1.0
    out["cpk"] = cpk
    out["onr"] = np.ones((128, 128), f)
    return out


def kernel(**inputs):
    n_cores = 8
    x = np.asarray(inputs["L0_attn_x"], np.float32)
    S = x.shape[1]
    SC = S // n_cores
    shared = _prep(inputs)
    nc = build(n_cores, SC)
    in_maps = []
    for c in range(n_cores):
        xc = np.ascontiguousarray(x[0, c * SC:(c + 1) * SC, :].T)
        xc = np.ascontiguousarray(
            xc.reshape(DC, 128, SC).transpose(1, 0, 2)).reshape(128, DC * SC)
        m = dict(shared)
        m["xin"] = xc
        in_maps.append(m)
    res = run_bass_kernel_spmd(nc, in_maps, core_ids=list(range(n_cores)))
    outs = []
    for c in range(n_cores):
        yc = np.asarray(res.results[c]["yout"])
        yc = yc.reshape(128, DC, SC).transpose(1, 0, 2).reshape(D, SC)
        outs.append(yc.T)
    return np.concatenate(outs, 0)[None, :, :].astype(np.float32)
